# version 05
# speedup vs baseline: 1.4735x; 1.0474x over previous
import math
import os as _os
import contextlib
B_TRUNC = int(_os.environ.get("B_TRUNC", "0"))
B_DEBUG = int(_os.environ.get("B_DEBUG", "0"))
import numpy as np
import ml_dtypes
import concourse.bass as bass
import concourse.mybir as mybir
from concourse.bass_utils import run_bass_kernel_spmd
from concourse.alu_op_type import AluOpType as ALU

F32 = mybir.dt.float32
BF16 = mybir.dt.bfloat16
AF = mybir.ActivationFunctionType
AX = mybir.AxisListType
BF16NP = ml_dtypes.bfloat16

NCORES = 8
D = 1024
T = 2048
NT = T // 128
ND = D // 128
NTOK = NCORES * T
NTILES = NTOK // 128
CAPACITY = 5120
NEG_BIG = -1.0e30
FLT_MAX = 3.4028235e38

def _host_consts():
    c = {}
    c["ident_bf"] = np.eye(128, dtype=BF16NP)
    c["ident_f"] = np.eye(128, dtype=np.float32)
    ones = np.ones((128, 128), dtype=np.float32)
    c["ones_f"] = ones
    c["ones_bf"] = ones.astype(BF16NP)
    tt = np.zeros((128, 128), dtype=np.float32)
    for m_ in range(128):
        tt[:m_, m_] = 1.0
    c["tstrict"] = tt
    c["iota8"] = np.arange(8, dtype=np.float32).reshape(1, 8)
    hd = 64
    inv = 1.0 / 10000.0 ** (np.arange(0, hd, 2, dtype=np.float32) / np.float32(hd))
    ang = np.arange(16, dtype=np.float32)[:, None] * inv[None, :8]
    ch = np.cos(ang.astype(np.float64)).astype(np.float32).reshape(1, 128)
    sh_ = np.sin(ang.astype(np.float64)).astype(np.float32).reshape(1, 128)
    c["rope0_cos"] = np.tile(ch, (128, 1))
    c["rope0_sin"] = np.tile(sh_, (128, 1))
    half = np.arange(0, 64, 2, dtype=np.float32)
    inv_freq = 1.0 / 10000.0 ** (half / 64.0)
    t = np.arange(2048, dtype=np.float32)
    freqs = t[:, None] * inv_freq[None, :]
    freqs = np.concatenate([freqs, freqs], -1)
    base_scale = (half + 0.4 * 64) / (1.4 * 64)
    power = (t - 1024) / 128.0
    scale = base_scale[None, :] ** power[:, None]
    scale = np.concatenate([scale, scale], -1).astype(np.float32)
    cos = np.cos(freqs.astype(np.float64)).astype(np.float32)
    sin = np.sin(freqs.astype(np.float64)).astype(np.float32)
    c["rope1_cs"] = (cos * scale).astype(np.float32)
    c["rope1_ss"] = (sin * scale).astype(np.float32)
    invs = (1.0 / scale).astype(np.float32)
    c["rope1_ci"] = (cos * invs).astype(np.float32)
    c["rope1_si"] = (sin * invs).astype(np.float32)
    i = np.arange(256)[:, None]
    jj = np.arange(512)[None, :]
    allowed = (jj >= i) & (jj <= i + 256)
    m = np.where(allowed, 0.0, -1e30).astype(np.float32)
    c["lmask"] = m
    m0 = m.copy()
    m0[:, :256] = -1e30
    c["lmask0"] = m0
    return c


def _prep(ins):
    f32 = np.float32
    sh = dict(_host_consts())
    x = np.ascontiguousarray(np.asarray(ins["L0_attn_x"], dtype=f32))
    xT = np.ascontiguousarray(x.reshape(NTOK, D).T)
    g = lambda k: np.asarray(ins[k], dtype=f32)
    sh["wqT"] = np.ascontiguousarray(g("L0_attn_wq").T)
    sh["wkaT"] = np.ascontiguousarray(g("L0_attn_wk_a").T)
    sh["wkbT"] = np.ascontiguousarray(g("L0_attn_wk_b").T)
    sh["wvaT"] = np.ascontiguousarray(g("L0_attn_wv_a").T)
    sh["wvbT"] = np.ascontiguousarray(g("L0_attn_wv_b").T)
    sh["woT"] = np.ascontiguousarray(g("L0_attn_wo").T)
    rw = g("L0_ffn_router_w")
    sh["rw_ext"] = np.ascontiguousarray(
        np.concatenate([rw, np.ones((D, 1), f32)], axis=1))
    sh["moe_w1"] = np.ascontiguousarray(g("L0_ffn_w1"))
    sh["moe_w2"] = np.ascontiguousarray(g("L0_ffn_w2"))
    sh["moe_b1"] = np.ascontiguousarray(
        g("L0_ffn_b1").reshape(8, 32, 128).transpose(0, 2, 1))
    b2ext = np.zeros((33, D), f32)
    b2ext[0:8] = g("L0_ffn_b2")
    b2ext[32] = g("L0_ffn_sb2")
    sh["b2ext"] = b2ext
    sh["sw1"] = np.ascontiguousarray(g("L0_ffn_sw1"))
    sh["sw2"] = np.ascontiguousarray(g("L0_ffn_sw2"))
    sh["sb1"] = np.ascontiguousarray(
        g("L0_ffn_sb1").reshape(32, 128).T)
    sh["wq1T"] = np.ascontiguousarray(g("L1_attn_Wq").T)
    sh["wk1T"] = np.ascontiguousarray(g("L1_attn_Wk").T)
    sh["wv1T"] = np.ascontiguousarray(g("L1_attn_Wv").T)
    sh["bq1"] = np.ascontiguousarray(g("L1_attn_bq").reshape(8, 128).T)
    sh["bk1row"] = g("L1_attn_bk").reshape(1, D).copy()
    sh["bv1row"] = g("L1_attn_bv").reshape(1, D).copy()
    sh["gamma1"] = np.ascontiguousarray(g("L1_attn_gamma").reshape(8, 128).T)
    sh["beta1"] = np.ascontiguousarray(g("L1_attn_beta").reshape(8, 128).T)
    sh["Wqkv1"] = np.ascontiguousarray(g("L1_ffn_Wqkv"))
    sh["Wo1"] = np.ascontiguousarray(g("L1_ffn_Wo"))
    sh["W11"] = np.ascontiguousarray(g("L1_ffn_W1"))
    sh["W21"] = np.ascontiguousarray(g("L1_ffn_W2"))
    b1 = g("L1_ffn_b1")
    b1a = np.zeros((6, 2816), f32); b1a[:, :2730] = b1[:, :2730]
    b1g = np.zeros((6, 2816), f32); b1g[:, :2730] = b1[:, 2730:]
    sh["b1a1"] = np.ascontiguousarray(b1a.reshape(6, 22, 128).transpose(0, 2, 1))
    sh["b1g1"] = np.ascontiguousarray(b1g.reshape(6, 22, 128).transpose(0, 2, 1))
    sh["b21row"] = np.ascontiguousarray(g("L1_ffn_b2"))
    sh["b21c"] = np.ascontiguousarray(g("L1_ffn_b2").reshape(6, 8, 128).transpose(0, 2, 1))
    sh["qs1"] = np.ascontiguousarray(g("L1_ffn_q_scale"))
    sh["ks1"] = np.ascontiguousarray(g("L1_ffn_k_scale"))
    sh["lng1"] = np.ascontiguousarray(g("L1_ffn_ln_g").reshape(6, 8, 128).transpose(0, 2, 1))
    sh["lnb1"] = np.ascontiguousarray(g("L1_ffn_ln_b").reshape(6, 8, 128).transpose(0, 2, 1))
    cores = []
    for c in range(NCORES):
        my = list(range(c * NT, (c + 1) * NT))
        rest = [t_ for t_ in range(NTILES) if t_ not in my]
        order = my + rest
        xg = xT.reshape(D, NTILES, 128)[:, order, :].reshape(D, NTOK)
        P = np.zeros((128, 128), f32)
        for tl, tg in enumerate(order):
            P[tl, tg] = 1.0
        dd = {"xg": np.ascontiguousarray(xg),
              "permT": P,
              "permT2": np.ascontiguousarray(P.T)}
        cores.append(dd)
    return sh, cores


ENGINES = ("tensor", "vector", "scalar", "gpsimd", "sync")
SAME_ENGINE_WAITS = True


class _Op:
    __slots__ = ("eng", "fn", "reads", "writes", "dma", "idx", "needs_inc",
                 "inc_val", "waits", "tag")

    def __init__(self, eng, fn, reads, writes, dma, idx, tag):
        self.eng = eng; self.fn = fn; self.reads = reads; self.writes = writes
        self.dma = dma; self.idx = idx; self.needs_inc = dma
        self.inc_val = None; self.waits = []; self.tag = tag


class Prog:

    def __init__(self, nc):
        self.nc = nc
        self.ops = {e: [] for e in ENGINES}
        self.all_ops = []
        self.last_write = {}
        self.readers = {}
        self.last_dma = {}
        self.psum_keys = set()
        self._pending_barrier = {}

    def mark_psum(self, *keys):
        self.psum_keys.update(keys)

    def op(self, eng, fn, reads=(), writes=(), dma=False, tag=""):
        o = _Op(eng, fn, list(reads), list(writes), dma, len(self.ops[eng]), tag)
        self.ops[eng].append(o)
        self.all_ops.append(o)
        deps = set()
        pb = self._pending_barrier.get(eng)
        if pb:
            for b_ in pb:
                if b_ is not o:
                    deps.add(b_)
            self._pending_barrier[eng] = None
        for r in o.reads:
            if r in self.psum_keys:
                for rd in self.readers.get(r, ()):
                    if rd.eng != eng and rd is not o:
                        deps.add(rd)
        if dma:
            prev = self.last_dma.get(eng)
            if prev is not None:
                deps.add(prev)
            self.last_dma[eng] = o
        for r in o.reads:
            p_ = self.last_write.get(r)
            if p_ is not None and p_ is not o:
                deps.add(p_)
            self.readers.setdefault(r, []).append(o)
        for w in o.writes:
            p_ = self.last_write.get(w)
            if p_ is not None and p_ is not o:
                deps.add(p_)
            for rd in self.readers.get(w, ()):
                if rd is not o:
                    deps.add(rd)
            self.last_write[w] = o
            self.readers[w] = []
        o.waits = list(deps)
        return o

    def barrier(self):
        snapshot = [self.ops[e][-1] for e in ENGINES if self.ops[e]]
        self._pending_barrier = {e: list(snapshot) for e in ENGINES}

    def emit(self):
        nc = self.nc
        for o in self.all_ops:
            for p_ in o.waits:
                p_.needs_inc = True
        for e in ENGINES:
            comp = [o for o in self.ops[e] if not o.dma]
            if comp:
                comp[-1].needs_inc = True
        for e in ENGINES:
            n_c = 0; n_d = 0
            for o in self.ops[e]:
                if o.dma:
                    n_d += 1
                    o.inc_val = 16 * n_d
                elif o.needs_inc:
                    n_c += 1
                    o.inc_val = n_c
        sems = {}
        for e in ENGINES:
            sems[("c", e)] = nc.semaphore(f"sem_c_{e}").__enter__()
            if any(o.dma for o in self.ops[e]):
                sems[("d", e)] = nc.semaphore(f"sem_d_{e}").__enter__()
        self._sems = sems
        finals = {}
        for e in ENGINES:
            cvals = [o.inc_val for o in self.ops[e] if (not o.dma) and o.needs_inc]
            dvals = [o.inc_val for o in self.ops[e] if o.dma]
            finals[("c", e)] = cvals[-1] if cvals else 0
            if dvals:
                finals[("d", e)] = dvals[-1]
        hb = {}

        with nc.Block() as block:
            for e in ENGINES:
                dec = getattr(block, e)

                def body(eng_obj, e=e):
                    for o in self.ops[e]:
                        for p_ in o.waits:
                            if p_.eng == e and not p_.dma:
                                if e == "tensor":
                                    continue
                                if not SAME_ENGINE_WAITS:
                                    continue
                            key = ("d", p_.eng) if p_.dma else ("c", p_.eng)
                            assert p_.inc_val is not None, (o.tag, p_.tag)
                            hw = hb.get((e, key), 0)
                            if p_.inc_val > hw:
                                eng_obj.wait_ge(sems[key], p_.inc_val)
                                hb[(e, key)] = p_.inc_val
                        inst = o.fn(eng_obj)
                        if o.needs_inc:
                            key = ("d", e) if o.dma else ("c", e)
                            assert inst is not None, (e, o.tag)
                            inst.then_inc(sems[key], 16 if o.dma else 1)
                    if ("d", e) in sems:
                        eng_obj.wait_ge(sems[("d", e)], finals[("d", e)])
                dec(body)
        return sems


def phase_ln_gating(p, d, es):
    nc = p.nc
    sb = lambda name, shape, dt: es.enter_context(nc.sbuf_tensor(name, shape, dt))
    ps = lambda name, shape, dt: es.enter_context(nc.psum_tensor(name, shape, dt))

    ones_f = sb("g_ones_f", [128, 128], F32)
    ones_bf = sb("g_ones_bf", [128, 128], BF16)
    tstrict = sb("g_tstrict", [128, 128], F32)
    iota8 = sb("g_iota8", [1, 8], F32)
    ident_f = sb("g_ident_f", [128, 128], F32)
    permT = sb("g_permT", [128, 128], F32)
    permT2 = sb("g_permT2", [128, 128], F32)
    rw_f = sb("g_rw_f", [128, 8 * 9], F32)
    for nm, t_ in (("ones_f", ones_f), ("tstrict", tstrict),
                   ("iota8", iota8), ("ident_f", ident_f), ("permT", permT),
                   ("permT2", permT2)):
        p.op("sync", lambda s, t_=t_, nm=nm: s.dma_start(t_[:, :], d[nm][:, :]),
             writes=[t_.name], dma=True, tag="ld_" + nm)
    p.op("gpsimd", lambda g_: g_.dma_start(ones_bf[:, :], d["ones_f"][:, :]),
         writes=[ones_bf.name], dma=True, tag="ld_ones_bf")
    p.op("sync", lambda s: s.dma_start(
        rw_f[:, :].rearrange("p (k n) -> p k n", n=9),
        d["rw_ext"][:, :].rearrange("(k p) n -> p k n", p=128)),
        writes=[rw_f.name], dma=True, tag="ld_rw")

    xgb = [sb(f"g_xgb{i}", [128, 8 * 512], F32) for i in range(2)]
    xsq = [sb(f"g_xsq{i}", [128, 8 * 512], F32) for i in range(2)]
    rs_ps = [ps(f"g_rs{i}", [128, 512], F32) for i in range(2)]
    p.mark_psum(rs_ps[0].name, rs_ps[1].name)
    RS = sb("g_RS", [128, 9 * 128], F32)
    XSQS = sb("g_XSQS", [128, 128], F32)
    RS9 = RS[:, :].rearrange("p (t n) -> p t n", n=9)

    nblk = NTOK // 512
    for b in range(nblk):
        buf = xgb[b % 2]; sq = xsq[b % 2]; rp = rs_ps[b % 2]
        p.op("sync" if b % 2 == 0 else "gpsimd", lambda g_, b=b, buf=buf: g_.dma_start(
            buf[:, :].rearrange("p (k n) -> p k n", n=512),
            d["xg"][:, b * 512:(b + 1) * 512].rearrange("(k p) n -> p k n", p=128)),
            writes=[buf.name], dma=True, tag=f"ld_xg{b}")
        p.op("scalar", lambda s, buf=buf, sq=sq: s.activation(
            sq[:, :], buf[:, :], AF.Square),
            reads=[buf.name], writes=[sq.name], tag=f"sq{b}")

        def mm_blk(t_eng, buf=buf, sq=sq, rp=rp):
            for i in range(4):
                for kt in range(8):
                    i1 = t_eng.matmul(
                        rp[:, i * 9:i * 9 + 9],
                        buf[:, kt * 512 + i * 128: kt * 512 + (i + 1) * 128],
                        rw_f[:, kt * 9:(kt + 1) * 9],
                        start=(kt == 0), stop=(kt == 7))
                for kt in range(8):
                    i1 = t_eng.matmul(
                        rp[:, 36 + i: 37 + i],
                        sq[:, kt * 512 + i * 128: kt * 512 + (i + 1) * 128],
                        ones_f[:, 0:1],
                        start=(kt == 0), stop=(kt == 7))
            return i1
        p.op("tensor", mm_blk, reads=[buf.name, sq.name, rw_f.name, ones_f.name],
             writes=[rp.name], tag=f"mm_blk{b}")
        p.op("vector", lambda v, b=b, rp=rp: v.tensor_copy(
            RS[:, b * 36:(b + 1) * 36], rp[:, 0:36]),
            reads=[rp.name], writes=["RSk"], tag=f"ev_rs{b}")
        p.op("vector", lambda v, b=b, rp=rp: v.tensor_copy(
            XSQS[:, b * 4:(b + 1) * 4], rp[:, 36:40]),
            reads=[rp.name], writes=[XSQS.name], tag=f"ev_sq{b}")

    MU = sb("g_MU", [128, 128], F32)
    RSTD = sb("g_RSTD", [128, 128], F32)
    ZL = sb("g_ZL", [128, 1024], F32)
    scr = sb("g_scr", [128, 1024], F32)
    scr2 = sb("g_scr2", [128, 1024], F32)
    sml = sb("g_sml", [128, 128 * 12], F32)

    def S(i):
        return sml[:, i * 128:(i + 1) * 128]

    def Sk(i):
        return f"sml{i}"
    psA = ps("g_psA", [128, 512], F32)
    psB = ps("g_psB", [128, 512], F32)
    psC = ps("g_psC", [128, 512], F32)
    psD = ps("g_psD", [128, 512], F32)
    p.mark_psum(psA.name, psB.name, psC.name, psD.name)

    p.op("vector", lambda v: v.tensor_scalar(
        MU[:, :].unsqueeze(2), RS9[:, :, 8:9], -1.0 / 1024.0, None, ALU.mult),
        reads=["RSk"], writes=[MU.name], tag="mu")
    p.op("vector", lambda v: v.tensor_tensor(S(0), MU[:, :], MU[:, :], ALU.mult),
         reads=[MU.name], writes=[Sk(0)], tag="mu2")
    p.op("vector", lambda v: v.tensor_scalar(S(1), XSQS[:, :], 1.0 / 1024.0, None, ALU.mult),
         reads=[XSQS.name], writes=[Sk(1)], tag="ex2")
    p.op("vector", lambda v: v.tensor_tensor(S(2), S(1), S(0), ALU.subtract),
         reads=[Sk(0), Sk(1)], writes=[Sk(2)], tag="var")
    epsc = sb("g_epsc", [128, 1], F32)
    p.op("vector", lambda v: v.memset(epsc[:, :], 1e-5), writes=[epsc.name], tag="epsc")
    p.op("scalar", lambda s: s.activation(S(3), S(2), AF.Ln, bias=epsc[:, :]),
         reads=[Sk(2), epsc.name], writes=[Sk(3)], tag="lnvar")
    p.op("scalar", lambda s: s.activation(RSTD[:, :], S(3), AF.Exp, scale=-0.5),
         reads=[Sk(3)], writes=[RSTD.name], tag="rstd")

    CSb = sb("g_CSb", [128, 9], F32)
    cs_row = sb("g_csrow", [1, 9], F32)

    def mm_cs(t):
        for kt in range(8):
            i1 = t.matmul(psA[0:1, 0:9], ones_f[:, 0:1], rw_f[:, kt * 9:(kt + 1) * 9],
                          start=(kt == 0), stop=(kt == 7))
        return i1
    p.op("tensor", mm_cs, reads=[ones_f.name, rw_f.name], writes=[psA.name], tag="cs")
    p.op("vector", lambda v: v.tensor_copy(cs_row[:, :], psA[0:1, 0:9]),
         reads=[psA.name], writes=[cs_row.name], tag="ev_cs")
    p.op("tensor", lambda t: t.matmul(psB[:, 0:9], ones_f[0:1, :], cs_row[:, :],
                                      start=True, stop=True),
         reads=[ones_f.name, cs_row.name], writes=[psB.name], tag="csb")
    p.op("vector", lambda v: v.tensor_copy(CSb[:, :], psB[:, 0:9]),
         reads=[psB.name], writes=[CSb.name], tag="ev_csb")

    for e in range(8):
        p.op("vector", lambda v, e=e: v.scalar_tensor_tensor(
            ZL[:, e * 128:(e + 1) * 128].unsqueeze(2), MU[:, :].unsqueeze(2),
            CSb[:, e:e + 1], RS9[:, :, e:e + 1], ALU.mult, ALU.add),
            reads=[MU.name, CSb.name, "RSk"], writes=[ZL.name], tag=f"zl{e}")

    IOTA8M = sb("g_iota8m", [128, 8], F32)
    IOm8 = sb("g_iom8", [128, 8], F32)
    p.op("tensor", lambda t: t.matmul(psA[:, 0:8], ones_f[0:1, :], iota8[:, :],
                                      start=True, stop=True),
         reads=[ones_f.name, iota8.name], writes=[psA.name], tag="iom")
    p.op("vector", lambda v: v.tensor_copy(IOTA8M[:, :], psA[:, 0:8]),
         reads=[psA.name], writes=[IOTA8M.name], tag="ev_iom")
    p.op("vector", lambda v: v.tensor_scalar(IOm8[:, :], IOTA8M[:, :], -8.0, None, ALU.add),
         reads=[IOTA8M.name], writes=[IOm8.name], tag="iom8")

    def v3(ap2):
        return ap2.rearrange("p (e t) -> p t e", t=128)

    def bc(ap128):
        return ap128.unsqueeze(2).broadcast_to((128, 128, 8))
    iob8 = IOm8[:, :].unsqueeze(1).broadcast_to((128, 128, 8))

    MSK1 = sb("g_msk1", [128, 1024], F32)
    MSK2 = sb("g_msk2", [128, 1024], F32)
    M1 = S(0); M2 = S(1)

    p.op("vector", lambda v: v.tensor_reduce(M1, v3(ZL[:, :]), AX.X, ALU.max),
         reads=[ZL.name], writes=[Sk(0)], tag="m1")
    p.op("vector", lambda v: v.tensor_tensor(v3(scr2[:, :]), v3(ZL[:, :]), bc(M1), ALU.is_ge),
         reads=[ZL.name, Sk(0)], writes=[scr2.name], tag="eq1")
    p.op("vector", lambda v: v.tensor_tensor(v3(scr[:, :]), v3(scr2[:, :]), iob8, ALU.mult),
         reads=[scr2.name], writes=[scr.name], tag="c1")
    IDX1 = S(2)
    p.op("vector", lambda v: v.tensor_reduce(IDX1, v3(scr[:, :]), AX.X, ALU.min),
         reads=[scr.name], writes=[Sk(2)], tag="idx1")
    p.op("vector", lambda v: v.tensor_tensor(v3(MSK1[:, :]), iob8, bc(IDX1), ALU.is_equal),
         reads=[Sk(2)], writes=[MSK1.name], tag="msk1")

    p.op("vector", lambda v: v.scalar_tensor_tensor(
        v3(scr[:, :]), v3(MSK1[:, :]), NEG_BIG, v3(ZL[:, :]), ALU.mult, ALU.add),
        reads=[MSK1.name, ZL.name], writes=[scr.name], tag="zl2")
    p.op("vector", lambda v: v.tensor_reduce(M2, v3(scr[:, :]), AX.X, ALU.max),
         reads=[scr.name], writes=[Sk(1)], tag="m2")
    p.op("vector", lambda v: v.tensor_tensor(v3(scr2[:, :]), v3(scr[:, :]), bc(M2), ALU.is_ge),
         reads=[scr.name, Sk(1)], writes=[scr2.name], tag="eq2")
    p.op("vector", lambda v: v.tensor_tensor(v3(scr[:, :]), v3(scr2[:, :]), iob8, ALU.mult),
         reads=[scr2.name], writes=[scr.name], tag="c2")
    IDX2 = S(3)
    p.op("vector", lambda v: v.tensor_reduce(IDX2, v3(scr[:, :]), AX.X, ALU.min),
         reads=[scr.name], writes=[Sk(3)], tag="idx2")
    p.op("vector", lambda v: v.tensor_tensor(v3(MSK2[:, :]), iob8, bc(IDX2), ALU.is_equal),
         reads=[Sk(3)], writes=[MSK2.name], tag="msk2")

    L1v = S(4); G2P = S(5)
    p.op("vector", lambda v: v.tensor_tensor(L1v, M1, RSTD[:, :], ALU.mult),
         reads=[Sk(0), RSTD.name], writes=[Sk(4)], tag="l1")
    p.op("vector", lambda v: v.tensor_tensor(S(6), M2, M1, ALU.subtract),
         reads=[Sk(0), Sk(1)], writes=[Sk(6)], tag="dm")
    p.op("vector", lambda v: v.tensor_tensor(S(7), S(6), RSTD[:, :], ALU.mult),
         reads=[Sk(6), RSTD.name], writes=[Sk(7)], tag="dm2")
    p.op("scalar", lambda s: s.activation(G2P, S(7), AF.Exp),
         reads=[Sk(7)], writes=[Sk(5)], tag="g2p")
    p.op("vector", lambda v: v.tensor_tensor(v3(scr[:, :]), v3(ZL[:, :]), bc(RSTD[:, :]), ALU.mult),
         reads=[ZL.name, RSTD.name], writes=[scr.name], tag="lz")
    p.op("vector", lambda v: v.tensor_tensor(v3(scr[:, :]), v3(scr[:, :]), bc(L1v), ALU.subtract),
         reads=[scr.name, Sk(4)], writes=[scr.name], tag="lzs")
    p.op("scalar", lambda s: s.activation(scr2[:, :], scr[:, :], AF.Exp),
         reads=[scr.name], writes=[scr2.name], tag="exps")
    Ssum = S(6)
    p.op("vector", lambda v: v.tensor_reduce(Ssum, v3(scr2[:, :]), AX.X, ALU.add),
         reads=[scr2.name], writes=[Sk(6)], tag="ssum")

    Stp = sb("g_stp", [128, 8], F32)
    CAR = sb("g_car", [128, 8], F32)
    crow = sb("g_crow", [1, 1024], F32)
    c1p = sb("g_c1p", [1, 8], F32)
    POSK = sb("g_keep", [128, 1024], F32)

    def cumsum_pass(MSKt, mskkey, with_offset, tagp):
        def mm_stp(t, MSKt=MSKt):
            for e in range(8):
                i1 = t.matmul(psC[:, e:e + 1], MSKt[:, e * 128:(e + 1) * 128],
                              ones_f[:, 0:1], start=(e == 0), stop=(e == 7))
            return i1
        p.op("tensor", mm_stp, reads=[mskkey, ones_f.name], writes=[psC.name],
             tag=tagp + "stp")
        p.op("vector", lambda v: v.tensor_copy(Stp[:, :], psC[:, 0:8]),
             reads=[psC.name], writes=[Stp.name], tag=tagp + "evstp")
        p.op("tensor", lambda t: t.matmul(psD[:, 0:8], permT[:, :], Stp[:, :],
                                          start=True, stop=True),
             reads=[permT.name, Stp.name], writes=[psD.name], tag=tagp + "perm")
        p.op("vector", lambda v: v.tensor_copy(Stp[:, :], psD[:, 0:8]),
             reads=[psD.name], writes=[Stp.name], tag=tagp + "evperm")
        if with_offset:
            def mm_car(t):
                t.matmul(psC[:, 0:8], tstrict[:, :], Stp[:, :], start=True, stop=False)
                return t.matmul(psC[:, 0:8], ones_f[0:1, :], c1p[:, :],
                                start=False, stop=True)
            rd = [tstrict.name, Stp.name, ones_f.name, c1p.name]
        else:
            def mm_car(t):
                return t.matmul(psC[:, 0:8], tstrict[:, :], Stp[:, :],
                                start=True, stop=True)
            rd = [tstrict.name, Stp.name]
        p.op("tensor", mm_car, reads=rd, writes=[psC.name], tag=tagp + "car")
        p.op("vector", lambda v: v.tensor_copy(CAR[:, :], psC[:, 0:8]),
             reads=[psC.name], writes=[CAR.name], tag=tagp + "evcar")
        p.op("tensor", lambda t: t.matmul(psD[:, 0:8], permT2[:, :], CAR[:, :],
                                          start=True, stop=True),
             reads=[permT2.name, CAR.name], writes=[psD.name], tag=tagp + "carl")
        p.op("vector", lambda v: v.tensor_copy(CAR[:, :], psD[:, 0:8]),
             reads=[psD.name], writes=[CAR.name], tag=tagp + "evcarl")
        p.op("tensor", lambda t: t.transpose(psD[0:8, 0:128], CAR[:, :], ident_f[:, :]),
             reads=[CAR.name, ident_f.name], writes=[psD.name], tag=tagp + "trc")
        CAR8 = sb(f"g_car8{tagp}", [8, 128], F32)
        p.op("vector", lambda v, CAR8=CAR8: v.tensor_copy(CAR8[:, :], psD[0:8, 0:128]),
             reads=[psD.name], writes=[CAR8.name], tag=tagp + "evtrc")
        p.op("sync", lambda s, CAR8=CAR8: s.dma_start(d["car_d"][:, :], CAR8[:, :]),
             reads=[CAR8.name], writes=["car_d"], dma=True, tag=tagp + "st_car")
        p.op("sync", lambda s: s.dma_start(
            crow[:, :].rearrange("o (e t) -> o e t", t=128),
            d["car_d"][:, :].unsqueeze(0)),
            reads=["car_d"], writes=[crow.name], dma=True, tag=tagp + "ld_crow")
        def mm_posA(t, MSKt=MSKt):
            t.matmul(psD[:, 0:512], tstrict[:, :], MSKt[:, 0:512],
                     start=True, stop=False)
            return t.matmul(psD[:, 0:512], ones_f[0:1, :], crow[:, 0:512],
                            start=False, stop=True)
        def mm_posB(t, MSKt=MSKt):
            t.matmul(psC[:, 0:512], tstrict[:, :], MSKt[:, 512:1024],
                     start=True, stop=False)
            return t.matmul(psC[:, 0:512], ones_f[0:1, :], crow[:, 512:1024],
                            start=False, stop=True)
        p.op("tensor", mm_posA, reads=[mskkey, tstrict.name, ones_f.name, crow.name],
             writes=[psD.name], tag=tagp + "posA")
        p.op("tensor", mm_posB, reads=[mskkey, tstrict.name, ones_f.name, crow.name],
             writes=[psC.name], tag=tagp + "posB")
        p.op("vector", lambda v: v.tensor_scalar(
            POSK[:, 0:512], psD[:, 0:512], float(CAPACITY), None, ALU.is_lt),
            reads=[psD.name], writes=[POSK.name], tag=tagp + "keepA")
        p.op("vector", lambda v: v.tensor_scalar(
            POSK[:, 512:1024], psC[:, 0:512], float(CAPACITY), None, ALU.is_lt),
            reads=[psC.name], writes=[POSK.name], tag=tagp + "keepB")
        p.op("vector", lambda v, MSKt=MSKt: v.tensor_tensor(
            MSKt[:, :], MSKt[:, :], POSK[:, :], ALU.mult),
            reads=[POSK.name, mskkey], writes=[mskkey], tag=tagp + "maskp")

    cumsum_pass(MSK1[:, :], MSK1.name, False, "c1_")

    def mm_c1p(t):
        for e in range(8):
            i1 = t.matmul(psC[:, e:e + 1], MSK1[:, e * 128:(e + 1) * 128],
                          ones_f[:, 0:1], start=(e == 0), stop=(e == 7))
        return i1
    p.op("tensor", mm_c1p, reads=[MSK1.name, ones_f.name], writes=[psC.name], tag="c1p_stp")
    p.op("vector", lambda v: v.tensor_copy(Stp[:, :], psC[:, 0:8]),
         reads=[psC.name], writes=[Stp.name], tag="c1p_ev")
    p.op("tensor", lambda t: t.matmul(psD[0:1, 0:8], ones_f[:, 0:1], Stp[:, :],
                                      start=True, stop=True),
         reads=[ones_f.name, Stp.name], writes=[psD.name], tag="c1p_tot")
    p.op("vector", lambda v: v.tensor_copy(c1p[:, :], psD[0:1, 0:8]),
         reads=[psD.name], writes=[c1p.name], tag="c1p_evt")
    cumsum_pass(MSK2[:, :], MSK2.name, True, "c2_")

    K1 = S(8); K2 = S(9)
    p.op("vector", lambda v: v.tensor_reduce(K1, v3(MSK1[:, :]), AX.X, ALU.add),
         reads=[MSK1.name], writes=[Sk(8)], tag="k1")
    p.op("vector", lambda v: v.tensor_reduce(K2, v3(MSK2[:, :]), AX.X, ALU.add),
         reads=[MSK2.name], writes=[Sk(9)], tag="k2")
    RS_ = S(10)
    p.op("vector", lambda v: v.reciprocal(RS_, Ssum),
         reads=[Sk(6)], writes=[Sk(10)], tag="rS")
    G1 = S(4); G2 = S(5)
    p.op("vector", lambda v: v.tensor_tensor(G1, K1, RS_, ALU.mult),
         reads=[Sk(8), Sk(10)], writes=[Sk(4)], tag="g1v")
    p.op("vector", lambda v: v.tensor_tensor(G2, G2P, RS_, ALU.mult),
         reads=[Sk(5), Sk(10)], writes=[Sk(5)], tag="g2v0")
    p.op("vector", lambda v: v.tensor_tensor(G2, G2, K2, ALU.mult),
         reads=[Sk(5), Sk(9)], writes=[Sk(5)], tag="g2v")
    DEN = S(11)
    p.op("vector", lambda v: v.tensor_tensor(DEN, G1, G2, ALU.add),
         reads=[Sk(4), Sk(5)], writes=[Sk(11)], tag="den0")
    p.op("vector", lambda v: v.tensor_scalar(DEN, DEN, 1.1920929e-07, None, ALU.max),
         reads=[Sk(11)], writes=[Sk(11)], tag="den")
    p.op("vector", lambda v: v.reciprocal(S(7), DEN),
         reads=[Sk(11)], writes=[Sk(7)], tag="rden")
    p.op("vector", lambda v: v.tensor_tensor(G1, G1, S(7), ALU.mult),
         reads=[Sk(4), Sk(7)], writes=[Sk(4)], tag="gh1")
    p.op("vector", lambda v: v.tensor_tensor(G2, G2, S(7), ALU.mult),
         reads=[Sk(5), Sk(7)], writes=[Sk(5)], tag="gh2")
    Wg = sb("g_Wg", [128, 1024], F32)
    p.op("vector", lambda v: v.tensor_tensor(v3(Wg[:, :]), v3(MSK1[:, :]), bc(G1), ALU.mult),
         reads=[MSK1.name, Sk(4)], writes=[Wg.name], tag="wg1")
    p.op("vector", lambda v: v.tensor_tensor(v3(scr[:, :]), v3(MSK2[:, :]), bc(G2), ALU.mult),
         reads=[MSK2.name, Sk(5)], writes=[scr.name], tag="wg2")
    p.op("vector", lambda v: v.tensor_tensor(Wg[:, :], Wg[:, :], scr[:, :], ALU.add),
         reads=[scr.name, Wg.name], writes=[Wg.name], tag="wg")
    p.op("sync", lambda s: s.dma_start(d["Wg_d"][:, :], Wg[:, :]),
         reads=[Wg.name], writes=["Wg_d"], dma=True, tag="st_wg")

    WT9 = sb("g_WT9", [33, 2048], F32)
    Wg3 = Wg[:, :].rearrange("p (e t) -> p t e", t=128)
    p.op("vector", lambda v: v.memset(WT9[:, :], 0.0), writes=[WT9.name], tag="wt9z")
    p.op("vector", lambda v: v.memset(WT9[32:33, :], 0.1),
         reads=[WT9.name], writes=["WT9s"], tag="wt9s")
    for g_ in range(4):
        def mm_wt9g(t, g_=g_):
            for j in range(4):
                i1 = t.transpose(psC[0:8, j * 128:(j + 1) * 128],
                                 Wg3[:, g_ * 4 + j:g_ * 4 + j + 1, :].squeeze(1),
                                 ident_f[:, :])
            return i1
        p.op("tensor", mm_wt9g, reads=[Wg.name, ident_f.name], writes=[psC.name],
             tag=f"wt9t{g_}")
        p.op("vector", lambda v, g_=g_: v.tensor_copy(
            WT9[0:8, g_ * 512:(g_ + 1) * 512], psC[0:8, 0:512]),
            reads=[psC.name, WT9.name], writes=[WT9.name], tag=f"wt9e{g_}")
    p.op("sync", lambda s: s.dma_start(d["WT9_d"][:, :], WT9[:, :]),
         reads=[WT9.name, "WT9s"], writes=["WT9_d"], dma=True, tag="st_wt9")

    Wtr = sb("g_wtr", [16, 1024], F32)
    for g_ in range(2):
        def mm_wtr(t, g_=g_):
            for j in range(4):
                e = g_ * 4 + j
                i1 = t.transpose(psD[0:16, j * 128:(j + 1) * 128],
                                 Wg[:, e * 128:e * 128 + 16], ident_f[:, :])
            return i1
        p.op("tensor", mm_wtr, reads=[Wg.name, ident_f.name], writes=[psD.name],
             tag=f"wtrt{g_}")
        p.op("vector", lambda v, g_=g_: v.tensor_copy(
            Wtr[:, g_ * 512:(g_ + 1) * 512], psD[0:16, 0:512]),
            reads=[psD.name, Wtr.name], writes=[Wtr.name], tag=f"wtre{g_}")
    p.op("sync", lambda s: s.dma_start(d["Wtr_d"][:, :], Wtr[:, :]),
         reads=[Wtr.name], writes=["Wtr_d"], dma=True, tag="st_wtr")

    MR = sb("g_mr", [16, 256], F32)
    p.op("tensor", lambda t: t.transpose(psC[0:16, 0:128], MU[:, 0:16], ident_f[:, :]),
         reads=[MU.name, ident_f.name], writes=[psC.name], tag="trmu")
    p.op("vector", lambda v: v.tensor_copy(MR[:, 0:128], psC[0:16, 0:128]),
         reads=[psC.name], writes=[MR.name], tag="evmu")
    p.op("tensor", lambda t: t.transpose(psD[0:16, 0:128], RSTD[:, 0:16], ident_f[:, :]),
         reads=[RSTD.name, ident_f.name], writes=[psD.name], tag="trrs")
    p.op("vector", lambda v: v.tensor_copy(MR[:, 128:256], psD[0:16, 0:128]),
         reads=[psD.name, MR.name], writes=[MR.name], tag="evrs")
    p.op("sync", lambda s: s.dma_start(d["mr_d"][:, :], MR[:, :]),
         reads=[MR.name], writes=["mr_d"], dma=True, tag="st_mr")
    murow = sb("g_murow", [1, 2048], F32)
    arow = sb("g_arow", [1, 2048], F32)
    p.op("sync", lambda s: s.dma_start(
        murow[:, :].rearrange("o (t p) -> o t p", p=128),
        d["mr_d"][:, 0:128].unsqueeze(0)),
        reads=["mr_d"], writes=[murow.name], dma=True, tag="ld_murow")
    p.op("sync", lambda s: s.dma_start(
        arow[:, :].rearrange("o (t p) -> o t p", p=128),
        d["mr_d"][:, 128:256].unsqueeze(0)),
        reads=["mr_d"], writes=[arow.name], dma=True, tag="ld_arow")
    p.op("sync", lambda s: s.dma_start(d["murow_d"][:, :], murow[:, :]),
         reads=[murow.name], writes=["murow_d"], dma=True, tag="st_murow")
    p.op("sync", lambda s: s.dma_start(d["arow_d"][:, :], arow[:, :]),
         reads=[arow.name], writes=["arow_d"], dma=True, tag="st_arow")

    Mb = sb("g_Mb", [128, 2048], F32)
    Ab = sb("g_Ab", [128, 2048], F32)
    for half in range(4):
        p.op("tensor", lambda t, half=half: t.matmul(
            psC[:, 0:512], ones_f[0:1, :], murow[:, half * 512:(half + 1) * 512],
            start=True, stop=True),
            reads=[ones_f.name, murow.name], writes=[psC.name], tag=f"bmu{half}")
        p.op("vector", lambda v, half=half: v.tensor_copy(
            Mb[:, half * 512:(half + 1) * 512], psC[:, 0:512]),
            reads=[psC.name], writes=[Mb.name], tag=f"evbmu{half}")
        p.op("tensor", lambda t, half=half: t.matmul(
            psD[:, 0:512], ones_f[0:1, :], arow[:, half * 512:(half + 1) * 512],
            start=True, stop=True),
            reads=[ones_f.name, arow.name], writes=[psD.name], tag=f"bar{half}")
        p.op("vector", lambda v, half=half: v.tensor_copy(
            Ab[:, half * 512:(half + 1) * 512], psD[:, 0:512]),
            reads=[psD.name], writes=[Ab.name], tag=f"evbar{half}")

    xres = sb("g_x", [128, 2048], F32)
    xnscr = sb("g_xnscr", [128, 2048], F32)
    xnb = sb("g_xnb", [128, 2048], BF16)
    for dt in range(ND):
        p.op("sync", lambda s, dt=dt: s.dma_start(
            xres[:, :], d["xg"][dt * 128:(dt + 1) * 128, 0:2048]),
            writes=[xres.name], dma=True, tag=f"ldx{dt}")
        p.op("vector", lambda v: v.tensor_tensor(
            xnscr[:, :], xres[:, :], Mb[:, :], ALU.add),
            reads=[xres.name, Mb.name], writes=[xnscr.name], tag=f"xn0_{dt}")
        p.op("vector", lambda v: v.tensor_tensor(
            xnb[:, :], xnscr[:, :], Ab[:, :], ALU.mult),
            reads=[xnscr.name, Ab.name], writes=[xnb.name], tag=f"xn1_{dt}")
        p.op("sync", lambda s, dt=dt: s.dma_start(
            d["xn_d"][dt * 128:(dt + 1) * 128, :], xnb[:, :]),
            reads=[xnb.name], writes=["xn_d"], dma=True, tag=f"stxn{dt}")



def phase_l0_attn(p, d, es):
    nc = p.nc
    sb = lambda name, shape, dt: es.enter_context(nc.sbuf_tensor(name, shape, dt))
    ps = lambda name, shape, dt: es.enter_context(nc.psum_tensor(name, shape, dt))

    ident_bf = sb("b_ident_bf", [128, 128], BF16)
    p.op("gpsimd", lambda g_: g_.dma_start(ident_bf[:, :], d["ident_f"][:, :]),
         writes=[ident_bf.name], dma=True, tag="b_ld_ident")
    ones_fb = sb("b_ones_fb", [1, 128], F32)
    p.op("sync", lambda s: s.dma_start(ones_fb[:, :], d["ones_f"][0:1, :]),
         writes=["b_onesf"], dma=True, tag="b_ldonesf")
    ident_f2 = sb("b_ident_f2", [128, 128], F32)
    p.op("sync", lambda s: s.dma_start(ident_f2[:, :], d["ident_f"][:, :]),
         writes=["b_identf2"], dma=True, tag="b_ldif2")
    cosT = sb("b_cos", [128, 128], F32)
    sinT = sb("b_sin", [128, 128], F32)
    p.op("sync", lambda s: s.dma_start(cosT[:, :], d["rope0_cos"][:, :]),
         writes=[cosT.name], dma=True, tag="b_ld_cos")
    p.op("sync", lambda s: s.dma_start(sinT[:, :], d["rope0_sin"][:, :]),
         writes=[sinT.name], dma=True, tag="b_ld_sin")

    A4a = sb("b_A4a", [128, 16384], BF16)
    qT = sb("b_qT", [128, 16384], BF16)
    kT = sb("b_kT", [128, 16384], BF16)
    v65 = sb("b_v65", [128, 16384], BF16)
    lat = sb("b_lat", [128, 8192], BF16)
    wq_bf = sb("b_wq", [128, 8192], BF16)
    wsm = sb("b_wsm", [128, 8192], BF16)
    P1 = ps("b_P1", [128, 512], F32)
    P2 = ps("b_P2", [128, 512], F32)
    TP = ps("b_TP", [128, 2048], BF16)
    SC = ps("b_SC", [128, 2048], F32)
    SCKS = ["b_SC0a", "b_SC1a"]
    p.mark_psum(P1.name, P2.name, TP.name, "b_SC0a", "b_SC1a", "b_TP0", "b_TP1")

    xn = A4a
    XN = "b_xn"; AO = "b_ao"

    for dt_ in range(ND):
        p.op("sync", lambda s, dt_=dt_: s.dma_start(
            xn[:, dt_ * 2048:(dt_ + 1) * 2048], d["xn_d"][dt_ * 128:(dt_ + 1) * 128, :]),
            writes=[XN], dma=True, tag=f"b_ldxn{dt_}")
    p.op("gpsimd", lambda g_: g_.dma_start(
        wq_bf[:, :].rearrange("p (k n) -> p k n", n=1024),
        d["wqT"][:, :].rearrange("(k p) n -> p k n", p=128)),
        writes=["b_wq1"], dma=True, tag="b_ldwq")
    p.op("gpsimd", lambda g_: g_.dma_start(
        wsm[:, 0:2048].rearrange("p (k n) -> p k n", n=256),
        d["wkaT"][:, :].rearrange("(k p) n -> p k n", p=128)),
        writes=["b_wka"], dma=True, tag="b_ldwka")
    p.op("gpsimd", lambda g_: g_.dma_start(
        wsm[:, 2048:4096].rearrange("p (k n) -> p k n", n=1024),
        d["wkbT"][:, :].rearrange("(k p) n -> p k n", p=128)),
        writes=["b_wkb"], dma=True, tag="b_ldwkb")
    p.op("gpsimd", lambda g_: g_.dma_start(
        wsm[:, 4096:6144].rearrange("p (k n) -> p k n", n=256),
        d["wvaT"][:, :].rearrange("(k p) n -> p k n", p=128)),
        writes=["b_wva"], dma=True, tag="b_ldwva")
    p.op("gpsimd", lambda g_: g_.dma_start(
        wsm[:, 6144:8192].rearrange("p (k n) -> p k n", n=1024),
        d["wvbT"][:, :].rearrange("(k p) n -> p k n", p=128)),
        writes=["b_wvb"], dma=True, tag="b_ldwvb")

    for which, off, wkey, woff in (("k", 0, "b_wka", 0), ("v", 4096, "b_wva", 4096)):
        for m_ in range(2):
            for nch in range(4):
                pp = P1 if (nch % 2 == 0) else P2
                def mm_lat(t, m_=m_, nch=nch, pp=pp, woff=woff):
                    for kt in range(8):
                        i1 = t.matmul(pp[:, 0:512],
                                      wsm[:, woff + kt * 256 + m_ * 128: woff + kt * 256 + (m_ + 1) * 128],
                                      xn[:, kt * 2048 + nch * 512: kt * 2048 + (nch + 1) * 512],
                                      start=(kt == 0), stop=(kt == 7))
                    return i1
                p.op("tensor", mm_lat, reads=[wkey, XN], writes=[pp.name],
                     tag=f"b_lat{which}{m_}{nch}")
                p.op("vector", lambda v, m_=m_, nch=nch, pp=pp, off=off: v.tensor_copy(
                    lat[:, off + m_ * 2048 + nch * 512: off + m_ * 2048 + (nch + 1) * 512][0:128, :],
                    pp[:, 0:512]),
                    reads=[pp.name], writes=["b_lat" + which], tag=f"b_evlat{which}{m_}{nch}")

    if B_TRUNC == 1:
        return
    qtok = [sb(f"b_qtok{i}", [128, 1024], BF16) for i in range(3)]
    rsc = [sb(f"b_rsc{i}", [128, 256], F32) for i in range(3)]
    rs2 = [sb(f"b_rs2{i}", [128, 256], F32) for i in range(3)]
    rs3 = [sb(f"b_rs3{i}", [128, 256], F32) for i in range(3)]

    qn2 = sb("b_qn2", [128, 16 * 16], F32)
    kn2 = sb("b_kn2", [128, 16 * 16], F32)
    nsq = sb("b_nsq", [128, 1024], F32)

    def norms_from(tt, buf, dst2, dkey2, tag):
        p.op("scalar", lambda s, buf=buf: s.activation(nsq[:, :], buf[:, :], AF.Square),
             reads=[buf.name], writes=[nsq.name], tag=tag + "sq")
        p.op("vector", lambda v, tt=tt, dst2=dst2: v.tensor_reduce(
            dst2[:, tt * 16:(tt + 1) * 16],
            nsq[:, :].rearrange("p (h e) -> p h e", e=64), AX.X, ALU.add),
            reads=[nsq.name], writes=[dkey2], tag=tag + "n2")

    def transpose_store(tt, buf, dst, dkey, tag):
        def mm_tr(t, buf=buf):
            for h in range(16):
                i1 = t.transpose(TP[0:64, h * 128:(h + 1) * 128],
                                 buf[:, h * 64:(h + 1) * 64], ident_bf[:, :])
            return i1
        p.op("tensor", mm_tr, reads=[buf.name, ident_bf.name], writes=[TP.name],
             tag=tag + "tr")
        for h in range(16):
            p.op("vector", lambda v, h=h, dst=dst, tt=tt: v.tensor_copy(
                dst[(h % 2) * 64:(h % 2) * 64 + 64,
                    (h // 2) * 2048 + tt * 128:(h // 2) * 2048 + (tt + 1) * 128],
                TP[0:64, h * 128:(h + 1) * 128]),
                reads=[TP.name], writes=[dkey], tag=tag + f"e{h}")

    def rope_and_store(tt, buf, sc1, sc2, sc3, dst, dkey, tag):
        pv = SC[:, 0:1024].rearrange("p (h x) -> p h x", x=64)
        p.op("scalar", lambda s, buf=buf: s.copy(buf[:, :], SC[:, 0:1024]),
             reads=["b_SC0a"], writes=[buf.name], tag=tag + "cp")
        if B_TRUNC == 11:
            return
        bv = buf[:, :].rearrange("p (h x) -> p h x", x=64)
        p.op("vector", lambda v, sc1=sc1, bv=bv: v.tensor_copy(
            sc1[:, :].rearrange("p (h x) -> p h x", x=16), bv[:, :, 0:16]),
            reads=[buf.name], writes=[sc1.name], tag=tag + "g")
        if B_TRUNC == 12:
            return
        s4 = sc1[:, :].rearrange("p (h y z) -> p h y z", y=8, z=2)
        qe = s4[:, :, :, 0:1].squeeze(3)
        qo = s4[:, :, :, 1:2].squeeze(3)
        cosb = cosT[:, :].rearrange("p (h y) -> p h y", y=8)
        sinb = sinT[:, :].rearrange("p (h y) -> p h y", y=8)
        o4 = sc2[:, :].rearrange("p (h y z) -> p h y z", y=8, z=2)
        oe = o4[:, :, :, 0:1].squeeze(3)
        oo = o4[:, :, :, 1:2].squeeze(3)
        t4 = sc3[:, :].rearrange("p (h y z) -> p h y z", y=8, z=2)
        te = t4[:, :, :, 0:1].squeeze(3)
        to = t4[:, :, :, 1:2].squeeze(3)
        p.op("vector", lambda v: v.tensor_tensor(oe, qe, cosb, ALU.mult),
             reads=[sc1.name, cosT.name], writes=[sc2.name], tag=tag + "r1")
        p.op("vector", lambda v: v.tensor_tensor(oo, qe, sinb, ALU.mult),
             reads=[sc1.name, sinT.name], writes=[sc2.name], tag=tag + "r2")
        p.op("vector", lambda v: v.tensor_tensor(te, qo, sinb, ALU.mult),
             reads=[sc1.name, sinT.name], writes=[sc3.name], tag=tag + "r3")
        p.op("vector", lambda v: v.tensor_tensor(to, qo, cosb, ALU.mult),
             reads=[sc1.name, cosT.name], writes=[sc3.name], tag=tag + "r4")
        p.op("vector", lambda v: v.tensor_tensor(oe, oe, te, ALU.subtract),
             reads=[sc2.name, sc3.name], writes=[sc2.name], tag=tag + "r5")
        p.op("vector", lambda v: v.tensor_tensor(oo, oo, to, ALU.add),
             reads=[sc2.name, sc3.name], writes=[sc2.name], tag=tag + "r6")
        b4 = buf[:, :].rearrange("p (h y z) -> p h y z", y=32, z=2)
        p.op("vector", lambda v: v.tensor_copy(b4[:, :, 0:8, 0:1].squeeze(3), oe),
             reads=[sc2.name], writes=[buf.name], tag=tag + "w1")
        p.op("vector", lambda v: v.tensor_copy(b4[:, :, 0:8, 1:2].squeeze(3), oo),
             reads=[sc2.name], writes=[buf.name], tag=tag + "w2")
        if B_TRUNC == 13:
            return
        return

    pending = []
    gi = 0
    for part in ("q", "k", "v"):
        for tt in range(NT):
            gi += 1
            buf = qtok[gi % 3]; sc1 = rsc[gi % 3]; sc2 = rs2[gi % 3]; sc3 = rs3[gi % 3]
            if part == "q":
                def mm_q(t, tt=tt):
                    for och in range(2):
                        for kt in range(8):
                            i1 = t.matmul(SC[:, och * 512:(och + 1) * 512],
                                          xn[:, kt * 2048 + tt * 128: kt * 2048 + (tt + 1) * 128],
                                          wq_bf[:, kt * 1024 + och * 512: kt * 1024 + (och + 1) * 512],
                                          start=(kt == 0), stop=(kt == 7))
                    return i1
                p.op("tensor", mm_q, reads=[XN, "b_wq1"], writes=SCKS, tag=f"b_q{tt}")
                rope_and_store(tt, buf, sc1, sc2, sc3, qT, "b_qTf", f"b_rq{tt}_")
                pending.append((tt, buf, qT, "b_qTf", f"b_tq{tt}_"))
            elif part == "k":
                def mm_k(t, tt=tt):
                    for och in range(2):
                        for kt in range(2):
                            i1 = t.matmul(SC[:, och * 512:(och + 1) * 512],
                                          lat[:, kt * 2048 + tt * 128: kt * 2048 + (tt + 1) * 128],
                                          wsm[:, 2048 + kt * 1024 + och * 512: 2048 + kt * 1024 + (och + 1) * 512],
                                          start=(kt == 0), stop=(kt == 1))
                    return i1
                p.op("tensor", mm_k, reads=["b_latk", "b_wkb"], writes=SCKS, tag=f"b_k{tt}")
                rope_and_store(tt, buf, sc1, sc2, sc3, kT, "b_kTf", f"b_rk{tt}_")
                pending.append((tt, buf, kT, "b_kTf", f"b_tk{tt}_"))
            else:
                def mm_v(t, tt=tt):
                    for och in range(2):
                        for kt in range(2):
                            i1 = t.matmul(SC[:, och * 512:(och + 1) * 512],
                                          lat[:, 4096 + kt * 2048 + tt * 128: 4096 + kt * 2048 + (tt + 1) * 128],
                                          wsm[:, 6144 + kt * 1024 + och * 512: 6144 + kt * 1024 + (och + 1) * 512],
                                          start=(kt == 0), stop=(kt == 1))
                    return i1
                p.op("tensor", mm_v, reads=["b_latv", "b_wvb"], writes=SCKS, tag=f"b_v{tt}")
                p.op("vector", lambda v, tt=tt: v.tensor_copy(
                    v65[:, tt * 1024:(tt + 1) * 1024], SC[:, 0:1024]),
                    reads=["b_SC0a"], writes=["b_v65"], tag=f"b_ev{tt}")
            if len(pending) > 2:
                a_ = pending.pop(0)
                transpose_store(a_[0], a_[1], a_[2], a_[3], a_[4])
                if a_[2] is qT:
                    norms_from(a_[0], a_[1], qn2, qn2.name, a_[4] + "nq")
                elif a_[2] is kT:
                    norms_from(a_[0], a_[1], kn2, kn2.name, a_[4] + "nk")
    while pending:
        a_ = pending.pop(0)
        transpose_store(a_[0], a_[1], a_[2], a_[3], a_[4])
        if a_[2] is qT:
            norms_from(a_[0], a_[1], qn2, qn2.name, a_[4] + "nq")
        elif a_[2] is kT:
            norms_from(a_[0], a_[1], kn2, kn2.name, a_[4] + "nk")

    if B_TRUNC == 2:
        return
    p.op("scalar", lambda s: s.activation(qn2[:, :], qn2[:, :], AF.Ln),
         reads=[qn2.name], writes=[qn2.name], tag="b_qln")
    p.op("scalar", lambda s: s.activation(qn2[:, :], qn2[:, :], AF.Exp, scale=0.5),
         reads=[qn2.name], writes=[qn2.name], tag="b_qex")
    p.op("scalar", lambda s: s.activation(kn2[:, :], kn2[:, :], AF.Ln),
         reads=[kn2.name], writes=[kn2.name], tag="b_kln")
    p.op("scalar", lambda s: s.activation(kn2[:, :], kn2[:, :], AF.Exp, scale=0.5),
         reads=[kn2.name], writes=[kn2.name], tag="b_kex")
    kmaxa = sb("b_kmaxa", [128, 16], F32)
    kmaxr = sb("b_kmaxr", [16, 128], F32)
    maxkb = sb("b_maxkb", [128, 16], F32)
    p.op("vector", lambda v: v.tensor_reduce(
        kmaxa[:, :], kn2[:, :].rearrange("p (t h) -> p h t", h=16), AX.X, ALU.max),
        reads=[kn2.name], writes=[kmaxa.name], tag="b_kmx1")
    p.op("tensor", lambda t: t.transpose(SC[0:16, 0:128], kmaxa[:, :], ident_f2[:, :]),
         reads=[kmaxa.name, "b_identf2"], writes=["b_SC0a"], tag="b_kmt")
    p.op("vector", lambda v: v.tensor_copy(kmaxr[:, :], SC[0:16, 0:128]),
         reads=["b_SC0a"], writes=[kmaxr.name], tag="b_kme")
    p.op("vector", lambda v: v.tensor_reduce(
        kmaxr[0:16, 0:1], kmaxr[:, :], AX.X, ALU.max),
        reads=[kmaxr.name], writes=[kmaxr.name + "m"], tag="b_kmx2")
    p.op("tensor", lambda t: t.transpose(SC[0:1, 0:16], kmaxr[0:16, 0:1],
                                         ident_f2[0:16, 0:16]),
         reads=[kmaxr.name + "m", "b_identf2"], writes=["b_SC0a"], tag="b_kmt2")
    p.op("vector", lambda v: v.tensor_scalar(
        kmaxr[0:1, 16:32], SC[0:1, 0:16], -0.125, None, ALU.mult),
        reads=["b_SC0a"], writes=[kmaxr.name + "r"], tag="b_kms")
    p.op("tensor", lambda t: t.matmul(SC[:, 0:16], ones_fb[0:1, :],
                                      kmaxr[0:1, 16:32], start=True, stop=True),
         reads=["b_onesf", kmaxr.name + "r"], writes=["b_SC0a"], tag="b_kmb")
    p.op("vector", lambda v: v.tensor_copy(maxkb[:, :], SC[:, 0:16]),
         reads=["b_SC0a"], writes=[maxkb.name], tag="b_kme2")

    p.barrier()
    attn_bf = [lat[:, 0:2048], lat[:, 2048:4096]]
    attnT_sb = [lat[:, 4096:6144], lat[:, 6144:8192]]
    _ABK = ["b_abf0", "b_abf1"]
    _ATK = ["b_at0", "b_at1"]
    nmx = sb("b_nmx", [128, 16], F32)
    attn_out = A4a

    def stage1(h, qt):
        par = (h * 16 + qt) % 2
        ab = attn_bf[par]
        abk = _ABK[par]
        hb = (h % 2) * 64
        hc = (h // 2) * 2048
        nc0 = par * 8
        ik = f"{h}_{qt}"
        def mm_sc(t, hb=hb, hc=hc, qt=qt):
            for nch in range(4):
                i1 = t.matmul(SC[:, nch * 512:(nch + 1) * 512],
                              qT[hb:hb + 64, hc + qt * 128:hc + (qt + 1) * 128],
                              kT[hb:hb + 64, hc + nch * 512:hc + (nch + 1) * 512],
                              start=True, stop=True)
            return i1
        p.op("tensor", mm_sc, reads=["b_qTf", "b_kTf"], writes=SCKS,
             tag=f"b_s{ik}")
        p.op("vector", lambda v, nc0=nc0, h=h, qt=qt: v.tensor_tensor(
            nmx[:, nc0 + 3:nc0 + 4], qn2[:, qt * 16 + h:qt * 16 + h + 1],
            maxkb[:, h:h + 1], ALU.mult),
            reads=[qn2.name, maxkb.name], writes=[f"b_nm{par}"], tag=f"b_ns{ik}")
        p.op("scalar", lambda s, ab=ab, nc0=nc0: s.activation(
            ab[:, :], SC[:, 0:2048], AF.Exp,
            bias=nmx[:, nc0 + 3:nc0 + 4], scale=0.125,
            accum_out=nmx[:, nc0 + 6:nc0 + 7]),
            reads=SCKS + [f"b_nm{par}"], writes=[abk, f"b_nm{par}z"],
            tag=f"b_e{ik}")
        p.op("vector", lambda v, nc0=nc0: v.reciprocal(
            nmx[:, nc0 + 7:nc0 + 8], nmx[:, nc0 + 6:nc0 + 7]),
            reads=[f"b_nm{par}z"], writes=[f"b_nm{par}z"], tag=f"b_rz{ik}")
        p.op("vector", lambda v, ab=ab, nc0=nc0: v.tensor_scalar(
            ab[:, :], ab[:, :], nmx[:, nc0 + 7:nc0 + 8], None, ALU.mult),
            reads=[abk, f"b_nm{par}z"], writes=[abk], tag=f"b_n{ik}")

    def stage2(h, qt):
        par = (h * 16 + qt) % 2
        ab = attn_bf[par]
        at = attnT_sb[par]
        abk = _ABK[par]
        atk = _ATK[par]
        AVP = P1 if par == 0 else P2
        hb = (h % 2) * 64
        hc = (h // 2) * 2048
        ik = f"{h}_{qt}"
        for grp in range(2):
            TPg = TP[:, grp * 1024:(grp + 1) * 1024]
            tk = f"b_TP{grp}"
            def mm_at(t, ab=ab, grp=grp, TPg=TPg):
                for j in range(8):
                    kt_ = grp * 8 + j
                    i1 = t.transpose(TPg[:, j * 128:(j + 1) * 128],
                                     ab[:, kt_ * 128:(kt_ + 1) * 128],
                                     ident_bf[:, :])
                return i1
            p.op("tensor", mm_at, reads=[abk, ident_bf.name], writes=[tk],
                 tag=f"b_t{grp}_{ik}")
            p.op("scalar", lambda s, at=at, grp=grp, TPg=TPg: s.copy(
                at[:, grp * 1024:(grp + 1) * 1024], TPg),
                reads=[tk], writes=[atk], tag=f"b_v{grp}_{ik}")
        def mm_av(t, h=h, at=at, AVP=AVP):
            for kt_ in range(16):
                i1 = t.matmul(AVP[0:64, 0:128],
                              v65[:, kt_ * 1024 + h * 64: kt_ * 1024 + (h + 1) * 64],
                              at[:, kt_ * 128:(kt_ + 1) * 128],
                              start=(kt_ == 0), stop=(kt_ == 15))
            return i1
        p.op("tensor", mm_av, reads=["b_v65", atk], writes=[AVP.name],
             tag=f"b_a{ik}")
        p.op("vector", lambda v, hb=hb, hc=hc, qt=qt, AVP=AVP: v.tensor_copy(
            attn_out[hb:hb + 64, hc + qt * 128:hc + (qt + 1) * 128],
            AVP[0:64, 0:128]),
            reads=[AVP.name], writes=[AO], tag=f"b_o{ik}")

    prev = None
    for h in range(16):
        for qt in range(NT):
            stage1(h, qt)
            if prev is not None:
                stage2(*prev)
            prev = (h, qt)
    stage2(*prev)

    p.barrier()
    p.op("gpsimd", lambda g_: g_.dma_start(
        wq_bf[:, :].rearrange("p (k n) -> p k n", n=1024),
        d["woT"][:, :].rearrange("(k p) n -> p k n", p=128)),
        writes=["b_wo", "b_wq1"], dma=True, tag="b_ldwo")
    xres = lat[:, 0:4096].bitcast(F32)
    x1a = lat[:, 4096:8192].bitcast(F32)
    XRK = ["b_xres", "b_latk", "b_latv"]
    X1K = ["b_x1a", "b_latk", "b_latv"]
    for dt_ in range(ND):
        p.op("sync", lambda s, dt_=dt_: s.dma_start(
            xres[:, :], d["xg"][dt_ * 128:(dt_ + 1) * 128, 0:2048]),
            writes=XRK, dma=True, tag=f"b_ldx{dt_}")
        for nch in range(4):
            pp = P1 if nch % 2 == 0 else P2
            def mm_wo(t, dt_=dt_, nch=nch, pp=pp):
                for kt in range(8):
                    i1 = t.matmul(pp[:, 0:512],
                                  wq_bf[:, kt * 1024 + dt_ * 128: kt * 1024 + dt_ * 128 + 128],
                                  attn_out[:, kt * 2048 + nch * 512: kt * 2048 + (nch + 1) * 512],
                                  start=(kt == 0), stop=(kt == 7))
                return i1
            p.op("tensor", mm_wo, reads=["b_wo", AO], writes=[pp.name],
                 tag=f"b_wo{dt_}_{nch}")
            p.op("vector", lambda v, nch=nch, pp=pp: v.tensor_tensor(
                x1a[:, nch * 512:(nch + 1) * 512], pp[:, 0:512],
                xres[:, nch * 512:(nch + 1) * 512], ALU.add),
                reads=[pp.name, "b_xres"], writes=X1K, tag=f"b_ra{dt_}_{nch}")
        p.op("sync", lambda s, dt_=dt_: s.dma_start(
            d["x1a_d"][dt_ * 128:(dt_ + 1) * 128, :], x1a[:, :]),
            reads=["b_x1a"], writes=["x1a_d"], dma=True, tag=f"b_stx{dt_}")



def phase_moe(p, d, es):
    nc = p.nc
    sb = lambda name, shape, dt: es.enter_context(nc.sbuf_tensor(name, shape, dt))
    ps = lambda name, shape, dt: es.enter_context(nc.psum_tensor(name, shape, dt))

    ident_f = sb("c_ident_f", [128, 128], F32)
    p.op("sync", lambda s: s.dma_start(ident_f[:, :], d["ident_f"][:, :]),
         writes=[ident_f.name], dma=True, tag="c_ld_ident")
    xnsb = sb("c_xn", [128, 16384], BF16)
    for dt_ in range(ND):
        p.op("sync", lambda s, dt_=dt_: s.dma_start(
            xnsb[:, dt_ * 2048:(dt_ + 1) * 2048],
            d["xn_d"][dt_ * 128:(dt_ + 1) * 128, :]),
            writes=["c_xn"], dma=True, tag=f"c_ldxn{dt_}")
    ysb = sb("c_y", [128, 16384], F32)
    wA = sb("c_wA", [128, 16384], BF16)
    wB = sb("c_wB", [128, 16384], BF16)
    hsb = sb("c_h", [128, 8192], BF16)
    Wgsb = sb("c_Wg", [128, 1024], F32)
    WT9sb = sb("c_WT9", [33, 2048], F32)
    b2sb = sb("c_b2", [33, 1024], F32)
    b1sb = sb("c_b1", [128, 8 * 32], F32)
    sb1sb = sb("c_sb1", [128, 32], F32)
    p.op("sync", lambda s: s.dma_start(Wgsb[:, :], d["Wg_d"][:, :]),
         writes=[Wgsb.name], dma=True, tag="c_ldwg")
    p.op("sync", lambda s: s.dma_start(WT9sb[:, :], d["WT9_d"][:, :]),
         writes=[WT9sb.name], dma=True, tag="c_ldwt9")
    p.op("sync", lambda s: s.dma_start(b2sb[:, :], d["b2ext"][:, :]),
         writes=[b2sb.name], dma=True, tag="c_ldb2")
    p.op("sync", lambda s: s.dma_start(
        b1sb[:, :].rearrange("p (e n) -> p e n", n=32),
        d["moe_b1"][:, :, :].rearrange("e p n -> p e n")),
        writes=[b1sb.name], dma=True, tag="c_ldb1")
    p.op("sync", lambda s: s.dma_start(sb1sb[:, :], d["sb1"][:, :]),
         writes=[sb1sb.name], dma=True, tag="c_ldsb1")

    P1 = ps("c_P1", [128, 512], F32)
    P2 = ps("c_P2", [128, 512], F32)
    PO = ps("c_PO", [128, 1024], F32)
    PO2 = ps("c_PO2", [128, 1024], F32)
    PT = ps("c_PT", [128, 512], F32)
    p.mark_psum(P1.name, P2.name, PO.name, PO2.name, PT.name)

    for tt in range(NT):
        def mm_seed(t, tt=tt):
            for nch in range(2):
                i1 = t.matmul(PO[:, nch * 512:(nch + 1) * 512],
                              WT9sb[:, tt * 128:(tt + 1) * 128],
                              b2sb[:, nch * 512:(nch + 1) * 512],
                              start=True, stop=True)
            return i1
        p.op("tensor", mm_seed, reads=[WT9sb.name, b2sb.name], writes=[PO.name],
             tag=f"c_seed{tt}")
        p.op("vector", lambda v, tt=tt: v.tensor_copy(
            ysb[:, tt * 1024:(tt + 1) * 1024], PO[:, 0:1024]),
            reads=[PO.name], writes=[ysb.name], tag=f"c_evseed{tt}")

    NTB = 4
    for e in range(9):
        for fh in range(2):
            if e < 8:
                w1src = d["moe_w1"]
                p.op("gpsimd", lambda g_, e=e, fh=fh: g_.dma_start(
                    wA[:, :].rearrange("p (k n) -> p k n", n=2048),
                    d["moe_w1"][e, :, fh * 2048:(fh + 1) * 2048]
                    .rearrange("(k p) n -> p k n", p=128)),
                    writes=[wA.name], dma=True, tag=f"c_ldw1_{e}_{fh}")
                p.op("gpsimd", lambda g_, e=e, fh=fh: g_.dma_start(
                    wB[:, :].rearrange("p (k n) -> p k n", n=1024),
                    d["moe_w2"][e, fh * 2048:(fh + 1) * 2048, :]
                    .rearrange("(k p) n -> p k n", p=128)),
                    writes=[wB.name], dma=True, tag=f"c_ldw2_{e}_{fh}")
            else:
                p.op("gpsimd", lambda g_, fh=fh: g_.dma_start(
                    wA[:, :].rearrange("p (k n) -> p k n", n=2048),
                    d["sw1"][:, fh * 2048:(fh + 1) * 2048]
                    .rearrange("(k p) n -> p k n", p=128)),
                    writes=[wA.name], dma=True, tag=f"c_ldsw1_{fh}")
                p.op("gpsimd", lambda g_, fh=fh: g_.dma_start(
                    wB[:, :].rearrange("p (k n) -> p k n", n=1024),
                    d["sw2"][fh * 2048:(fh + 1) * 2048, :]
                    .rearrange("(k p) n -> p k n", p=128)),
                    writes=[wB.name], dma=True, tag=f"c_ldsw2_{fh}")
            for tb in range(NTB):
                for mt in range(16):
                    pp = P1 if mt % 2 == 0 else P2
                    def mm_h(t, mt=mt, tb=tb, pp=pp):
                        for kt in range(8):
                            i1 = t.matmul(
                                pp[:, 0:512],
                                wA[:, kt * 2048 + mt * 128: kt * 2048 + (mt + 1) * 128],
                                xnsb[:, kt * 2048 + tb * 512: kt * 2048 + (tb + 1) * 512],
                                start=(kt == 0), stop=(kt == 7))
                        return i1
                    p.op("tensor", mm_h, reads=[wA.name, "c_xn"], writes=[pp.name],
                         tag=f"c_h{e}_{fh}_{tb}_{mt}")
                    bcol = (sb1sb[:, fh * 16 + mt: fh * 16 + mt + 1] if e == 8
                            else b1sb[:, e * 32 + fh * 16 + mt: e * 32 + fh * 16 + mt + 1])
                    bkey = sb1sb.name if e == 8 else b1sb.name
                    p.op("scalar", lambda s, mt=mt, pp=pp, bcol=bcol: s.activation(
                        hsb[:, mt * 512:(mt + 1) * 512], pp[:, 0:512], AF.Silu,
                        bias=bcol),
                        reads=[pp.name, bkey], writes=[hsb.name],
                        tag=f"c_si{e}_{fh}_{tb}_{mt}")
                for tloc in range(4):
                    tt = tb * 4 + tloc
                    po_ = PO if tloc % 2 == 0 else PO2
                    def mm_o(t, tloc=tloc, po_=po_):
                        for nch in range(2):
                            for kt in range(16):
                                i1 = t.matmul(
                                    po_[:, nch * 512:(nch + 1) * 512],
                                    hsb[:, kt * 512 + tloc * 128: kt * 512 + (tloc + 1) * 128],
                                    wB[:, kt * 1024 + nch * 512: kt * 1024 + (nch + 1) * 512],
                                    start=(kt == 0), stop=(kt == 15))
                        return i1
                    p.op("tensor", mm_o, reads=[hsb.name, wB.name], writes=[po_.name],
                         tag=f"c_o{e}_{fh}_{tb}_{tloc}")
                    for nch in range(2):
                        if e < 8:
                            p.op("vector", lambda v, e=e, tt=tt, nch=nch, po_=po_:
                                 v.scalar_tensor_tensor(
                                     ysb[:, tt * 1024 + nch * 512: tt * 1024 + (nch + 1) * 512],
                                     po_[:, nch * 512:(nch + 1) * 512],
                                     Wgsb[:, e * 128 + tt: e * 128 + tt + 1],
                                     ysb[:, tt * 1024 + nch * 512: tt * 1024 + (nch + 1) * 512],
                                     ALU.mult, ALU.add),
                                 reads=[po_.name, Wgsb.name, ysb.name], writes=[ysb.name],
                                 tag=f"c_y{e}_{fh}_{tb}_{tloc}_{nch}")
                        else:
                            p.op("vector", lambda v, tt=tt, nch=nch, po_=po_:
                                 v.scalar_tensor_tensor(
                                     ysb[:, tt * 1024 + nch * 512: tt * 1024 + (nch + 1) * 512],
                                     po_[:, nch * 512:(nch + 1) * 512],
                                     0.1,
                                     ysb[:, tt * 1024 + nch * 512: tt * 1024 + (nch + 1) * 512],
                                     ALU.mult, ALU.add),
                                 reads=[po_.name, ysb.name], writes=[ysb.name],
                                 tag=f"c_ys{fh}_{tb}_{tloc}_{nch}")

    x1asb = wA[:, 0:4096].bitcast(F32)
    x1sb = wA[:, 4096:8192].bitcast(F32)
    XAK = ["c_x1a", wA.name]
    X1K = ["c_x1", wA.name]
    for dt_ in range(ND):
        p.op("sync", lambda s, dt_=dt_: s.dma_start(
            x1asb[:, :], d["x1a_d"][dt_ * 128:(dt_ + 1) * 128, :]),
            writes=XAK, dma=True, tag=f"c_ldx1a{dt_}")
        for g_ in range(4):
            def mm_t(t, dt_=dt_, g_=g_):
                for j in range(4):
                    tt = g_ * 4 + j
                    i1 = t.transpose(PT[:, j * 128:(j + 1) * 128],
                                     ysb[:, tt * 1024 + dt_ * 128: tt * 1024 + (dt_ + 1) * 128],
                                     ident_f[:, :])
                return i1
            p.op("tensor", mm_t, reads=[ysb.name, ident_f.name], writes=[PT.name],
                 tag=f"c_t{dt_}_{g_}")
            p.op("vector", lambda v, g_=g_: v.tensor_tensor(
                x1sb[:, g_ * 512:(g_ + 1) * 512], PT[:, 0:512],
                x1asb[:, g_ * 512:(g_ + 1) * 512], ALU.add),
                reads=[PT.name, "c_x1a"], writes=X1K, tag=f"c_a{dt_}_{g_}")
        p.op("sync", lambda s, dt_=dt_: s.dma_start(
            d["x1_d"][dt_ * 128:(dt_ + 1) * 128, :], x1sb[:, :]),
            reads=["c_x1"], writes=["x1_d"], dma=True, tag=f"c_stx1{dt_}")


def build_phaseC():
    nc = bass.Bass("TRN2", target_bir_lowering=False)
    es = contextlib.ExitStack()
    d = {}
    d["ident_f"] = nc.dram_tensor("ident_f", [128, 128], F32, kind="ExternalInput").ap()
    d["moe_w1"] = nc.dram_tensor("moe_w1", [8, D, 4096], F32, kind="ExternalInput").ap()
    d["moe_w2"] = nc.dram_tensor("moe_w2", [8, 4096, D], F32, kind="ExternalInput").ap()
    d["moe_b1"] = nc.dram_tensor("moe_b1", [8, 128, 32], F32, kind="ExternalInput").ap()
    d["b2ext"] = nc.dram_tensor("b2ext", [33, D], F32, kind="ExternalInput").ap()
    d["sw1"] = nc.dram_tensor("sw1", [D, 4096], F32, kind="ExternalInput").ap()
    d["sw2"] = nc.dram_tensor("sw2", [4096, D], F32, kind="ExternalInput").ap()
    d["sb1"] = nc.dram_tensor("sb1", [128, 32], F32, kind="ExternalInput").ap()
    d["Wg_d"] = nc.dram_tensor("Wg_d", [128, 1024], F32, kind="ExternalInput").ap()
    d["WT9_d"] = nc.dram_tensor("WT9_d", [33, 2048], F32, kind="ExternalInput").ap()
    d["xn_d"] = nc.dram_tensor("xn_d", [D, T], BF16, kind="ExternalInput").ap()
    d["x1a_d"] = nc.dram_tensor("x1a_d", [D, T], F32, kind="ExternalInput").ap()
    d["x1_d"] = nc.dram_tensor("x1_d", [D, T], F32, kind="ExternalOutput").ap()
    p = Prog(nc)
    with es:
        phase_moe(p, d, es)
        p.emit()
    nc.reset()
    return nc



def phase_l1_attn(p, d, es):
    nc = p.nc
    sb = lambda name, shape, dt: es.enter_context(nc.sbuf_tensor(name, shape, dt))
    ps = lambda name, shape, dt: es.enter_context(nc.psum_tensor(name, shape, dt))

    ones_f = sb("d_ones_f", [128, 128], F32)
    ones_bf = sb("d_ones_bf", [128, 128], BF16)
    p.op("sync", lambda s: s.dma_start(ones_f[:, :], d["ones_f"][:, :]),
         writes=[ones_f.name], dma=True, tag="d_ldones")
    p.op("gpsimd", lambda g_: g_.dma_start(ones_bf[:, :], d["ones_f"][:, :]),
         writes=[ones_bf.name], dma=True, tag="d_ldonesb")

    xn1 = sb("d_xn1", [128, 16384], BF16)
    qy = sb("d_qy", [128, 16384], BF16)
    ktok = sb("d_k", [128, 16384], BF16)
    v65 = sb("d_v65", [128, 16 * 1040], BF16)
    wbuf = sb("d_w", [128, 8192], BF16)
    x1dt = sb("d_x1dt", [128, 2048], F32)
    xsq = sb("d_xsq", [128, 2048], F32)
    Mb = sb("d_Mb", [128, 2048], F32)
    Ab = sb("d_Ab", [128, 2048], F32)
    rowsA = sb("d_rowsA", [1, 2048], F32)
    rowsB = sb("d_rowsB", [1, 2048], F32)
    bq1 = sb("d_bq1", [128, 8], F32)
    gam = sb("d_gam", [128, 8], F32)
    bet = sb("d_bet", [128, 8], F32)
    bkrow = sb("d_bkrow", [1, 1024], BF16)
    bvrow = sb("d_bvrow", [1, 1024], BF16)
    kv_sb = sb("d_kv", [64, 65], BF16)
    qcp = sb("d_qcp", [64, 512], BF16)
    rb_sb = sb("d_rb", [64, 512], F32)
    tmp_sb = sb("d_tmp", [128, 512], F32)
    dn_row = sb("d_dn", [1, 512], F32)
    epsc = sb("d_epsc", [128, 1], F32)
    p.op("vector", lambda v: v.memset(epsc[:, :], 1e-5), writes=[epsc.name], tag="d_eps")

    PA = ps("d_PA", [1, 512], F32)
    P1 = ps("d_P1", [128, 512], F32)
    P2 = ps("d_P2", [128, 512], F32)
    PT = ps("d_PT", [128, 512], F32)
    PO = ps("d_PO", [128, 1024], F32)
    p.mark_psum(PA.name, P1.name, P2.name, PT.name, PO.name)

    for nm, t_ in (("bq1", bq1), ("gamma1", gam), ("beta1", bet)):
        p.op("sync", lambda s, nm=nm, t_=t_: s.dma_start(t_[:, :], d[nm][:, :]),
             writes=[t_.name], dma=True, tag="d_ld" + nm)
    p.op("gpsimd", lambda g_: g_.dma_start(bkrow[:, :], d["bk1row"][:, :]),
         writes=[bkrow.name], dma=True, tag="d_ldbk")
    p.op("gpsimd", lambda g_: g_.dma_start(bvrow[:, :], d["bv1row"][:, :]),
         writes=[bvrow.name], dma=True, tag="d_ldbv")

    def ln_stats(load_src, srckey, is_bf, tagp):
        p.op("vector", lambda v: v.memset(rowsA[:, :], 0.0),
             writes=["d_rA"], tag=tagp + "za")
        p.op("vector", lambda v: v.memset(rowsB[:, :], 0.0),
             writes=["d_rB"], tag=tagp + "zb")
        for dt_ in range(ND):
            src = load_src(dt_)
            onesc = ones_bf if is_bf else ones_f
            for nch in range(4):
                p.op("tensor", lambda t, src=src, onesc=onesc, nch=nch: t.matmul(
                    PA[0:1, 0:512], onesc[:, 0:1],
                    src[:, nch * 512:(nch + 1) * 512], start=True, stop=True),
                    reads=[srckey, onesc.name], writes=[PA.name],
                    tag=tagp + f"s1_{dt_}_{nch}")
                p.op("vector", lambda v, nch=nch: v.tensor_tensor(
                    rowsA[:, nch * 512:(nch + 1) * 512],
                    rowsA[:, nch * 512:(nch + 1) * 512], PA[0:1, 0:512], ALU.add),
                    reads=[PA.name, "d_rA"], writes=["d_rA"], tag=tagp + f"a1_{dt_}_{nch}")
            p.op("scalar", lambda s, src=src: s.activation(xsq[:, :], src[:, :], AF.Square),
                 reads=[srckey], writes=[xsq.name], tag=tagp + f"sq_{dt_}")
            for nch in range(4):
                p.op("tensor", lambda t, nch=nch: t.matmul(
                    PA[0:1, 0:512], ones_f[:, 0:1],
                    xsq[:, nch * 512:(nch + 1) * 512], start=True, stop=True),
                    reads=[xsq.name, ones_f.name], writes=[PA.name],
                    tag=tagp + f"s2_{dt_}_{nch}")
                p.op("vector", lambda v, nch=nch: v.tensor_tensor(
                    rowsB[:, nch * 512:(nch + 1) * 512],
                    rowsB[:, nch * 512:(nch + 1) * 512], PA[0:1, 0:512], ALU.add),
                    reads=[PA.name, "d_rB"], writes=["d_rB"], tag=tagp + f"a2_{dt_}_{nch}")
        p.op("vector", lambda v: v.tensor_scalar(
            rowsA[:, :], rowsA[:, :], -1.0 / 1024.0, None, ALU.mult),
            reads=["d_rA"], writes=["d_rA"], tag=tagp + "mu")
        p.op("vector", lambda v: v.tensor_tensor(
            xsq[0:1, 0:2048], rowsA[:, :], rowsA[:, :], ALU.mult),
            reads=["d_rA"], writes=[xsq.name], tag=tagp + "mu2")
        p.op("vector", lambda v: v.scalar_tensor_tensor(
            rowsB[:, :], rowsB[:, :], 1.0 / 1024.0,
            xsq[0:1, 0:2048], ALU.mult, ALU.subtract),
            reads=["d_rB", xsq.name], writes=["d_rB"], tag=tagp + "var")
        p.op("scalar", lambda s: s.activation(
            rowsB[:, :], rowsB[:, :], AF.Ln, bias=epsc[0:1, :]),
            reads=["d_rB", epsc.name], writes=["d_rB"], tag=tagp + "ln")
        p.op("scalar", lambda s: s.activation(
            rowsB[:, :], rowsB[:, :], AF.Exp, scale=-0.5),
            reads=["d_rB"], writes=["d_rB"], tag=tagp + "rstd")
        for half in range(4):
            p.op("tensor", lambda t, half=half: t.matmul(
                P1[:, 0:512], ones_f[0:1, :],
                rowsA[:, half * 512:(half + 1) * 512],
                start=True, stop=True),
                reads=[ones_f.name, "d_rA"], writes=[P1.name], tag=tagp + f"bm{half}")
            p.op("vector", lambda v, half=half: v.tensor_copy(
                Mb[:, half * 512:(half + 1) * 512], P1[:, 0:512]),
                reads=[P1.name], writes=[Mb.name], tag=tagp + f"em{half}")
            p.op("tensor", lambda t, half=half: t.matmul(
                P2[:, 0:512], ones_f[0:1, :],
                rowsB[:, half * 512:(half + 1) * 512],
                start=True, stop=True),
                reads=[ones_f.name, "d_rB"], writes=[P2.name], tag=tagp + f"ba{half}")
            p.op("vector", lambda v, half=half: v.tensor_copy(
                Ab[:, half * 512:(half + 1) * 512], P2[:, 0:512]),
                reads=[P2.name], writes=[Ab.name], tag=tagp + f"ea{half}")

    def load_x1(dt_):
        p.op("sync", lambda s, dt_=dt_: s.dma_start(
            x1dt[:, :], d["x1_d"][dt_ * 128:(dt_ + 1) * 128, :]),
            writes=[x1dt.name], dma=True, tag=f"d_lx1s{dt_}")
        return x1dt[:, :]
    ln_stats(load_x1, x1dt.name, False, "d_l1_")
    p.op("sync", lambda s: s.dma_start(d["murow1_d"][:, :], rowsA[:, :]),
         reads=["d_rA"], writes=["murow1_d"], dma=True, tag="d_stmu")
    p.op("sync", lambda s: s.dma_start(d["arow1_d"][:, :], rowsB[:, :]),
         reads=["d_rB"], writes=["arow1_d"], dma=True, tag="d_star")
    for dt_ in range(ND):
        p.op("sync", lambda s, dt_=dt_: s.dma_start(
            x1dt[:, :], d["x1_d"][dt_ * 128:(dt_ + 1) * 128, :]),
            writes=[x1dt.name], dma=True, tag=f"d_lx1b{dt_}")
        p.op("vector", lambda v: v.tensor_tensor(
            xsq[:, :], x1dt[:, :], Mb[:, :], ALU.add),
            reads=[x1dt.name, Mb.name], writes=[xsq.name], tag=f"d_xn0_{dt_}")
        p.op("vector", lambda v, dt_=dt_: v.tensor_tensor(
            xn1[:, dt_ * 2048:(dt_ + 1) * 2048], xsq[:, :], Ab[:, :], ALU.mult),
            reads=[xsq.name, Ab.name], writes=[xn1.name], tag=f"d_xn1_{dt_}")
        p.op("sync", lambda s, dt_=dt_: s.dma_start(
            d["xn1_d"][dt_ * 128:(dt_ + 1) * 128, :],
            xn1[:, dt_ * 2048:(dt_ + 1) * 2048]),
            reads=[xn1.name], writes=["xn1_d"], dma=True, tag=f"d_stxn{dt_}")

    p.op("gpsimd", lambda g_: g_.dma_start(
        wbuf[:, :].rearrange("p (k n) -> p k n", n=1024),
        d["wq1T"][:, :].rearrange("(k p) n -> p k n", p=128)),
        writes=[wbuf.name], dma=True, tag="d_ldwq")
    for m_ in range(ND):
        for nch in range(4):
            pp = P1 if nch % 2 == 0 else P2
            def mm_q(t, m_=m_, nch=nch, pp=pp):
                for kt in range(8):
                    i1 = t.matmul(pp[:, 0:512],
                                  wbuf[:, kt * 1024 + m_ * 128: kt * 1024 + (m_ + 1) * 128],
                                  xn1[:, kt * 2048 + nch * 512: kt * 2048 + (nch + 1) * 512],
                                  start=(kt == 0), stop=(kt == 7))
                return i1
            p.op("tensor", mm_q, reads=[wbuf.name, xn1.name], writes=[pp.name],
                 tag=f"d_q{m_}_{nch}")
            p.op("scalar", lambda s, m_=m_, nch=nch, pp=pp: s.activation(
                qy[:, m_ * 2048 + nch * 512: m_ * 2048 + (nch + 1) * 512],
                pp[:, 0:512], AF.Relu, bias=bq1[:, m_:m_ + 1]),
                reads=[pp.name, bq1.name], writes=["d_qf"], tag=f"d_eq{m_}_{nch}")

    for which, wsrc, brow, bkey in (("k", "wk1T", bkrow, bkrow.name),
                                    ("v", "wv1T", bvrow, bvrow.name)):
        p.op("gpsimd", lambda g_, wsrc=wsrc: g_.dma_start(
            wbuf[:, :].rearrange("p (k n) -> p k n", n=1024),
            d[wsrc][:, :].rearrange("(k p) n -> p k n", p=128)),
            writes=[wbuf.name], dma=True, tag=f"d_ldw{which}")
        for tt in range(NT):
            def mm_kv(t, tt=tt, brow=brow):
                for och in range(2):
                    for kt in range(8):
                        t.matmul(PO[:, och * 512:(och + 1) * 512],
                                 xn1[:, kt * 2048 + tt * 128: kt * 2048 + (tt + 1) * 128],
                                 wbuf[:, kt * 1024 + och * 512: kt * 1024 + (och + 1) * 512],
                                 start=(kt == 0), stop=False)
                    i1 = t.matmul(PO[:, och * 512:(och + 1) * 512],
                                  ones_bf[0:1, :],
                                  brow[:, och * 512:(och + 1) * 512],
                                  start=False, stop=True)
                return i1
            p.op("tensor", mm_kv, reads=[xn1.name, wbuf.name, bkey, ones_bf.name],
                 writes=[PO.name], tag=f"d_{which}{tt}")
            if which == "k":
                p.op("scalar", lambda s, tt=tt: s.activation(
                    ktok[:, tt * 1024:(tt + 1) * 1024], PO[:, 0:1024], AF.Relu),
                    reads=[PO.name], writes=[ktok.name], tag=f"d_ek{tt}")
            else:
                v3v = v65[:, tt * 1040:(tt + 1) * 1040].rearrange("p (h c) -> p h c", c=65)
                pvv = PO[:, 0:1024].rearrange("p (h x) -> p h x", x=64)
                p.op("scalar", lambda s, v3v=v3v, pvv=pvv: s.copy(v3v[:, :, 0:64], pvv),
                    reads=[PO.name], writes=["d_v65"], tag=f"d_ev{tt}")
    p.op("vector", lambda v: v.memset(
        v65[:, :].rearrange("p (h c) -> p h c", c=65)[:, :, 64:65], 1.0),
        writes=["d_v65o"], tag="d_vones")

    for h in range(16):
        hb_ = (h % 2) * 64
        hc = (h // 2) * 2048
        def mm_kv2(t, h=h):
            for kt in range(16):
                i1 = t.matmul(P1[0:64, 0:65],
                              ktok[:, kt * 1024 + h * 64: kt * 1024 + (h + 1) * 64],
                              v65[:, kt * 1040 + h * 65: kt * 1040 + (h + 1) * 65],
                              start=(kt == 0), stop=(kt == 15))
            return i1
        p.op("tensor", mm_kv2, reads=[ktok.name, "d_v65", "d_v65o"], writes=[P1.name],
             tag=f"d_kv{h}")
        p.op("vector", lambda v: v.tensor_copy(kv_sb[:, :], P1[0:64, 0:65]),
             reads=[P1.name], writes=[kv_sb.name], tag=f"d_ekv{h}")
        for nch in range(4):
            p.op("vector", lambda v, hb_=hb_, hc=hc, nch=nch: v.tensor_copy(
                qcp[:, :], qy[hb_:hb_ + 64, hc + nch * 512: hc + (nch + 1) * 512]),
                reads=["d_qf"], writes=[qcp.name], tag=f"d_qc{h}_{nch}")
            p.op("tensor", lambda t: t.matmul(
                P2[0:65, 0:512], kv_sb[:, :], qcp[:, :],
                start=True, stop=True),
                reads=[kv_sb.name, qcp.name], writes=[P2.name], tag=f"d_cx{h}_{nch}")
            p.op("vector", lambda v: v.tensor_scalar(
                dn_row[:, :], P2[64:65, 0:512], 1e-6, None, ALU.max),
                reads=[P2.name], writes=[dn_row.name], tag=f"d_dn{h}_{nch}")
            p.op("vector", lambda v: v.reciprocal(dn_row[:, :], dn_row[:, :]),
                 reads=[dn_row.name], writes=[dn_row.name], tag=f"d_rc{h}_{nch}")
            p.op("tensor", lambda t: t.matmul(PT[0:64, 0:512], ones_f[0:1, 0:64],
                                              dn_row[:, :], start=True, stop=True),
                 reads=[ones_f.name, dn_row.name], writes=[PT.name], tag=f"d_rb{h}_{nch}")
            p.op("vector", lambda v: v.tensor_copy(rb_sb[:, :], PT[0:64, 0:512]),
                 reads=[PT.name], writes=[rb_sb.name], tag=f"d_erb{h}_{nch}")
            p.op("vector", lambda v, hb_=hb_: v.tensor_tensor(
                tmp_sb[hb_:hb_ + 64, :], P2[0:64, 0:512], rb_sb[:, :], ALU.mult),
                reads=[P2.name, rb_sb.name], writes=[tmp_sb.name], tag=f"d_o{h}_{nch}")
            p.op("vector", lambda v, hb_=hb_, hc=hc, nch=nch: v.tensor_tensor(
                qy[hb_:hb_ + 64, hc + nch * 512: hc + (nch + 1) * 512],
                tmp_sb[hb_:hb_ + 64, :],
                xn1[hb_:hb_ + 64, hc + nch * 512: hc + (nch + 1) * 512], ALU.add),
                reads=[tmp_sb.name, xn1.name], writes=["d_qf"], tag=f"d_y{h}_{nch}")

    def load_y(dt_):
        return qy[:, dt_ * 2048:(dt_ + 1) * 2048]
    ln_stats(load_y, "d_qf", True, "d_l2_")
    for dt_ in range(ND):
        p.op("sync", lambda s, dt_=dt_: s.dma_start(
            x1dt[:, :], d["x1_d"][dt_ * 128:(dt_ + 1) * 128, :]),
            writes=[x1dt.name], dma=True, tag=f"d_lx1c{dt_}")
        p.op("vector", lambda v, dt_=dt_: v.tensor_tensor(
            xsq[:, :], qy[:, dt_ * 2048:(dt_ + 1) * 2048], Mb[:, :], ALU.add),
            reads=["d_qf", Mb.name], writes=[xsq.name], tag=f"d_n0_{dt_}")
        p.op("vector", lambda v: v.tensor_tensor(
            xsq[:, :], xsq[:, :], Ab[:, :], ALU.mult),
            reads=[xsq.name, Ab.name], writes=[xsq.name], tag=f"d_n1_{dt_}")
        p.op("scalar", lambda s, dt_=dt_: s.activation(
            xsq[:, :], xsq[:, :], AF.Identity,
            bias=bet[:, dt_:dt_ + 1], scale=gam[:, dt_:dt_ + 1]),
            reads=[xsq.name, gam.name, bet.name], writes=[xsq.name], tag=f"d_n2_{dt_}")
        p.op("vector", lambda v: v.tensor_tensor(
            xsq[:, :], xsq[:, :], x1dt[:, :], ALU.add),
            reads=[xsq.name, x1dt.name], writes=[xsq.name], tag=f"d_n3_{dt_}")
        p.op("sync", lambda s, dt_=dt_: s.dma_start(
            d["x1b_d"][dt_ * 128:(dt_ + 1) * 128, :], xsq[:, :]),
            reads=[xsq.name], writes=["x1b_d"], dma=True, tag=f"d_stx{dt_}")


def build_phaseD():
    nc = bass.Bass("TRN2", target_bir_lowering=False)
    es = contextlib.ExitStack()
    d = {}
    for nm, shp in (("ones_f", (128, 128)), ("wq1T", (D, D)), ("wk1T", (D, D)),
                    ("wv1T", (D, D)), ("bq1", (128, 8)), ("gamma1", (128, 8)),
                    ("beta1", (128, 8)), ("bk1row", (1, D)), ("bv1row", (1, D))):
        d[nm] = nc.dram_tensor(nm, list(shp), F32, kind="ExternalInput").ap()
    d["x1_d"] = nc.dram_tensor("x1_d", [D, T], F32, kind="ExternalInput").ap()
    d["xn1_d"] = nc.dram_tensor("xn1_d", [D, T], BF16, kind="ExternalOutput").ap()
    d["murow1_d"] = nc.dram_tensor("murow1_d", [1, T], F32, kind="ExternalOutput").ap()
    d["arow1_d"] = nc.dram_tensor("arow1_d", [1, T], F32, kind="ExternalOutput").ap()
    d["x1b_d"] = nc.dram_tensor("x1b_d", [D, T], F32, kind="ExternalOutput").ap()
    p = Prog(nc)
    with es:
        phase_l1_attn(p, d, es)
        p.emit()
    nc.reset()
    return nc



XFW = 2052

def phase_l1_ffn(p, d, es):
    nc = p.nc
    sb = lambda name, shape, dt: es.enter_context(nc.sbuf_tensor(name, shape, dt))
    ps = lambda name, shape, dt: es.enter_context(nc.psum_tensor(name, shape, dt))

    ident_bf = sb("e_identb", [128, 128], BF16)
    ident_f = sb("e_identf", [128, 128], F32)
    ones_f = sb("e_ones", [128, 128], F32)
    p.op("gpsimd", lambda g_: g_.dma_start(ident_bf[:, :], d["ident_f"][:, :]),
         writes=[ident_bf.name], dma=True, tag="e_ldib")
    p.op("sync", lambda s: s.dma_start(ident_f[:, :], d["ident_f"][:, :]),
         writes=[ident_f.name], dma=True, tag="e_ldif")
    p.op("sync", lambda s: s.dma_start(ones_f[:, :], d["ones_f"][:, :]),
         writes=[ones_f.name], dma=True, tag="e_ldof")
    rcs = sb("e_rcs", [128, 1024], BF16)
    rss = sb("e_rss", [128, 1024], BF16)
    rci = sb("e_rci", [128, 1024], BF16)
    rsi = sb("e_rsi", [128, 1024], BF16)
    for nm, t_ in (("rope1_cs", rcs), ("rope1_ss", rss), ("rope1_ci", rci),
                   ("rope1_si", rsi)):
        p.op("gpsimd", lambda g_, nm=nm, t_=t_: g_.dma_start(
            t_[:, :].rearrange("p (t n) -> p t n", n=64),
            d[nm][:, :].rearrange("(t p) n -> p t n", p=128)),
            writes=[t_.name], dma=True, tag="e_ld" + nm)
    lm = sb("e_lm", [128, 1024], BF16)
    lm0 = sb("e_lm0", [128, 1024], BF16)
    for nm, t_ in (("lmask", lm), ("lmask0", lm0)):
        p.op("gpsimd", lambda g_, nm=nm, t_=t_: g_.dma_start(
            t_[:, :].rearrange("p (t n) -> p t n", n=512),
            d[nm][:, :].rearrange("(t p) n -> p t n", p=128)),
            writes=[t_.name], dma=True, tag="e_ld" + nm)
    qsr = sb("e_qsr", [1, 64], F32)
    ksr = sb("e_ksr", [1, 64], F32)
    QSM = sb("e_qsm", [128, 64], F32)
    KSM = sb("e_ksm", [128, 64], F32)
    lngc = sb("e_lng", [128, 8], F32)
    lnbc = sb("e_lnb", [128, 8], F32)
    b1ac = sb("e_b1a", [128, 22], F32)
    b1gc = sb("e_b1g", [128, 22], F32)
    b2c = sb("e_b2c", [128, 8], F32)
    epsc = sb("e_epsc", [128, 1], F32)
    p.op("vector", lambda v: v.memset(epsc[:, :], 1e-5), writes=[epsc.name], tag="e_eps")

    xf = sb("e_xf", [128, 8 * XFW], F32)
    castb = sb("e_cast", [128, 1024], BF16)
    castb2 = sb("e_cast2", [128, 1024], BF16)
    rowsA = sb("e_rA", [1, 2048], F32)
    rowsB = sb("e_rB", [1, 2048], F32)
    POOL = sb("e_pool", [128, 55296], BF16)

    def pv(off_kb, nelem, dt):
        off = int(off_kb * 512)
        if dt == BF16:
            return POOL[:, off:off + nelem]
        return POOL[:, off:off + nelem * 2].bitcast(F32)

    P1 = ps("e_P1", [128, 512], F32)
    P2 = ps("e_P2", [128, 512], F32)
    PO = ps("e_PO", [128, 512], F32)
    PR = ps("e_PR", [128, 512], F32)
    PT = ps("e_PT", [128, 1024], BF16)
    PT2 = ps("e_PT2", [128, 1024], BF16)
    AVa = ps("e_AVa", [128, 512], F32)
    AVb = ps("e_AVb", [128, 512], F32)
    PA = AVa
    p.mark_psum(P1.name, P2.name, PT.name, PT2.name, PO.name, PR.name,
                AVa.name, AVb.name)

    def bcast_row(rowap, rowkey, dst, dstkey, nchunks, tagp):
        for ch in range(nchunks):
            p.op("tensor", lambda t, ch=ch: t.matmul(
                P1[:, 0:512], ones_f[0:1, :], rowap[:, ch * 512:(ch + 1) * 512],
                start=True, stop=True),
                reads=[ones_f.name, rowkey], writes=[P1.name], tag=tagp + f"b{ch}")
            p.op("vector", lambda v, ch=ch: v.tensor_copy(
                dst[:, ch * 512:(ch + 1) * 512], P1[:, 0:512]),
                reads=[P1.name], writes=[dstkey], tag=tagp + f"e{ch}")

    x1dt = pv(0, 2048, F32)
    Mb = pv(8, 2048, F32)
    Ab = pv(16, 2048, F32)
    murow = pv(24, 2048, F32)
    arow = pv(32, 2048, F32)
    p.op("sync", lambda s: s.dma_start(murow[0:1, :], d["murow1_d"][:, :]),
         writes=["e_mur"], dma=True, tag="e_ldmu")
    p.op("sync", lambda s: s.dma_start(arow[0:1, :], d["arow1_d"][:, :]),
         writes=["e_ar"], dma=True, tag="e_ldar")
    bcast_row(murow[0:1, :], "e_mur", Mb, "e_Mb", 4, "e_ibm")
    bcast_row(arow[0:1, :], "e_ar", Ab, "e_Ab", 4, "e_iba")
    for kt in range(8):
        p.op("vector", lambda v, kt=kt: v.memset(xf[:, kt * XFW:kt * XFW + 1], 0.0),
             writes=["e_xf"], tag=f"e_xz{kt}")
    for dt_ in range(ND):
        p.op("sync", lambda s, dt_=dt_: s.dma_start(
            x1dt[:, :], d["x1_d"][dt_ * 128:(dt_ + 1) * 128, :]),
            writes=["e_x1dt"], dma=True, tag=f"e_lx1{dt_}")
        p.op("vector", lambda v, dt_=dt_: v.tensor_tensor(
            xf[:, dt_ * XFW + 1: dt_ * XFW + 2049], x1dt[:, :], Mb[:, :], ALU.add),
            reads=["e_x1dt", "e_Mb", "e_xf"], writes=["e_xf"], tag=f"e_xi0{dt_}")
        p.op("vector", lambda v, dt_=dt_: v.tensor_tensor(
            xf[:, dt_ * XFW + 1: dt_ * XFW + 2049],
            xf[:, dt_ * XFW + 1: dt_ * XFW + 2049], Ab[:, :], ALU.mult),
            reads=["e_Ab", "e_xf"], writes=["e_xf"], tag=f"e_xi1{dt_}")
    p.barrier()

    def cast_lhsT(tt, cb, tagp):
        p.op("vector", lambda v, tt=tt, cb=cb: v.tensor_copy(
            cb[:, 0:512].rearrange("p (k n) -> p k n", n=128),
            xf[:, :].rearrange("p (k n) -> p k n", n=XFW)[:, 0:4, tt * 128 + 1: tt * 128 + 129]),
            reads=["e_xf"], writes=[cb.name], tag=tagp + "a")
        p.op("vector", lambda v, tt=tt, cb=cb: v.tensor_copy(
            cb[:, 512:1024].rearrange("p (k n) -> p k n", n=128),
            xf[:, :].rearrange("p (k n) -> p k n", n=XFW)[:, 4:8, tt * 128: tt * 128 + 128]),
            reads=["e_xf"], writes=[cb.name], tag=tagp + "b")

    for li in range(int(_os.environ.get("E_LAYERS", "6"))):
        lk = f"e{li}_"
        for nm, t_, tg in (("qs1", qsr, "qs"), ("ks1", ksr, "ks")):
            p.op("sync", lambda s, li=li, nm=nm, t_=t_: s.dma_start(
                t_[:, :], d[nm][li:li + 1, :]),
                writes=[t_.name], dma=True, tag=lk + "ld" + tg)
        p.op("tensor", lambda t: t.matmul(P1[:, 0:64], ones_f[0:1, :], qsr[:, :],
                                          start=True, stop=True),
             reads=[ones_f.name, qsr.name], writes=[P1.name], tag=lk + "qsm")
        p.op("vector", lambda v: v.tensor_copy(QSM[:, :], P1[:, 0:64]),
             reads=[P1.name], writes=[QSM.name], tag=lk + "eqsm")
        p.op("tensor", lambda t: t.matmul(P2[:, 0:64], ones_f[0:1, :], ksr[:, :],
                                          start=True, stop=True),
             reads=[ones_f.name, ksr.name], writes=[P2.name], tag=lk + "ksm")
        p.op("vector", lambda v: v.tensor_copy(KSM[:, :], P2[:, 0:64]),
             reads=[P2.name], writes=[KSM.name], tag=lk + "eksm")
        for nm, t_, tg in (("lng1", lngc, "g"), ("lnb1", lnbc, "b"),
                           ("b1a1", b1ac, "ba"), ("b1g1", b1gc, "bg"),
                           ("b21c", b2c, "b2")):
            p.op("sync", lambda s, li=li, nm=nm, t_=t_: s.dma_start(
                t_[:, :], d[nm][li, :, :]),
                writes=[t_.name], dma=True, tag=lk + "ld" + tg)

        Wq8 = pv(0, 4096, BF16)
        qraw = pv(8, 2048, F32)
        sqs = pv(16, 2048, F32)
        q8 = pv(24, 8192, BF16)
        k8 = pv(40, 8192, BF16)
        v8 = pv(56, 8192, BF16)
        ao = pv(72, 16384, BF16)
        qhT = pv(8, 2048, BF16)
        khT = pv(12, 2048, BF16)
        atn = pv(16, 1024, BF16)
        nrm = pv(18, 64, F32)
        nmx = pv(18.5, 64, F32)
        Wo8 = pv(0, 8192, BF16)
        WQ8K, QRK, SQK = lk + "W8", lk + "qraw", lk + "sqs"
        Q8K, K8K, V8K, AOK = lk + "q8", lk + "k8", lk + "v8", lk + "ao"
        KqhT, KkhT, Katn, Knrm, Knmx = (lk + "qhT", lk + "khT", lk + "atn",
                                        lk + "nrm", lk + "nmx")
        for hh in range(2):
            hk = lk + f"h{hh}_"
            for pi, pname in ((0, "q"), (1, "k"), (2, "v")):
                p.op("gpsimd", lambda g_, li=li, pi=pi, hh=hh: g_.dma_start(
                    Wq8.rearrange("p (k n) -> p k n", n=512),
                    d["Wqkv1"][li, :, pi * 1024 + hh * 512: pi * 1024 + (hh + 1) * 512]
                    .rearrange("(k p) n -> p k n", p=128)),
                    writes=[WQ8K], dma=True, tag=hk + "ldw" + pname)
                for tb4 in range(4):
                    for t4 in range(4):
                        tt = tb4 * 4 + t4
                        cb = castb if tt % 2 == 0 else castb2
                        pq = PO if tt % 2 == 0 else PR
                        cast_lhsT(tt, cb, hk + pname + f"c{tt}_")
                        def mm_p(t, cb=cb, pq=pq):
                            for kt in range(8):
                                i1 = t.matmul(pq[:, 0:512],
                                              cb[:, kt * 128:(kt + 1) * 128],
                                              Wq8[:, kt * 512:(kt + 1) * 512],
                                              start=(kt == 0), stop=(kt == 7))
                            return i1
                        p.op("tensor", mm_p, reads=[cb.name, WQ8K], writes=[pq.name],
                             tag=hk + pname + f"m{tt}")
                        if pname == "v":
                            p.op("scalar", lambda s, tt=tt, pq=pq: s.copy(
                                v8[:, tt * 512:(tt + 1) * 512], pq[:, 0:512]),
                                reads=[pq.name], writes=[V8K], tag=hk + f"ev{tt}")
                        else:
                            p.op("scalar", lambda s, t4=t4, pq=pq: s.copy(
                                qraw[:, t4 * 512:(t4 + 1) * 512], pq[:, 0:512]),
                                reads=[pq.name], writes=[QRK], tag=hk + pname + f"e{tt}")
                    if pname == "v":
                        continue
                    dst = q8 if pname == "q" else k8
                    dkey = Q8K if pname == "q" else K8K
                    SM = QSM if pname == "q" else KSM
                    smk = QSM.name if pname == "q" else KSM.name
                    ct, st_ = (rcs, rss) if pname == "q" else (rci, rsi)
                    ck, sk = ct.name, st_.name
                    q3 = qraw.rearrange("p (g e) -> p g e", e=64)
                    s3 = sqs.rearrange("p (g e) -> p g e", e=64)
                    p.op("scalar", lambda s: s.activation(sqs[:, :], qraw[:, :], AF.Square),
                         reads=[QRK], writes=[SQK], tag=hk + pname + f"sq{tb4}")
                    p.op("vector", lambda v, q3=q3, s3=s3: v.tensor_reduce(
                        nrm[:, 0:32], s3, AX.X, ALU.add),
                        reads=[SQK], writes=[lk + "nrm"], tag=hk + pname + f"n2{tb4}")
                    p.op("scalar", lambda s: s.activation(nrm[:, 0:32], nrm[:, 0:32], AF.Ln),
                         reads=[lk + "nrm"], writes=[lk + "nrm"], tag=hk + pname + f"ln{tb4}")
                    p.op("scalar", lambda s: s.activation(nrm[:, 0:32], nrm[:, 0:32],
                                                          AF.Exp, scale=0.5),
                         reads=[lk + "nrm"], writes=[lk + "nrm"], tag=hk + pname + f"ex{tb4}")
                    p.op("vector", lambda v: v.tensor_scalar(
                        nrm[:, 0:32], nrm[:, 0:32], 1e-12, None, ALU.max),
                        reads=[lk + "nrm"], writes=[lk + "nrm"], tag=hk + pname + f"mx{tb4}")
                    p.op("vector", lambda v: v.reciprocal(nrm[:, 0:32], nrm[:, 0:32]),
                         reads=[lk + "nrm"], writes=[lk + "nrm"], tag=hk + pname + f"rc{tb4}")
                    nb = nrm[:, 0:32].unsqueeze(2).broadcast_to((128, 32, 64))
                    p.op("vector", lambda v, q3=q3, nb=nb: v.tensor_tensor(
                        q3, q3, nb, ALU.mult),
                        reads=[QRK, lk + "nrm"], writes=[QRK], tag=hk + pname + f"nm{tb4}")
                    smb = SM[:, :].unsqueeze(1).broadcast_to((128, 32, 64))
                    p.op("vector", lambda v, q3=q3, smb=smb: v.tensor_tensor(
                        q3, q3, smb, ALU.mult),
                        reads=[QRK, smk], writes=[QRK], tag=hk + pname + f"sm{tb4}")
                    for t4 in range(4):
                        tt = tb4 * 4 + t4
                        qt3 = qraw[:, t4 * 512:(t4 + 1) * 512].rearrange(
                            "p (h e) -> p h e", e=64)
                        st3 = sqs[:, t4 * 512:(t4 + 1) * 512].rearrange(
                            "p (h e) -> p h e", e=64)
                        d3 = dst[:, tt * 512:(tt + 1) * 512].rearrange(
                            "p (h e) -> p h e", e=64)
                        cb1 = ct[:, tt * 64:tt * 64 + 32].unsqueeze(1).broadcast_to((128, 8, 32))
                        cb2 = ct[:, tt * 64 + 32:tt * 64 + 64].unsqueeze(1).broadcast_to((128, 8, 32))
                        sb1 = st_[:, tt * 64:tt * 64 + 32].unsqueeze(1).broadcast_to((128, 8, 32))
                        sb2 = st_[:, tt * 64 + 32:tt * 64 + 64].unsqueeze(1).broadcast_to((128, 8, 32))
                        p.op("vector", lambda v, qt3=qt3, st3=st3, cb1=cb1: v.tensor_tensor(
                            st3[:, :, 0:32], qt3[:, :, 0:32], cb1, ALU.mult),
                            reads=[QRK, ck], writes=[SQK], tag=hk + pname + f"r1_{tt}")
                        p.op("vector", lambda v, qt3=qt3, st3=st3, sb1=sb1: v.tensor_tensor(
                            st3[:, :, 32:64], qt3[:, :, 32:64], sb1, ALU.mult),
                            reads=[QRK, sk], writes=[SQK], tag=hk + pname + f"r2_{tt}")
                        p.op("vector", lambda v, st3=st3, d3=d3: v.tensor_tensor(
                            d3[:, :, 0:32], st3[:, :, 0:32], st3[:, :, 32:64], ALU.subtract),
                            reads=[SQK], writes=[dkey], tag=hk + pname + f"r3_{tt}")
                        p.op("vector", lambda v, qt3=qt3, st3=st3, cb2=cb2: v.tensor_tensor(
                            st3[:, :, 0:32], qt3[:, :, 32:64], cb2, ALU.mult),
                            reads=[QRK, ck], writes=[SQK], tag=hk + pname + f"r4_{tt}")
                        p.op("vector", lambda v, qt3=qt3, st3=st3, sb2=sb2: v.tensor_tensor(
                            st3[:, :, 32:64], qt3[:, :, 0:32], sb2, ALU.mult),
                            reads=[QRK, sk], writes=[SQK], tag=hk + pname + f"r5_{tt}")
                        p.op("vector", lambda v, st3=st3, d3=d3: v.tensor_tensor(
                            d3[:, :, 32:64], st3[:, :, 0:32], st3[:, :, 32:64], ALU.add),
                            reads=[SQK], writes=[dkey], tag=hk + pname + f"r6_{tt}")
            eb = [pv(19, 1024, BF16), pv(21, 1024, BF16)]
            atn2 = [pv(16, 1024, BF16), pv(106, 1024, BF16)]
            nrm2v = pv(104, 128, F32)
            nmx2v = pv(105, 128, F32)

            def e_stage1(hl, w, it2):
                par = it2 % 2
                simps = (P1, P2) if par == 0 else (PO, PR)
                ebuf = eb[par]
                maskt = lm0 if w == 0 else lm
                maskk = lm0.name if w == 0 else lm.name
                wk_ = hk + f"a{hl}_w{w}_"
                for it in range(2):
                    pp = simps[it]
                    def mm_sim(t, w=w, it=it, pp=pp, maskt=maskt):
                        t.matmul(pp[:, 0:512], ident_bf[:, :],
                                 maskt[:, it * 512:(it + 1) * 512],
                                 start=True, stop=False)
                        if w == 0:
                            return t.matmul(pp[:, 256:512],
                                            qhT[0:64, it * 128:(it + 1) * 128],
                                            khT[0:64, 0:256], start=False, stop=True)
                        return t.matmul(pp[:, 0:512],
                                        qhT[0:64, w * 256 + it * 128: w * 256 + (it + 1) * 128],
                                        khT[0:64, (w - 1) * 256:(w + 1) * 256],
                                        start=False, stop=True)
                    p.op("tensor", mm_sim,
                         reads=[KqhT, KkhT, ident_bf.name, maskk],
                         writes=[pp.name], tag=wk_ + f"sim{it}")
                for it in range(2):
                    pp = simps[it]
                    nb = par * 8 + it * 4
                    p.op("vector", lambda v, pp=pp, nb=nb: v.reduce_max(
                        nmx2v[:, nb:nb + 1], pp[:, 0:512], AX.X),
                        reads=[pp.name], writes=[f"{lk}nmx{par}"], tag=wk_ + f"mx{it}")
                    p.op("vector", lambda v, nb=nb: v.tensor_scalar(
                        nmx2v[:, nb + 1:nb + 2], nmx2v[:, nb:nb + 1],
                        -8.0, None, ALU.mult),
                        reads=[f"{lk}nmx{par}"], writes=[f"{lk}nmx{par}"],
                        tag=wk_ + f"nm{it}")
                    p.op("scalar", lambda s, pp=pp, it=it, nb=nb, ebuf=ebuf: s.activation(
                        ebuf[:, it * 512:(it + 1) * 512], pp[:, 0:512], AF.Exp,
                        bias=nmx2v[:, nb + 1:nb + 2], scale=8.0,
                        accum_out=nmx2v[:, nb + 2:nb + 3]),
                        reads=[pp.name, f"{lk}nmx{par}"],
                        writes=[f"{lk}eb{par}", f"{lk}nmx{par}z"], tag=wk_ + f"ex{it}")
                    p.op("vector", lambda v, nb=nb: v.reciprocal(
                        nmx2v[:, nb + 3:nb + 4], nmx2v[:, nb + 2:nb + 3]),
                        reads=[f"{lk}nmx{par}z"], writes=[f"{lk}nmx{par}z"],
                        tag=wk_ + f"rc{it}")
                    p.op("vector", lambda v, it=it, nb=nb, ebuf=ebuf: v.tensor_scalar(
                        ebuf[:, it * 512:(it + 1) * 512],
                        ebuf[:, it * 512:(it + 1) * 512],
                        nmx2v[:, nb + 3:nb + 4], None, ALU.mult),
                        reads=[f"{lk}eb{par}", f"{lk}nmx{par}z"],
                        writes=[f"{lk}eb{par}"], tag=wk_ + f"sc{it}")

            def e_stage2(hl, w, it2):
                par = it2 % 2
                ebuf = eb[par]
                PTp = PT if par == 0 else PT2
                ptk = f"e_PTx{par}"
                AVp = AVa if par == 0 else AVb
                av = atn2[par]
                wk_ = hk + f"a{hl}_w{w}_"
                def mm_at(t, ebuf=ebuf, PTp=PTp):
                    for jt in range(4):
                        for it in range(2):
                            i1 = t.transpose(
                                PTp[:, jt * 256 + it * 128: jt * 256 + (it + 1) * 128],
                                ebuf[:, it * 512 + jt * 128: it * 512 + (jt + 1) * 128],
                                ident_bf[:, :])
                    return i1
                p.op("tensor", mm_at, reads=[f"{lk}eb{par}", ident_bf.name],
                     writes=[ptk], tag=wk_ + "atr")
                p.op("scalar", lambda s, av=av, PTp=PTp: s.copy(av[:, :], PTp[:, 0:1024]),
                     reads=[ptk], writes=[f"{lk}atn{par}"], tag=wk_ + "ate")
                def mm_av(t, hl=hl, w=w, av=av, AVp=AVp):
                    jts = list(range(2, 4) if w == 0 else range(4))
                    for ix, jt in enumerate(jts):
                        gt = (w - 1) * 2 + jt
                        i1 = t.matmul(AVp[0:64, 0:256],
                                      v8[:, gt * 512 + hl * 64: gt * 512 + (hl + 1) * 64],
                                      av[:, jt * 256:(jt + 1) * 256],
                                      start=(ix == 0), stop=(ix == len(jts) - 1))
                    return i1
                p.op("tensor", mm_av, reads=[V8K, f"{lk}atn{par}"], writes=[AVp.name],
                     tag=wk_ + "av")
                hg = hh * 8 + hl
                p.op("vector", lambda v, hg=hg, w=w, AVp=AVp: v.tensor_copy(
                    ao[(hg % 2) * 64:(hg % 2) * 64 + 64,
                       (hg // 2) * 2048 + w * 256:(hg // 2) * 2048 + (w + 1) * 256],
                    AVp[0:64, 0:256]),
                    reads=[AVp.name], writes=[AOK], tag=wk_ + "no")

            eprev = None
            it2 = 0
            for hl in range(8):
                ck_ = hk + f"a{hl}_"
                for srcb, dsT, dkT, srckey in ((q8, qhT, KqhT, Q8K),
                                               (k8, khT, KkhT, K8K)):
                    for grp in range(2):
                        def mm_tr8(t, srcb=srcb, hl=hl, grp=grp):
                            for j in range(8):
                                tt = grp * 8 + j
                                i1 = t.transpose(PT[0:64, j * 128:(j + 1) * 128],
                                                 srcb[:, tt * 512 + hl * 64: tt * 512 + (hl + 1) * 64],
                                                 ident_bf[:, :])
                            return i1
                        p.op("tensor", mm_tr8, reads=[srckey, ident_bf.name],
                             writes=["e_PTx0"], tag=ck_ + f"tr{grp}" + dkT)
                        p.op("vector", lambda v, dsT=dsT, grp=grp: v.tensor_copy(
                            dsT[0:64, grp * 1024:(grp + 1) * 1024], PT[0:64, 0:1024]),
                            reads=["e_PTx0"], writes=[dkT, QRK], tag=ck_ + f"te{grp}" + dkT)
                for w in range(8):
                    e_stage1(hl, w, it2)
                    if eprev is not None:
                        e_stage2(*eprev)
                    eprev = (hl, w, it2)
                    it2 += 1
            e_stage2(*eprev)
        p.op("gpsimd", lambda g_, li=li: g_.dma_start(
            Wo8.rearrange("p (k n) -> p k n", n=1024),
            d["Wo1"][li, :, :].rearrange("(k p) n -> p k n", p=128)),
            writes=[lk + "Wo8", WQ8K, QRK, KqhT, KkhT], dma=True, tag=lk + "ldwo")
        for dt_ in range(ND):
            for nch in range(4):
                def mm_wo(t, dt_=dt_, nch=nch):
                    for kt in range(8):
                        i1 = t.matmul(PO[:, 0:512],
                                      Wo8[:, kt * 1024 + dt_ * 128: kt * 1024 + dt_ * 128 + 128],
                                      ao[:, kt * 2048 + nch * 512: kt * 2048 + (nch + 1) * 512],
                                      start=(kt == 0), stop=(kt == 7))
                    return i1
                p.op("tensor", mm_wo, reads=[lk + "Wo8", AOK], writes=[PO.name],
                     tag=lk + f"wo{dt_}_{nch}")
                p.op("vector", lambda v, dt_=dt_, nch=nch: v.tensor_tensor(
                    xf[:, dt_ * XFW + 1 + nch * 512: dt_ * XFW + 1 + (nch + 1) * 512],
                    PO[:, 0:512],
                    xf[:, dt_ * XFW + 1 + nch * 512: dt_ * XFW + 1 + (nch + 1) * 512],
                    ALU.add),
                    reads=[PO.name, "e_xf"], writes=["e_xf"], tag=lk + f"wa{dt_}_{nch}")
        p.barrier()
        if _os.environ.get("E_NOFFN"):
            continue

        W1a = pv(0, 8192, BF16)
        W1g = pv(16, 8192, BF16)
        W2s = pv(32, 8192, BF16)
        gsb = pv(48, 4096, BF16)
        xnf = pv(56, 16384, BF16)
        xsqe = pv(88, 512, F32)
        Mbc = pv(90, 512, F32)
        Abc = pv(92, 512, F32)
        KW1a, KW1g, KW2 = lk + "W1a", lk + "W1g", lk + "W2"
        Kg, Kxnf, Kxq = lk + "g", lk + "xnf", lk + "xq"
        Kmb, Kab = lk + "mbc", lk + "abc"
        p.op("vector", lambda v: v.memset(rowsA[:, :], 0.0), writes=["e_rA"], tag=lk + "za")
        p.op("vector", lambda v: v.memset(rowsB[:, :], 0.0), writes=["e_rB"], tag=lk + "zb")
        for dt_ in range(ND):
            off = dt_ * XFW + (1 if dt_ < 4 else 0)
            for nch in range(4):
                p.op("tensor", lambda t, off=off, nch=nch: t.matmul(
                    PA[0:1, 0:512], ones_f[:, 0:1],
                    xf[:, off + nch * 512: off + (nch + 1) * 512],
                    start=True, stop=True),
                    reads=["e_xf", ones_f.name], writes=[PA.name],
                    tag=lk + f"s1_{dt_}_{nch}")
                p.op("vector", lambda v, nch=nch: v.tensor_tensor(
                    rowsA[:, nch * 512:(nch + 1) * 512],
                    rowsA[:, nch * 512:(nch + 1) * 512], PA[0:1, 0:512], ALU.add),
                    reads=[PA.name, "e_rA"], writes=["e_rA"], tag=lk + f"a1_{dt_}_{nch}")
                p.op("scalar", lambda s, off=off, nch=nch: s.activation(
                    xsqe[:, 0:512], xf[:, off + nch * 512: off + (nch + 1) * 512],
                    AF.Square),
                    reads=["e_xf"], writes=[Kxq], tag=lk + f"sq_{dt_}_{nch}")
                p.op("tensor", lambda t: t.matmul(
                    PA[0:1, 0:512], ones_f[:, 0:1], xsqe[:, 0:512],
                    start=True, stop=True),
                    reads=[Kxq, ones_f.name], writes=[PA.name],
                    tag=lk + f"s2_{dt_}_{nch}")
                p.op("vector", lambda v, nch=nch: v.tensor_tensor(
                    rowsB[:, nch * 512:(nch + 1) * 512],
                    rowsB[:, nch * 512:(nch + 1) * 512], PA[0:1, 0:512], ALU.add),
                    reads=[PA.name, "e_rB"], writes=["e_rB"], tag=lk + f"a2_{dt_}_{nch}")
        p.op("vector", lambda v: v.tensor_scalar(
            rowsA[:, :], rowsA[:, :], -1.0 / 1024.0, None, ALU.mult),
            reads=["e_rA"], writes=["e_rA"], tag=lk + "mu")
        for nch in range(4):
            p.op("vector", lambda v, nch=nch: v.tensor_tensor(
                xsqe[0:1, 0:512], rowsA[:, nch * 512:(nch + 1) * 512],
                rowsA[:, nch * 512:(nch + 1) * 512], ALU.mult),
                reads=["e_rA"], writes=[Kxq], tag=lk + f"mu2{nch}")
            p.op("vector", lambda v, nch=nch: v.scalar_tensor_tensor(
                rowsB[:, nch * 512:(nch + 1) * 512],
                rowsB[:, nch * 512:(nch + 1) * 512], 1.0 / 1024.0,
                xsqe[0:1, 0:512], ALU.mult, ALU.subtract),
                reads=["e_rB", Kxq], writes=["e_rB"], tag=lk + f"var{nch}")
        p.op("scalar", lambda s: s.activation(rowsB[:, :], rowsB[:, :], AF.Ln,
                                              bias=epsc[0:1, :]),
             reads=["e_rB", epsc.name], writes=["e_rB"], tag=lk + "lnv")
        p.op("scalar", lambda s: s.activation(rowsB[:, :], rowsB[:, :], AF.Exp, scale=-0.5),
             reads=["e_rB"], writes=["e_rB"], tag=lk + "rstd")
        for tb in range(4):
            tbk0 = lk + f"n{tb}_"
            p.op("tensor", lambda t, tb=tb: t.matmul(
                P1[:, 0:512], ones_f[0:1, :], rowsA[:, tb * 512:(tb + 1) * 512],
                start=True, stop=True),
                reads=[ones_f.name, "e_rA"], writes=[P1.name], tag=tbk0 + "bm")
            p.op("vector", lambda v: v.tensor_copy(Mbc[:, 0:512], P1[:, 0:512]),
                 reads=[P1.name], writes=[Kmb], tag=tbk0 + "em")
            p.op("tensor", lambda t, tb=tb: t.matmul(
                P2[:, 0:512], ones_f[0:1, :], rowsB[:, tb * 512:(tb + 1) * 512],
                start=True, stop=True),
                reads=[ones_f.name, "e_rB"], writes=[P2.name], tag=tbk0 + "ba")
            p.op("vector", lambda v: v.tensor_copy(Abc[:, 0:512], P2[:, 0:512]),
                 reads=[P2.name], writes=[Kab], tag=tbk0 + "ea")
            for kt in range(8):
                off = kt * XFW + (1 if kt < 4 else 0) + tb * 512
                p.op("vector", lambda v, off=off: v.tensor_tensor(
                    xsqe[:, 0:512], xf[:, off:off + 512], Mbc[:, 0:512], ALU.add),
                    reads=["e_xf", Kmb], writes=[Kxq], tag=tbk0 + f"x0_{kt}")
                p.op("vector", lambda v: v.tensor_tensor(
                    xsqe[:, 0:512], xsqe[:, 0:512], Abc[:, 0:512], ALU.mult),
                    reads=[Kxq, Kab], writes=[Kxq], tag=tbk0 + f"x1_{kt}")
                p.op("scalar", lambda s, kt=kt, tb=tb: s.activation(
                    xnf[:, kt * 2048 + tb * 512: kt * 2048 + (tb + 1) * 512],
                    xsqe[:, 0:512], AF.Identity,
                    bias=lnbc[:, kt:kt + 1], scale=lngc[:, kt:kt + 1]),
                    reads=[Kxq, lngc.name, lnbc.name], writes=[Kxnf],
                    tag=tbk0 + f"x2_{kt}")
        for fc in range(3):
            fck = lk + f"f{fc}_"
            fbase = fc * 1024
            fwid = 1024 if fc < 2 else 682
            mt_sizes = [128] * 8 if fc < 2 else [128] * 5 + [42]
            nmt = len(mt_sizes)
            p.op("gpsimd", lambda g_, li=li, fbase=fbase, fwid=fwid: g_.dma_start(
                W1a.rearrange("p (k n) -> p k n", n=1024)[:, :, 0:fwid],
                d["W11"][li, :, fbase:fbase + fwid].rearrange("(k p) n -> p k n", p=128)),
                writes=[KW1a], dma=True, tag=fck + "ldw1a")
            p.op("gpsimd", lambda g_, li=li, fbase=fbase, fwid=fwid: g_.dma_start(
                W1g.rearrange("p (k n) -> p k n", n=1024)[:, :, 0:fwid],
                d["W11"][li, :, 2730 + fbase:2730 + fbase + fwid]
                .rearrange("(k p) n -> p k n", p=128)),
                writes=[KW1g], dma=True, tag=fck + "ldw1g")
            nfull = fwid // 128
            p.op("gpsimd", lambda g_, li=li, fbase=fbase, nfull=nfull: g_.dma_start(
                W2s[:, 0:nfull * 1024].rearrange("p (k n) -> p k n", n=1024),
                d["W21"][li, fbase:fbase + nfull * 128, :]
                .rearrange("(k p) n -> p k n", p=128)),
                writes=[KW2], dma=True, tag=fck + "ldw2")
            if fwid % 128 != 0:
                p.op("gpsimd", lambda g_, li=li, fbase=fbase, nfull=nfull: g_.dma_start(
                    W2s[0:42, nfull * 1024:(nfull + 1) * 1024],
                    d["W21"][li, fbase + nfull * 128:fbase + fwid, :]),
                    writes=[KW2], dma=True, tag=fck + "ldw2t")
            for tb in range(4):
                tbk = fck + f"t{tb}_"
                for mt in range(nmt):
                    msz = mt_sizes[mt]
                    pa_ = P1 if mt % 2 == 0 else PO
                    pg_ = P2 if mt % 2 == 0 else PR
                    def mm_a(t, mt=mt, msz=msz, tb=tb, pa_=pa_):
                        for kt in range(8):
                            i1 = t.matmul(pa_[0:msz, 0:512],
                                          W1a[:, kt * 1024 + mt * 128: kt * 1024 + mt * 128 + msz],
                                          xnf[:, kt * 2048 + tb * 512: kt * 2048 + (tb + 1) * 512],
                                          start=(kt == 0), stop=(kt == 7))
                        return i1
                    p.op("tensor", mm_a, reads=[KW1a, Kxnf], writes=[pa_.name],
                         tag=tbk + f"am{mt}")
                    def mm_g(t, mt=mt, msz=msz, tb=tb, pg_=pg_):
                        for kt in range(8):
                            i1 = t.matmul(pg_[0:msz, 0:512],
                                          W1g[:, kt * 1024 + mt * 128: kt * 1024 + mt * 128 + msz],
                                          xnf[:, kt * 2048 + tb * 512: kt * 2048 + (tb + 1) * 512],
                                          start=(kt == 0), stop=(kt == 7))
                        return i1
                    p.op("tensor", mm_g, reads=[KW1g, Kxnf], writes=[pg_.name],
                         tag=tbk + f"gm{mt}")
                    bia = b1ac[:, fbase // 128 + mt: fbase // 128 + mt + 1]
                    big = b1gc[:, fbase // 128 + mt: fbase // 128 + mt + 1]
                    p.op("scalar", lambda s, mt=mt, msz=msz, big=big, pg_=pg_: s.activation(
                        gsb[0:msz, mt * 512:(mt + 1) * 512], pg_[0:msz, 0:512],
                        AF.Gelu, bias=big[0:msz, :]),
                        reads=[pg_.name, b1gc.name], writes=[Kg], tag=tbk + f"ge{mt}")
                    p.op("vector", lambda v, mt=mt, msz=msz, bia=bia, pa_=pa_: v.scalar_tensor_tensor(
                        gsb[0:msz, mt * 512:(mt + 1) * 512], pa_[0:msz, 0:512],
                        bia[0:msz, :], gsb[0:msz, mt * 512:(mt + 1) * 512],
                        ALU.add, ALU.mult),
                        reads=[pa_.name, b1ac.name, Kg], writes=[Kg], tag=tbk + f"h2_{mt}")
                for dt_ in range(ND):
                    po_ = AVa if dt_ % 2 == 0 else AVb
                    def mm_o(t, dt_=dt_, mt_sizes=mt_sizes, nmt=nmt, po_=po_):
                        for kt in range(nmt):
                            ksz = mt_sizes[kt]
                            i1 = t.matmul(po_[:, 0:512],
                                          W2s[0:ksz, kt * 1024 + dt_ * 128: kt * 1024 + (dt_ + 1) * 128],
                                          gsb[0:ksz, kt * 512:(kt + 1) * 512],
                                          start=(kt == 0), stop=(kt == nmt - 1))
                        return i1
                    p.op("tensor", mm_o, reads=[KW2, Kg], writes=[po_.name],
                         tag=tbk + f"o{dt_}")
                    xslice = xf[:, dt_ * XFW + 1 + tb * 512: dt_ * XFW + 1 + (tb + 1) * 512]
                    if fc == 2:
                        p.op("vector", lambda v, xslice=xslice, dt_=dt_, po_=po_: v.scalar_tensor_tensor(
                            xslice, po_[:, 0:512], b2c[:, dt_:dt_ + 1], xslice,
                            ALU.add, ALU.add),
                            reads=[po_.name, b2c.name, "e_xf"], writes=["e_xf"],
                            tag=tbk + f"oa{dt_}")
                    else:
                        p.op("vector", lambda v, xslice=xslice, po_=po_: v.tensor_tensor(
                            xslice, po_[:, 0:512], xslice, ALU.add),
                            reads=[po_.name, "e_xf"], writes=["e_xf"], tag=tbk + f"oa{dt_}")
        p.barrier()

    outsb = pv(0, 512, F32)
    sumb = pv(2, 2048, F32)
    x1f = pv(10, 2048, F32)
    for dt_ in range(ND):
        p.op("sync", lambda s, dt_=dt_: s.dma_start(
            x1f[:, :], d["x1b_d"][dt_ * 128:(dt_ + 1) * 128, :]),
            writes=["e_x1f"], dma=True, tag=f"e_fx{dt_}")
        p.op("vector", lambda v, dt_=dt_: v.tensor_tensor(
            sumb[:, :], xf[:, dt_ * XFW + 1: dt_ * XFW + 2049], x1f[:, :], ALU.add),
            reads=["e_xf", "e_x1f"], writes=["e_sumb"], tag=f"e_fs{dt_}")
        for g_ in range(4):
            def mm_ft(t, g_=g_):
                for j in range(4):
                    tt = g_ * 4 + j
                    i1 = t.transpose(PR[:, j * 128:(j + 1) * 128],
                                     sumb[:, tt * 128:(tt + 1) * 128], ident_f[:, :])
                return i1
            p.op("tensor", mm_ft, reads=["e_sumb", ident_f.name], writes=[PR.name],
                 tag=f"e_ft{dt_}_{g_}")
            p.op("vector", lambda v: v.tensor_copy(outsb[:, 0:512], PR[:, 0:512]),
                 reads=[PR.name], writes=["e_outsb"], tag=f"e_fe{dt_}_{g_}")
            for j in range(4):
                tt = g_ * 4 + j
                eng = "sync" if j % 2 == 0 else "gpsimd"
                p.op(eng, lambda s, dt_=dt_, tt=tt, j=j: s.dma_start(
                    d["out_d"][tt * 128:(tt + 1) * 128, dt_ * 128:(dt_ + 1) * 128],
                    outsb[:, j * 128:(j + 1) * 128]),
                    reads=["e_outsb"], writes=["out_d"], dma=True,
                    tag=f"e_st{dt_}_{tt}")


def build_phaseE():
    nc = bass.Bass("TRN2", target_bir_lowering=False)
    es = contextlib.ExitStack()
    d = {}
    for nm, shp in (("ident_f", (128, 128)), ("ones_f", (128, 128)),
                    ("rope1_cs", (2048, 64)), ("rope1_ss", (2048, 64)),
                    ("rope1_ci", (2048, 64)), ("rope1_si", (2048, 64)),
                    ("lmask", (256, 512)), ("lmask0", (256, 512)),
                    ("Wqkv1", (6, D, 3072)), ("Wo1", (6, D, D)),
                    ("W11", (6, D, 5460)), ("W21", (6, 2730, D)),
                    ("qs1", (6, 64)), ("ks1", (6, 64)),
                    ("lng1", (6, 128, 8)), ("lnb1", (6, 128, 8)),
                    ("b1a1", (6, 128, 22)), ("b1g1", (6, 128, 22)),
                    ("b21c", (6, 128, 8)),
                    ("murow1_d", (1, T)), ("arow1_d", (1, T))):
        d[nm] = nc.dram_tensor(nm, list(shp), F32, kind="ExternalInput").ap()
    d["x1_d"] = nc.dram_tensor("x1_d", [D, T], F32, kind="ExternalInput").ap()
    d["x1b_d"] = nc.dram_tensor("x1b_d", [D, T], F32, kind="ExternalInput").ap()
    d["out_d"] = nc.dram_tensor("out_d", [T, D], F32, kind="ExternalOutput").ap()
    p = Prog(nc)
    with es:
        phase_l1_ffn(p, d, es)
        p.emit()
    nc.reset()
    return nc


def build_phaseB():
    nc = bass.Bass("TRN2", target_bir_lowering=False)
    es = contextlib.ExitStack()
    d = {}
    shapes = {"ident_f": (128, 128), "ones_f": (128, 128),
              "rope0_cos": (128, 128), "rope0_sin": (128, 128),
              "wqT": (D, D), "wkaT": (D, 256), "wkbT": (256, D),
              "wvaT": (D, 256), "wvbT": (256, D), "woT": (D, D)}
    for nm, shp in shapes.items():
        d[nm] = nc.dram_tensor(nm, list(shp), F32, kind="ExternalInput").ap()
    d["xg"] = nc.dram_tensor("xg", [D, NTOK], F32, kind="ExternalInput").ap()
    d["xn_d"] = nc.dram_tensor("xn_d", [D, T], BF16, kind="ExternalInput").ap()
    d["x1a_d"] = nc.dram_tensor("x1a_d", [D, T], F32, kind="ExternalOutput").ap()
    if B_DEBUG:
        d["dbg_qT"] = nc.dram_tensor("dbg_qT", [128, 16384], BF16, kind="ExternalOutput").ap()
        d["dbg_kT"] = nc.dram_tensor("dbg_kT", [128, 16384], BF16, kind="ExternalOutput").ap()
        d["dbg_v65"] = nc.dram_tensor("dbg_v65", [128, 16640], BF16, kind="ExternalOutput").ap()
        d["dbg_ao"] = nc.dram_tensor("dbg_ao", [128, 16384], BF16, kind="ExternalOutput").ap()
    p = Prog(nc)
    with es:
        phase_l0_attn(p, d, es)
        p.emit()
    nc.reset()
    return nc


_PERCORE_SHAPES = {"xg": (D, NTOK), "permT": (128, 128), "permT2": (128, 128)}

_PHASEA_SHARED = ["ones_f", "tstrict", "iota8", "ident_f", "rw_ext"]


def build_phaseA():
    nc = bass.Bass("TRN2", target_bir_lowering=False)
    es = contextlib.ExitStack()
    d = {}
    shapes = {"ones_f": (128, 128), "tstrict": (128, 128), "iota8": (1, 8),
              "ident_f": (128, 128), "rw_ext": (D, 9)}
    for nm, shp in shapes.items():
        d[nm] = nc.dram_tensor(nm, list(shp), F32, kind="ExternalInput").ap()
    for nm, shp in _PERCORE_SHAPES.items():
        d[nm] = nc.dram_tensor(nm, list(shp), F32, kind="ExternalInput").ap()
    d["car_d"] = nc.dram_tensor("car_d", [8, 128], F32, kind="Internal").ap()
    d["mr_d"] = nc.dram_tensor("mr_d", [16, 256], F32, kind="Internal").ap()
    d["Wg_d"] = nc.dram_tensor("Wg_d", [128, 1024], F32, kind="ExternalOutput").ap()
    d["WT9_d"] = nc.dram_tensor("WT9_d", [33, 2048], F32, kind="ExternalOutput").ap()
    d["Wtr_d"] = nc.dram_tensor("Wtr_d", [16, 1024], F32, kind="ExternalOutput").ap()
    d["murow_d"] = nc.dram_tensor("murow_d", [1, 2048], F32, kind="ExternalOutput").ap()
    d["arow_d"] = nc.dram_tensor("arow_d", [1, 2048], F32, kind="ExternalOutput").ap()
    d["xn_d"] = nc.dram_tensor("xn_d", [D, T], BF16, kind="ExternalOutput").ap()
    p = Prog(nc)
    with es:
        phase_ln_gating(p, d, es)
        p.emit()
    nc.reset()
    return nc


def build_fused():
    nc = bass.Bass("TRN2", target_bir_lowering=False)
    d = {}

    def ein(nm, shp, dt=F32):
        d[nm] = nc.dram_tensor(nm, list(shp), dt, kind="ExternalInput").ap()

    def intl(nm, shp, dt=F32):
        d[nm] = nc.dram_tensor(nm, list(shp), dt, kind="Internal").ap()

    ein("ones_f", (128, 128)); ein("tstrict", (128, 128)); ein("iota8", (1, 8))
    ein("ident_f", (128, 128)); ein("rw_ext", (D, 9))
    ein("xg", (D, NTOK)); ein("permT", (128, 128)); ein("permT2", (128, 128))
    ein("rope0_cos", (128, 128)); ein("rope0_sin", (128, 128))
    ein("wqT", (D, D)); ein("wkaT", (D, 256)); ein("wkbT", (256, D))
    ein("wvaT", (D, 256)); ein("wvbT", (256, D)); ein("woT", (D, D))
    ein("moe_w1", (8, D, 4096)); ein("moe_w2", (8, 4096, D))
    ein("moe_b1", (8, 128, 32)); ein("b2ext", (33, D))
    ein("sw1", (D, 4096)); ein("sw2", (4096, D)); ein("sb1", (128, 32))
    ein("wq1T", (D, D)); ein("wk1T", (D, D)); ein("wv1T", (D, D))
    ein("bq1", (128, 8)); ein("gamma1", (128, 8)); ein("beta1", (128, 8))
    ein("bk1row", (1, D)); ein("bv1row", (1, D))
    ein("rope1_cs", (2048, 64)); ein("rope1_ss", (2048, 64))
    ein("rope1_ci", (2048, 64)); ein("rope1_si", (2048, 64))
    ein("lmask", (256, 512)); ein("lmask0", (256, 512))
    ein("Wqkv1", (6, D, 3072)); ein("Wo1", (6, D, D))
    ein("W11", (6, D, 5460)); ein("W21", (6, 2730, D))
    ein("qs1", (6, 64)); ein("ks1", (6, 64))
    ein("lng1", (6, 128, 8)); ein("lnb1", (6, 128, 8))
    ein("b1a1", (6, 128, 22)); ein("b1g1", (6, 128, 22)); ein("b21c", (6, 128, 8))
    intl("car_d", (8, 128)); intl("mr_d", (16, 256))
    intl("Wg_d", (128, 1024)); intl("WT9_d", (33, 2048)); intl("Wtr_d", (16, 1024))
    intl("murow_d", (1, T)); intl("arow_d", (1, T))
    intl("xn_d", (D, T), BF16)
    intl("x1a_d", (D, T)); intl("x1_d", (D, T))
    intl("xn1_d", (D, T), BF16)
    intl("murow1_d", (1, T)); intl("arow1_d", (1, T)); intl("x1b_d", (D, T))
    d["out_d"] = nc.dram_tensor("out_d", [T, D], F32, kind="ExternalOutput").ap()

    p = Prog(nc)
    for fn in (phase_ln_gating, phase_l0_attn, phase_moe, phase_l1_attn,
               phase_l1_ffn):
        with contextlib.ExitStack() as es:
            fn(p, d, es)
        p.barrier()
    with contextlib.ExitStack() as es:
        p.emit()
    nc.reset()
    return nc


_FUSED_IN = ["ones_f", "tstrict", "iota8", "ident_f", "rw_ext",
             "rope0_cos", "rope0_sin", "wqT", "wkaT", "wkbT", "wvaT", "wvbT",
             "woT", "moe_w1", "moe_w2", "moe_b1", "b2ext", "sw1", "sw2", "sb1",
             "wq1T", "wk1T", "wv1T", "bq1", "gamma1", "beta1", "bk1row",
             "bv1row", "rope1_cs", "rope1_ss", "rope1_ci", "rope1_si",
             "lmask", "lmask0", "Wqkv1", "Wo1", "W11", "W21", "qs1", "ks1",
             "lng1", "lnb1", "b1a1", "b1g1", "b21c"]


_PHASE_KEYS = {
    "A": (["ones_f", "tstrict", "iota8", "ident_f", "rw_ext"],
          ["xg", "permT", "permT2"]),
    "B": (["ident_f", "ones_f", "rope0_cos", "rope0_sin", "wqT", "wkaT",
           "wkbT", "wvaT", "wvbT", "woT"], ["xg"]),
    "C": (["ident_f", "moe_w1", "moe_w2", "moe_b1", "b2ext", "sw1", "sw2",
           "sb1"], []),
    "D": (["ones_f", "wq1T", "wk1T", "wv1T", "bq1", "gamma1", "beta1",
           "bk1row", "bv1row"], []),
    "E": (["ident_f", "ones_f", "rope1_cs", "rope1_ss", "rope1_ci", "rope1_si",
           "lmask", "lmask0", "Wqkv1", "Wo1", "W11", "W21", "qs1", "ks1",
           "lng1", "lnb1", "b1a1", "b1g1", "b21c"], []),
}

_HANDOFF = {
    "B": ["xn_d"],
    "C": ["Wg_d", "WT9_d", "xn_d", "x1a_d"],
    "D": ["x1_d"],
    "E": ["murow1_d", "arow1_d", "x1_d", "x1b_d"],
}

_BUILDERS = None
_FUSED_NC = None
UNFUSED = bool(int(_os.environ.get("UNFUSED", "0")))


def kernel(**inputs):
    global _BUILDERS, _FUSED_NC
    shared, cores = _prep(inputs)
    if not UNFUSED:
        if _FUSED_NC is None:
            _FUSED_NC = build_fused()
        in_maps = []
        for c in range(NCORES):
            m = {k: shared[k] for k in _FUSED_IN}
            m.update({k: cores[c][k] for k in ("xg", "permT", "permT2")})
            in_maps.append(m)
        res = run_bass_kernel_spmd(_FUSED_NC, in_maps,
                                   core_ids=list(range(NCORES)))
        return np.stack([res.results[c]["out_d"] for c in range(NCORES)])
    if _BUILDERS is None:
        _BUILDERS = {
            "A": build_phaseA(), "B": build_phaseB(), "C": build_phaseC(),
            "D": build_phaseD(), "E": build_phaseE(),
        }
    state = [dict() for _ in range(NCORES)]
    for ph in ("A", "B", "C", "D", "E"):
        skeys, ckeys = _PHASE_KEYS[ph]
        in_maps = []
        for c in range(NCORES):
            m = {k: shared[k] for k in skeys}
            m.update({k: cores[c][k] for k in ckeys})
            for k in _HANDOFF.get(ph, ()):
                m[k] = state[c][k]
            in_maps.append(m)
        res = run_bass_kernel_spmd(_BUILDERS[ph], in_maps,
                                   core_ids=list(range(NCORES)))
        for c in range(NCORES):
            state[c].update(res.results[c])
    out = np.stack([state[c]["out_d"] for c in range(NCORES)])
    return out


# version 06 (final)
# speedup vs baseline: 1.6213x; 1.1003x over previous
import math
import os as _os
import contextlib
B_TRUNC = int(_os.environ.get("B_TRUNC", "0"))
B_DEBUG = int(_os.environ.get("B_DEBUG", "0"))
import numpy as np
import ml_dtypes
import concourse.bass as bass
import concourse.mybir as mybir
from concourse.bass_utils import run_bass_kernel_spmd
from concourse.alu_op_type import AluOpType as ALU

F32 = mybir.dt.float32
BF16 = mybir.dt.bfloat16
AF = mybir.ActivationFunctionType
AX = mybir.AxisListType
BF16NP = ml_dtypes.bfloat16

NCORES = 8
D = 1024
T = 2048
NT = T // 128
ND = D // 128
NTOK = NCORES * T
NTILES = NTOK // 128
CAPACITY = 5120
NEG_BIG = -1.0e30
FLT_MAX = 3.4028235e38

def _host_consts():
    c = {}
    c["ident_bf"] = np.eye(128, dtype=BF16NP)
    c["ident_f"] = np.eye(128, dtype=np.float32)
    ones = np.ones((128, 128), dtype=np.float32)
    c["ones_f"] = ones
    c["ones_bf"] = ones.astype(BF16NP)
    tt = np.zeros((128, 128), dtype=np.float32)
    for m_ in range(128):
        tt[:m_, m_] = 1.0
    c["tstrict"] = tt
    c["iota8"] = np.arange(8, dtype=np.float32).reshape(1, 8)
    hd = 64
    inv = 1.0 / 10000.0 ** (np.arange(0, hd, 2, dtype=np.float32) / np.float32(hd))
    ang = np.arange(16, dtype=np.float32)[:, None] * inv[None, :8]
    ch = np.cos(ang.astype(np.float64)).astype(np.float32).reshape(1, 128)
    sh_ = np.sin(ang.astype(np.float64)).astype(np.float32).reshape(1, 128)
    c["rope0_cos"] = np.tile(ch, (128, 1))
    c["rope0_sin"] = np.tile(sh_, (128, 1))
    half = np.arange(0, 64, 2, dtype=np.float32)
    inv_freq = 1.0 / 10000.0 ** (half / 64.0)
    t = np.arange(2048, dtype=np.float32)
    freqs = t[:, None] * inv_freq[None, :]
    freqs = np.concatenate([freqs, freqs], -1)
    base_scale = (half + 0.4 * 64) / (1.4 * 64)
    power = (t - 1024) / 128.0
    scale = base_scale[None, :] ** power[:, None]
    scale = np.concatenate([scale, scale], -1).astype(np.float32)
    cos = np.cos(freqs.astype(np.float64)).astype(np.float32)
    sin = np.sin(freqs.astype(np.float64)).astype(np.float32)
    c["rope1_cs"] = (cos * scale).astype(np.float32)
    c["rope1_ss"] = (sin * scale).astype(np.float32)
    invs = (1.0 / scale).astype(np.float32)
    c["rope1_ci"] = (cos * invs).astype(np.float32)
    c["rope1_si"] = (sin * invs).astype(np.float32)
    i = np.arange(256)[:, None]
    jj = np.arange(512)[None, :]
    allowed = (jj >= i) & (jj <= i + 256)
    m = np.where(allowed, 0.0, -1e30).astype(np.float32)
    c["lmask"] = m
    m0 = m.copy()
    m0[:, :256] = -1e30
    c["lmask0"] = m0
    return c


def _prep(ins):
    f32 = np.float32
    sh = dict(_host_consts())
    x = np.ascontiguousarray(np.asarray(ins["L0_attn_x"], dtype=f32))
    xT = np.ascontiguousarray(x.reshape(NTOK, D).T)
    g = lambda k: np.asarray(ins[k], dtype=f32)
    sh["wqT"] = np.ascontiguousarray(g("L0_attn_wq").T)
    sh["wkaT"] = np.ascontiguousarray(g("L0_attn_wk_a").T)
    sh["wkbT"] = np.ascontiguousarray(g("L0_attn_wk_b").T)
    sh["wvaT"] = np.ascontiguousarray(g("L0_attn_wv_a").T)
    sh["wvbT"] = np.ascontiguousarray(g("L0_attn_wv_b").T)
    sh["woT"] = np.ascontiguousarray(g("L0_attn_wo").T)
    rw = g("L0_ffn_router_w")
    sh["rw_ext"] = np.ascontiguousarray(
        np.concatenate([rw, np.ones((D, 1), f32)], axis=1))
    sh["moe_w1"] = np.ascontiguousarray(g("L0_ffn_w1"))
    sh["moe_w2"] = np.ascontiguousarray(g("L0_ffn_w2"))
    sh["moe_b1"] = np.ascontiguousarray(
        g("L0_ffn_b1").reshape(8, 32, 128).transpose(0, 2, 1))
    b2ext = np.zeros((33, D), f32)
    b2ext[0:8] = g("L0_ffn_b2")
    b2ext[32] = g("L0_ffn_sb2")
    sh["b2ext"] = b2ext
    sh["sw1"] = np.ascontiguousarray(g("L0_ffn_sw1"))
    sh["sw2"] = np.ascontiguousarray(g("L0_ffn_sw2"))
    sh["sb1"] = np.ascontiguousarray(
        g("L0_ffn_sb1").reshape(32, 128).T)
    sh["wq1T"] = np.ascontiguousarray(g("L1_attn_Wq").T)
    sh["wk1T"] = np.ascontiguousarray(g("L1_attn_Wk").T)
    sh["wv1T"] = np.ascontiguousarray(g("L1_attn_Wv").T)
    sh["bq1"] = np.ascontiguousarray(g("L1_attn_bq").reshape(8, 128).T)
    sh["bk1row"] = g("L1_attn_bk").reshape(1, D).copy()
    sh["bv1row"] = g("L1_attn_bv").reshape(1, D).copy()
    sh["gamma1"] = np.ascontiguousarray(g("L1_attn_gamma").reshape(8, 128).T)
    sh["beta1"] = np.ascontiguousarray(g("L1_attn_beta").reshape(8, 128).T)
    sh["Wqkv1"] = np.ascontiguousarray(g("L1_ffn_Wqkv"))
    sh["Wo1"] = np.ascontiguousarray(g("L1_ffn_Wo"))
    sh["W11"] = np.ascontiguousarray(g("L1_ffn_W1"))
    sh["W21"] = np.ascontiguousarray(g("L1_ffn_W2"))
    b1 = g("L1_ffn_b1")
    b1a = np.zeros((6, 2816), f32); b1a[:, :2730] = b1[:, :2730]
    b1g = np.zeros((6, 2816), f32); b1g[:, :2730] = b1[:, 2730:]
    sh["b1a1"] = np.ascontiguousarray(b1a.reshape(6, 22, 128).transpose(0, 2, 1))
    sh["b1g1"] = np.ascontiguousarray(b1g.reshape(6, 22, 128).transpose(0, 2, 1))
    sh["b21row"] = np.ascontiguousarray(g("L1_ffn_b2"))
    sh["b21c"] = np.ascontiguousarray(g("L1_ffn_b2").reshape(6, 8, 128).transpose(0, 2, 1))
    sh["qs1"] = np.ascontiguousarray(g("L1_ffn_q_scale"))
    sh["ks1"] = np.ascontiguousarray(g("L1_ffn_k_scale"))
    sh["lng1"] = np.ascontiguousarray(g("L1_ffn_ln_g").reshape(6, 8, 128).transpose(0, 2, 1))
    sh["lnb1"] = np.ascontiguousarray(g("L1_ffn_ln_b").reshape(6, 8, 128).transpose(0, 2, 1))
    cores = []
    for c in range(NCORES):
        my = list(range(c * NT, (c + 1) * NT))
        rest = [t_ for t_ in range(NTILES) if t_ not in my]
        order = my + rest
        xg = xT.reshape(D, NTILES, 128)[:, order, :].reshape(D, NTOK)
        P = np.zeros((128, 128), f32)
        for tl, tg in enumerate(order):
            P[tl, tg] = 1.0
        dd = {"xg": np.ascontiguousarray(xg),
              "permT": P,
              "permT2": np.ascontiguousarray(P.T)}
        cores.append(dd)
    return sh, cores


ENGINES = ("tensor", "vector", "scalar", "gpsimd", "sync")
SAME_ENGINE_WAITS = True


class _Op:
    __slots__ = ("eng", "fn", "reads", "writes", "dma", "idx", "needs_inc",
                 "inc_val", "waits", "tag")

    def __init__(self, eng, fn, reads, writes, dma, idx, tag):
        self.eng = eng; self.fn = fn; self.reads = reads; self.writes = writes
        self.dma = dma; self.idx = idx; self.needs_inc = dma
        self.inc_val = None; self.waits = []; self.tag = tag


class Prog:

    def __init__(self, nc):
        self.nc = nc
        self.ops = {e: [] for e in ENGINES}
        self.all_ops = []
        self.last_write = {}
        self.readers = {}
        self.last_dma = {}
        self.psum_keys = set()
        self._pending_barrier = {}

    def mark_psum(self, *keys):
        self.psum_keys.update(keys)

    def op(self, eng, fn, reads=(), writes=(), dma=False, tag=""):
        o = _Op(eng, fn, list(reads), list(writes), dma, len(self.ops[eng]), tag)
        self.ops[eng].append(o)
        self.all_ops.append(o)
        deps = set()
        pb = self._pending_barrier.get(eng)
        if pb:
            for b_ in pb:
                if b_ is not o:
                    deps.add(b_)
            self._pending_barrier[eng] = None
        for r in o.reads:
            if r in self.psum_keys:
                for rd in self.readers.get(r, ()):
                    if rd.eng != eng and rd is not o:
                        deps.add(rd)
        if dma:
            prev = self.last_dma.get(eng)
            if prev is not None:
                deps.add(prev)
            self.last_dma[eng] = o
        for r in o.reads:
            p_ = self.last_write.get(r)
            if p_ is not None and p_ is not o:
                deps.add(p_)
            self.readers.setdefault(r, []).append(o)
        for w in o.writes:
            p_ = self.last_write.get(w)
            if p_ is not None and p_ is not o:
                deps.add(p_)
            for rd in self.readers.get(w, ()):
                if rd is not o:
                    deps.add(rd)
            self.last_write[w] = o
            self.readers[w] = []
        o.waits = list(deps)
        return o

    def barrier(self):
        snapshot = [self.ops[e][-1] for e in ENGINES if self.ops[e]]
        self._pending_barrier = {e: list(snapshot) for e in ENGINES}

    def emit(self):
        nc = self.nc
        for o in self.all_ops:
            for p_ in o.waits:
                p_.needs_inc = True
        for e in ENGINES:
            comp = [o for o in self.ops[e] if not o.dma]
            if comp:
                comp[-1].needs_inc = True
        for e in ENGINES:
            n_c = 0; n_d = 0
            for o in self.ops[e]:
                if o.dma:
                    n_d += 1
                    o.inc_val = 16 * n_d
                elif o.needs_inc:
                    n_c += 1
                    o.inc_val = n_c
        sems = {}
        for e in ENGINES:
            sems[("c", e)] = nc.semaphore(f"sem_c_{e}").__enter__()
            if any(o.dma for o in self.ops[e]):
                sems[("d", e)] = nc.semaphore(f"sem_d_{e}").__enter__()
        self._sems = sems
        finals = {}
        for e in ENGINES:
            cvals = [o.inc_val for o in self.ops[e] if (not o.dma) and o.needs_inc]
            dvals = [o.inc_val for o in self.ops[e] if o.dma]
            finals[("c", e)] = cvals[-1] if cvals else 0
            if dvals:
                finals[("d", e)] = dvals[-1]
        hb = {}

        with nc.Block() as block:
            for e in ENGINES:
                dec = getattr(block, e)

                def body(eng_obj, e=e):
                    for o in self.ops[e]:
                        for p_ in o.waits:
                            if p_.eng == e and not p_.dma:
                                if e == "tensor":
                                    continue
                                if not SAME_ENGINE_WAITS:
                                    continue
                            key = ("d", p_.eng) if p_.dma else ("c", p_.eng)
                            assert p_.inc_val is not None, (o.tag, p_.tag)
                            hw = hb.get((e, key), 0)
                            if p_.inc_val > hw:
                                eng_obj.wait_ge(sems[key], p_.inc_val)
                                hb[(e, key)] = p_.inc_val
                        inst = o.fn(eng_obj)
                        if o.needs_inc:
                            key = ("d", e) if o.dma else ("c", e)
                            assert inst is not None, (e, o.tag)
                            inst.then_inc(sems[key], 16 if o.dma else 1)
                    if ("d", e) in sems:
                        eng_obj.wait_ge(sems[("d", e)], finals[("d", e)])
                dec(body)
        return sems


def phase_ln_gating(p, d, es):
    nc = p.nc
    sb = lambda name, shape, dt: es.enter_context(nc.sbuf_tensor(name, shape, dt))
    ps = lambda name, shape, dt: es.enter_context(nc.psum_tensor(name, shape, dt))

    ones_f = sb("g_ones_f", [128, 128], F32)
    ones_bf = sb("g_ones_bf", [128, 128], BF16)
    tstrict = sb("g_tstrict", [128, 128], F32)
    iota8 = sb("g_iota8", [1, 8], F32)
    ident_f = sb("g_ident_f", [128, 128], F32)
    permT = sb("g_permT", [128, 128], F32)
    permT2 = sb("g_permT2", [128, 128], F32)
    rw_f = sb("g_rw_f", [128, 8 * 9], F32)
    for nm, t_ in (("ones_f", ones_f), ("tstrict", tstrict),
                   ("iota8", iota8), ("ident_f", ident_f), ("permT", permT),
                   ("permT2", permT2)):
        p.op("sync", lambda s, t_=t_, nm=nm: s.dma_start(t_[:, :], d[nm][:, :]),
             writes=[t_.name], dma=True, tag="ld_" + nm)
    p.op("gpsimd", lambda g_: g_.dma_start(ones_bf[:, :], d["ones_f"][:, :]),
         writes=[ones_bf.name], dma=True, tag="ld_ones_bf")
    p.op("sync", lambda s: s.dma_start(
        rw_f[:, :].rearrange("p (k n) -> p k n", n=9),
        d["rw_ext"][:, :].rearrange("(k p) n -> p k n", p=128)),
        writes=[rw_f.name], dma=True, tag="ld_rw")

    xgb = [sb(f"g_xgb{i}", [128, 8 * 512], F32) for i in range(2)]
    xsq = [sb(f"g_xsq{i}", [128, 8 * 512], F32) for i in range(2)]
    rs_ps = [ps(f"g_rs{i}", [128, 512], F32) for i in range(2)]
    p.mark_psum(rs_ps[0].name, rs_ps[1].name)
    RS = sb("g_RS", [128, 9 * 128], F32)
    XSQS = sb("g_XSQS", [128, 128], F32)
    RS9 = RS[:, :].rearrange("p (t n) -> p t n", n=9)

    nblk = NTOK // 512
    for b in range(nblk):
        buf = xgb[b % 2]; sq = xsq[b % 2]; rp = rs_ps[b % 2]
        p.op("sync" if b % 2 == 0 else "gpsimd", lambda g_, b=b, buf=buf: g_.dma_start(
            buf[:, :].rearrange("p (k n) -> p k n", n=512),
            d["xg"][:, b * 512:(b + 1) * 512].rearrange("(k p) n -> p k n", p=128)),
            writes=[buf.name], dma=True, tag=f"ld_xg{b}")
        p.op("scalar", lambda s, buf=buf, sq=sq: s.activation(
            sq[:, :], buf[:, :], AF.Square),
            reads=[buf.name], writes=[sq.name], tag=f"sq{b}")

        def mm_blk(t_eng, buf=buf, sq=sq, rp=rp):
            for i in range(4):
                for kt in range(8):
                    i1 = t_eng.matmul(
                        rp[:, i * 9:i * 9 + 9],
                        buf[:, kt * 512 + i * 128: kt * 512 + (i + 1) * 128],
                        rw_f[:, kt * 9:(kt + 1) * 9],
                        start=(kt == 0), stop=(kt == 7))
                for kt in range(8):
                    i1 = t_eng.matmul(
                        rp[:, 36 + i: 37 + i],
                        sq[:, kt * 512 + i * 128: kt * 512 + (i + 1) * 128],
                        ones_f[:, 0:1],
                        start=(kt == 0), stop=(kt == 7))
            return i1
        p.op("tensor", mm_blk, reads=[buf.name, sq.name, rw_f.name, ones_f.name],
             writes=[rp.name], tag=f"mm_blk{b}")
        p.op("vector", lambda v, b=b, rp=rp: v.tensor_copy(
            RS[:, b * 36:(b + 1) * 36], rp[:, 0:36]),
            reads=[rp.name], writes=["RSk"], tag=f"ev_rs{b}")
        p.op("vector", lambda v, b=b, rp=rp: v.tensor_copy(
            XSQS[:, b * 4:(b + 1) * 4], rp[:, 36:40]),
            reads=[rp.name], writes=[XSQS.name], tag=f"ev_sq{b}")

    MU = sb("g_MU", [128, 128], F32)
    RSTD = sb("g_RSTD", [128, 128], F32)
    ZL = sb("g_ZL", [128, 1024], F32)
    scr = sb("g_scr", [128, 1024], F32)
    scr2 = sb("g_scr2", [128, 1024], F32)
    sml = sb("g_sml", [128, 128 * 12], F32)

    def S(i):
        return sml[:, i * 128:(i + 1) * 128]

    def Sk(i):
        return f"sml{i}"
    psA = ps("g_psA", [128, 512], F32)
    psB = ps("g_psB", [128, 512], F32)
    psC = ps("g_psC", [128, 512], F32)
    psD = ps("g_psD", [128, 512], F32)
    p.mark_psum(psA.name, psB.name, psC.name, psD.name)

    p.op("vector", lambda v: v.tensor_scalar(
        MU[:, :].unsqueeze(2), RS9[:, :, 8:9], -1.0 / 1024.0, None, ALU.mult),
        reads=["RSk"], writes=[MU.name], tag="mu")
    p.op("vector", lambda v: v.tensor_tensor(S(0), MU[:, :], MU[:, :], ALU.mult),
         reads=[MU.name], writes=[Sk(0)], tag="mu2")
    p.op("vector", lambda v: v.tensor_scalar(S(1), XSQS[:, :], 1.0 / 1024.0, None, ALU.mult),
         reads=[XSQS.name], writes=[Sk(1)], tag="ex2")
    p.op("vector", lambda v: v.tensor_tensor(S(2), S(1), S(0), ALU.subtract),
         reads=[Sk(0), Sk(1)], writes=[Sk(2)], tag="var")
    epsc = sb("g_epsc", [128, 1], F32)
    p.op("vector", lambda v: v.memset(epsc[:, :], 1e-5), writes=[epsc.name], tag="epsc")
    p.op("scalar", lambda s: s.activation(S(3), S(2), AF.Ln, bias=epsc[:, :]),
         reads=[Sk(2), epsc.name], writes=[Sk(3)], tag="lnvar")
    p.op("scalar", lambda s: s.activation(RSTD[:, :], S(3), AF.Exp, scale=-0.5),
         reads=[Sk(3)], writes=[RSTD.name], tag="rstd")

    CSb = sb("g_CSb", [128, 9], F32)
    cs_row = sb("g_csrow", [1, 9], F32)

    def mm_cs(t):
        for kt in range(8):
            i1 = t.matmul(psA[0:1, 0:9], ones_f[:, 0:1], rw_f[:, kt * 9:(kt + 1) * 9],
                          start=(kt == 0), stop=(kt == 7))
        return i1
    p.op("tensor", mm_cs, reads=[ones_f.name, rw_f.name], writes=[psA.name], tag="cs")
    p.op("vector", lambda v: v.tensor_copy(cs_row[:, :], psA[0:1, 0:9]),
         reads=[psA.name], writes=[cs_row.name], tag="ev_cs")
    p.op("tensor", lambda t: t.matmul(psB[:, 0:9], ones_f[0:1, :], cs_row[:, :],
                                      start=True, stop=True),
         reads=[ones_f.name, cs_row.name], writes=[psB.name], tag="csb")
    p.op("vector", lambda v: v.tensor_copy(CSb[:, :], psB[:, 0:9]),
         reads=[psB.name], writes=[CSb.name], tag="ev_csb")

    for e in range(8):
        p.op("vector", lambda v, e=e: v.scalar_tensor_tensor(
            ZL[:, e * 128:(e + 1) * 128].unsqueeze(2), MU[:, :].unsqueeze(2),
            CSb[:, e:e + 1], RS9[:, :, e:e + 1], ALU.mult, ALU.add),
            reads=[MU.name, CSb.name, "RSk"], writes=[ZL.name], tag=f"zl{e}")

    IOTA8M = sb("g_iota8m", [128, 8], F32)
    IOm8 = sb("g_iom8", [128, 8], F32)
    p.op("tensor", lambda t: t.matmul(psA[:, 0:8], ones_f[0:1, :], iota8[:, :],
                                      start=True, stop=True),
         reads=[ones_f.name, iota8.name], writes=[psA.name], tag="iom")
    p.op("vector", lambda v: v.tensor_copy(IOTA8M[:, :], psA[:, 0:8]),
         reads=[psA.name], writes=[IOTA8M.name], tag="ev_iom")
    p.op("vector", lambda v: v.tensor_scalar(IOm8[:, :], IOTA8M[:, :], -8.0, None, ALU.add),
         reads=[IOTA8M.name], writes=[IOm8.name], tag="iom8")

    def v3(ap2):
        return ap2.rearrange("p (e t) -> p t e", t=128)

    def bc(ap128):
        return ap128.unsqueeze(2).broadcast_to((128, 128, 8))
    iob8 = IOm8[:, :].unsqueeze(1).broadcast_to((128, 128, 8))

    MSK1 = sb("g_msk1", [128, 1024], F32)
    MSK2 = sb("g_msk2", [128, 1024], F32)
    M1 = S(0); M2 = S(1)

    p.op("vector", lambda v: v.tensor_reduce(M1, v3(ZL[:, :]), AX.X, ALU.max),
         reads=[ZL.name], writes=[Sk(0)], tag="m1")
    p.op("vector", lambda v: v.tensor_tensor(v3(scr2[:, :]), v3(ZL[:, :]), bc(M1), ALU.is_ge),
         reads=[ZL.name, Sk(0)], writes=[scr2.name], tag="eq1")
    p.op("vector", lambda v: v.tensor_tensor(v3(scr[:, :]), v3(scr2[:, :]), iob8, ALU.mult),
         reads=[scr2.name], writes=[scr.name], tag="c1")
    IDX1 = S(2)
    p.op("vector", lambda v: v.tensor_reduce(IDX1, v3(scr[:, :]), AX.X, ALU.min),
         reads=[scr.name], writes=[Sk(2)], tag="idx1")
    p.op("vector", lambda v: v.tensor_tensor(v3(MSK1[:, :]), iob8, bc(IDX1), ALU.is_equal),
         reads=[Sk(2)], writes=[MSK1.name], tag="msk1")

    p.op("vector", lambda v: v.scalar_tensor_tensor(
        v3(scr[:, :]), v3(MSK1[:, :]), NEG_BIG, v3(ZL[:, :]), ALU.mult, ALU.add),
        reads=[MSK1.name, ZL.name], writes=[scr.name], tag="zl2")
    p.op("vector", lambda v: v.tensor_reduce(M2, v3(scr[:, :]), AX.X, ALU.max),
         reads=[scr.name], writes=[Sk(1)], tag="m2")
    p.op("vector", lambda v: v.tensor_tensor(v3(scr2[:, :]), v3(scr[:, :]), bc(M2), ALU.is_ge),
         reads=[scr.name, Sk(1)], writes=[scr2.name], tag="eq2")
    p.op("vector", lambda v: v.tensor_tensor(v3(scr[:, :]), v3(scr2[:, :]), iob8, ALU.mult),
         reads=[scr2.name], writes=[scr.name], tag="c2")
    IDX2 = S(3)
    p.op("vector", lambda v: v.tensor_reduce(IDX2, v3(scr[:, :]), AX.X, ALU.min),
         reads=[scr.name], writes=[Sk(3)], tag="idx2")
    p.op("vector", lambda v: v.tensor_tensor(v3(MSK2[:, :]), iob8, bc(IDX2), ALU.is_equal),
         reads=[Sk(3)], writes=[MSK2.name], tag="msk2")

    L1v = S(4); G2P = S(5)
    p.op("vector", lambda v: v.tensor_tensor(L1v, M1, RSTD[:, :], ALU.mult),
         reads=[Sk(0), RSTD.name], writes=[Sk(4)], tag="l1")
    p.op("vector", lambda v: v.tensor_tensor(S(6), M2, M1, ALU.subtract),
         reads=[Sk(0), Sk(1)], writes=[Sk(6)], tag="dm")
    p.op("vector", lambda v: v.tensor_tensor(S(7), S(6), RSTD[:, :], ALU.mult),
         reads=[Sk(6), RSTD.name], writes=[Sk(7)], tag="dm2")
    p.op("scalar", lambda s: s.activation(G2P, S(7), AF.Exp),
         reads=[Sk(7)], writes=[Sk(5)], tag="g2p")
    p.op("vector", lambda v: v.tensor_tensor(v3(scr[:, :]), v3(ZL[:, :]), bc(RSTD[:, :]), ALU.mult),
         reads=[ZL.name, RSTD.name], writes=[scr.name], tag="lz")
    p.op("vector", lambda v: v.tensor_tensor(v3(scr[:, :]), v3(scr[:, :]), bc(L1v), ALU.subtract),
         reads=[scr.name, Sk(4)], writes=[scr.name], tag="lzs")
    p.op("scalar", lambda s: s.activation(scr2[:, :], scr[:, :], AF.Exp),
         reads=[scr.name], writes=[scr2.name], tag="exps")
    Ssum = S(6)
    p.op("vector", lambda v: v.tensor_reduce(Ssum, v3(scr2[:, :]), AX.X, ALU.add),
         reads=[scr2.name], writes=[Sk(6)], tag="ssum")

    Stp = sb("g_stp", [128, 8], F32)
    CAR = sb("g_car", [128, 8], F32)
    crow = sb("g_crow", [1, 1024], F32)
    c1p = sb("g_c1p", [1, 8], F32)
    POSK = sb("g_keep", [128, 1024], F32)

    def cumsum_pass(MSKt, mskkey, with_offset, tagp):
        def mm_stp(t, MSKt=MSKt):
            for e in range(8):
                i1 = t.matmul(psC[:, e:e + 1], MSKt[:, e * 128:(e + 1) * 128],
                              ones_f[:, 0:1], start=(e == 0), stop=(e == 7))
            return i1
        p.op("tensor", mm_stp, reads=[mskkey, ones_f.name], writes=[psC.name],
             tag=tagp + "stp")
        p.op("vector", lambda v: v.tensor_copy(Stp[:, :], psC[:, 0:8]),
             reads=[psC.name], writes=[Stp.name], tag=tagp + "evstp")
        p.op("tensor", lambda t: t.matmul(psD[:, 0:8], permT[:, :], Stp[:, :],
                                          start=True, stop=True),
             reads=[permT.name, Stp.name], writes=[psD.name], tag=tagp + "perm")
        p.op("vector", lambda v: v.tensor_copy(Stp[:, :], psD[:, 0:8]),
             reads=[psD.name], writes=[Stp.name], tag=tagp + "evperm")
        if with_offset:
            def mm_car(t):
                t.matmul(psC[:, 0:8], tstrict[:, :], Stp[:, :], start=True, stop=False)
                return t.matmul(psC[:, 0:8], ones_f[0:1, :], c1p[:, :],
                                start=False, stop=True)
            rd = [tstrict.name, Stp.name, ones_f.name, c1p.name]
        else:
            def mm_car(t):
                return t.matmul(psC[:, 0:8], tstrict[:, :], Stp[:, :],
                                start=True, stop=True)
            rd = [tstrict.name, Stp.name]
        p.op("tensor", mm_car, reads=rd, writes=[psC.name], tag=tagp + "car")
        p.op("vector", lambda v: v.tensor_copy(CAR[:, :], psC[:, 0:8]),
             reads=[psC.name], writes=[CAR.name], tag=tagp + "evcar")
        p.op("tensor", lambda t: t.matmul(psD[:, 0:8], permT2[:, :], CAR[:, :],
                                          start=True, stop=True),
             reads=[permT2.name, CAR.name], writes=[psD.name], tag=tagp + "carl")
        p.op("vector", lambda v: v.tensor_copy(CAR[:, :], psD[:, 0:8]),
             reads=[psD.name], writes=[CAR.name], tag=tagp + "evcarl")
        p.op("tensor", lambda t: t.transpose(psD[0:8, 0:128], CAR[:, :], ident_f[:, :]),
             reads=[CAR.name, ident_f.name], writes=[psD.name], tag=tagp + "trc")
        CAR8 = sb(f"g_car8{tagp}", [8, 128], F32)
        p.op("vector", lambda v, CAR8=CAR8: v.tensor_copy(CAR8[:, :], psD[0:8, 0:128]),
             reads=[psD.name], writes=[CAR8.name], tag=tagp + "evtrc")
        p.op("sync", lambda s, CAR8=CAR8: s.dma_start(d["car_d"][:, :], CAR8[:, :]),
             reads=[CAR8.name], writes=["car_d"], dma=True, tag=tagp + "st_car")
        p.op("sync", lambda s: s.dma_start(
            crow[:, :].rearrange("o (e t) -> o e t", t=128),
            d["car_d"][:, :].unsqueeze(0)),
            reads=["car_d"], writes=[crow.name], dma=True, tag=tagp + "ld_crow")
        def mm_posA(t, MSKt=MSKt):
            t.matmul(psD[:, 0:512], tstrict[:, :], MSKt[:, 0:512],
                     start=True, stop=False)
            return t.matmul(psD[:, 0:512], ones_f[0:1, :], crow[:, 0:512],
                            start=False, stop=True)
        def mm_posB(t, MSKt=MSKt):
            t.matmul(psC[:, 0:512], tstrict[:, :], MSKt[:, 512:1024],
                     start=True, stop=False)
            return t.matmul(psC[:, 0:512], ones_f[0:1, :], crow[:, 512:1024],
                            start=False, stop=True)
        p.op("tensor", mm_posA, reads=[mskkey, tstrict.name, ones_f.name, crow.name],
             writes=[psD.name], tag=tagp + "posA")
        p.op("tensor", mm_posB, reads=[mskkey, tstrict.name, ones_f.name, crow.name],
             writes=[psC.name], tag=tagp + "posB")
        p.op("vector", lambda v: v.tensor_scalar(
            POSK[:, 0:512], psD[:, 0:512], float(CAPACITY), None, ALU.is_lt),
            reads=[psD.name], writes=[POSK.name], tag=tagp + "keepA")
        p.op("vector", lambda v: v.tensor_scalar(
            POSK[:, 512:1024], psC[:, 0:512], float(CAPACITY), None, ALU.is_lt),
            reads=[psC.name], writes=[POSK.name], tag=tagp + "keepB")
        p.op("vector", lambda v, MSKt=MSKt: v.tensor_tensor(
            MSKt[:, :], MSKt[:, :], POSK[:, :], ALU.mult),
            reads=[POSK.name, mskkey], writes=[mskkey], tag=tagp + "maskp")

    cumsum_pass(MSK1[:, :], MSK1.name, False, "c1_")

    def mm_c1p(t):
        for e in range(8):
            i1 = t.matmul(psC[:, e:e + 1], MSK1[:, e * 128:(e + 1) * 128],
                          ones_f[:, 0:1], start=(e == 0), stop=(e == 7))
        return i1
    p.op("tensor", mm_c1p, reads=[MSK1.name, ones_f.name], writes=[psC.name], tag="c1p_stp")
    p.op("vector", lambda v: v.tensor_copy(Stp[:, :], psC[:, 0:8]),
         reads=[psC.name], writes=[Stp.name], tag="c1p_ev")
    p.op("tensor", lambda t: t.matmul(psD[0:1, 0:8], ones_f[:, 0:1], Stp[:, :],
                                      start=True, stop=True),
         reads=[ones_f.name, Stp.name], writes=[psD.name], tag="c1p_tot")
    p.op("vector", lambda v: v.tensor_copy(c1p[:, :], psD[0:1, 0:8]),
         reads=[psD.name], writes=[c1p.name], tag="c1p_evt")
    cumsum_pass(MSK2[:, :], MSK2.name, True, "c2_")

    K1 = S(8); K2 = S(9)
    p.op("vector", lambda v: v.tensor_reduce(K1, v3(MSK1[:, :]), AX.X, ALU.add),
         reads=[MSK1.name], writes=[Sk(8)], tag="k1")
    p.op("vector", lambda v: v.tensor_reduce(K2, v3(MSK2[:, :]), AX.X, ALU.add),
         reads=[MSK2.name], writes=[Sk(9)], tag="k2")
    RS_ = S(10)
    p.op("vector", lambda v: v.reciprocal(RS_, Ssum),
         reads=[Sk(6)], writes=[Sk(10)], tag="rS")
    G1 = S(4); G2 = S(5)
    p.op("vector", lambda v: v.tensor_tensor(G1, K1, RS_, ALU.mult),
         reads=[Sk(8), Sk(10)], writes=[Sk(4)], tag="g1v")
    p.op("vector", lambda v: v.tensor_tensor(G2, G2P, RS_, ALU.mult),
         reads=[Sk(5), Sk(10)], writes=[Sk(5)], tag="g2v0")
    p.op("vector", lambda v: v.tensor_tensor(G2, G2, K2, ALU.mult),
         reads=[Sk(5), Sk(9)], writes=[Sk(5)], tag="g2v")
    DEN = S(11)
    p.op("vector", lambda v: v.tensor_tensor(DEN, G1, G2, ALU.add),
         reads=[Sk(4), Sk(5)], writes=[Sk(11)], tag="den0")
    p.op("vector", lambda v: v.tensor_scalar(DEN, DEN, 1.1920929e-07, None, ALU.max),
         reads=[Sk(11)], writes=[Sk(11)], tag="den")
    p.op("vector", lambda v: v.reciprocal(S(7), DEN),
         reads=[Sk(11)], writes=[Sk(7)], tag="rden")
    p.op("vector", lambda v: v.tensor_tensor(G1, G1, S(7), ALU.mult),
         reads=[Sk(4), Sk(7)], writes=[Sk(4)], tag="gh1")
    p.op("vector", lambda v: v.tensor_tensor(G2, G2, S(7), ALU.mult),
         reads=[Sk(5), Sk(7)], writes=[Sk(5)], tag="gh2")
    Wg = sb("g_Wg", [128, 1024], F32)
    p.op("vector", lambda v: v.tensor_tensor(v3(Wg[:, :]), v3(MSK1[:, :]), bc(G1), ALU.mult),
         reads=[MSK1.name, Sk(4)], writes=[Wg.name], tag="wg1")
    p.op("vector", lambda v: v.tensor_tensor(v3(scr[:, :]), v3(MSK2[:, :]), bc(G2), ALU.mult),
         reads=[MSK2.name, Sk(5)], writes=[scr.name], tag="wg2")
    p.op("vector", lambda v: v.tensor_tensor(Wg[:, :], Wg[:, :], scr[:, :], ALU.add),
         reads=[scr.name, Wg.name], writes=[Wg.name], tag="wg")
    p.op("sync", lambda s: s.dma_start(d["Wg_d"][:, :], Wg[:, :]),
         reads=[Wg.name], writes=["Wg_d"], dma=True, tag="st_wg")

    WT9 = sb("g_WT9", [33, 2048], F32)
    Wg3 = Wg[:, :].rearrange("p (e t) -> p t e", t=128)
    p.op("vector", lambda v: v.memset(WT9[:, :], 0.0), writes=[WT9.name], tag="wt9z")
    p.op("vector", lambda v: v.memset(WT9[32:33, :], 0.1),
         reads=[WT9.name], writes=["WT9s"], tag="wt9s")
    for g_ in range(4):
        def mm_wt9g(t, g_=g_):
            for j in range(4):
                i1 = t.transpose(psC[0:8, j * 128:(j + 1) * 128],
                                 Wg3[:, g_ * 4 + j:g_ * 4 + j + 1, :].squeeze(1),
                                 ident_f[:, :])
            return i1
        p.op("tensor", mm_wt9g, reads=[Wg.name, ident_f.name], writes=[psC.name],
             tag=f"wt9t{g_}")
        p.op("vector", lambda v, g_=g_: v.tensor_copy(
            WT9[0:8, g_ * 512:(g_ + 1) * 512], psC[0:8, 0:512]),
            reads=[psC.name, WT9.name], writes=[WT9.name], tag=f"wt9e{g_}")
    p.op("sync", lambda s: s.dma_start(d["WT9_d"][:, :], WT9[:, :]),
         reads=[WT9.name, "WT9s"], writes=["WT9_d"], dma=True, tag="st_wt9")

    Wtr = sb("g_wtr", [16, 1024], F32)
    for g_ in range(2):
        def mm_wtr(t, g_=g_):
            for j in range(4):
                e = g_ * 4 + j
                i1 = t.transpose(psD[0:16, j * 128:(j + 1) * 128],
                                 Wg[:, e * 128:e * 128 + 16], ident_f[:, :])
            return i1
        p.op("tensor", mm_wtr, reads=[Wg.name, ident_f.name], writes=[psD.name],
             tag=f"wtrt{g_}")
        p.op("vector", lambda v, g_=g_: v.tensor_copy(
            Wtr[:, g_ * 512:(g_ + 1) * 512], psD[0:16, 0:512]),
            reads=[psD.name, Wtr.name], writes=[Wtr.name], tag=f"wtre{g_}")
    p.op("sync", lambda s: s.dma_start(d["Wtr_d"][:, :], Wtr[:, :]),
         reads=[Wtr.name], writes=["Wtr_d"], dma=True, tag="st_wtr")

    MR = sb("g_mr", [16, 256], F32)
    p.op("tensor", lambda t: t.transpose(psC[0:16, 0:128], MU[:, 0:16], ident_f[:, :]),
         reads=[MU.name, ident_f.name], writes=[psC.name], tag="trmu")
    p.op("vector", lambda v: v.tensor_copy(MR[:, 0:128], psC[0:16, 0:128]),
         reads=[psC.name], writes=[MR.name], tag="evmu")
    p.op("tensor", lambda t: t.transpose(psD[0:16, 0:128], RSTD[:, 0:16], ident_f[:, :]),
         reads=[RSTD.name, ident_f.name], writes=[psD.name], tag="trrs")
    p.op("vector", lambda v: v.tensor_copy(MR[:, 128:256], psD[0:16, 0:128]),
         reads=[psD.name, MR.name], writes=[MR.name], tag="evrs")
    p.op("sync", lambda s: s.dma_start(d["mr_d"][:, :], MR[:, :]),
         reads=[MR.name], writes=["mr_d"], dma=True, tag="st_mr")
    murow = sb("g_murow", [1, 2048], F32)
    arow = sb("g_arow", [1, 2048], F32)
    p.op("sync", lambda s: s.dma_start(
        murow[:, :].rearrange("o (t p) -> o t p", p=128),
        d["mr_d"][:, 0:128].unsqueeze(0)),
        reads=["mr_d"], writes=[murow.name], dma=True, tag="ld_murow")
    p.op("sync", lambda s: s.dma_start(
        arow[:, :].rearrange("o (t p) -> o t p", p=128),
        d["mr_d"][:, 128:256].unsqueeze(0)),
        reads=["mr_d"], writes=[arow.name], dma=True, tag="ld_arow")
    p.op("sync", lambda s: s.dma_start(d["murow_d"][:, :], murow[:, :]),
         reads=[murow.name], writes=["murow_d"], dma=True, tag="st_murow")
    p.op("sync", lambda s: s.dma_start(d["arow_d"][:, :], arow[:, :]),
         reads=[arow.name], writes=["arow_d"], dma=True, tag="st_arow")

    Mb = sb("g_Mb", [128, 2048], F32)
    Ab = sb("g_Ab", [128, 2048], F32)
    for half in range(4):
        p.op("tensor", lambda t, half=half: t.matmul(
            psC[:, 0:512], ones_f[0:1, :], murow[:, half * 512:(half + 1) * 512],
            start=True, stop=True),
            reads=[ones_f.name, murow.name], writes=[psC.name], tag=f"bmu{half}")
        p.op("vector", lambda v, half=half: v.tensor_copy(
            Mb[:, half * 512:(half + 1) * 512], psC[:, 0:512]),
            reads=[psC.name], writes=[Mb.name], tag=f"evbmu{half}")
        p.op("tensor", lambda t, half=half: t.matmul(
            psD[:, 0:512], ones_f[0:1, :], arow[:, half * 512:(half + 1) * 512],
            start=True, stop=True),
            reads=[ones_f.name, arow.name], writes=[psD.name], tag=f"bar{half}")
        p.op("vector", lambda v, half=half: v.tensor_copy(
            Ab[:, half * 512:(half + 1) * 512], psD[:, 0:512]),
            reads=[psD.name], writes=[Ab.name], tag=f"evbar{half}")

    xres = sb("g_x", [128, 2048], F32)
    xnscr = sb("g_xnscr", [128, 2048], F32)
    xnb = sb("g_xnb", [128, 2048], BF16)
    for dt in range(ND):
        p.op("sync", lambda s, dt=dt: s.dma_start(
            xres[:, :], d["xg"][dt * 128:(dt + 1) * 128, 0:2048]),
            writes=[xres.name], dma=True, tag=f"ldx{dt}")
        p.op("vector", lambda v: v.tensor_tensor(
            xnscr[:, :], xres[:, :], Mb[:, :], ALU.add),
            reads=[xres.name, Mb.name], writes=[xnscr.name], tag=f"xn0_{dt}")
        p.op("vector", lambda v: v.tensor_tensor(
            xnb[:, :], xnscr[:, :], Ab[:, :], ALU.mult),
            reads=[xnscr.name, Ab.name], writes=[xnb.name], tag=f"xn1_{dt}")
        p.op("sync", lambda s, dt=dt: s.dma_start(
            d["xn_d"][dt * 128:(dt + 1) * 128, :], xnb[:, :]),
            reads=[xnb.name], writes=["xn_d"], dma=True, tag=f"stxn{dt}")



def phase_l0_attn(p, d, es):
    nc = p.nc
    sb = lambda name, shape, dt: es.enter_context(nc.sbuf_tensor(name, shape, dt))
    ps = lambda name, shape, dt: es.enter_context(nc.psum_tensor(name, shape, dt))

    ident_bf = sb("b_ident_bf", [128, 128], BF16)
    p.op("gpsimd", lambda g_: g_.dma_start(ident_bf[:, :], d["ident_f"][:, :]),
         writes=[ident_bf.name], dma=True, tag="b_ld_ident")
    ones_fb = sb("b_ones_fb", [1, 128], F32)
    p.op("sync", lambda s: s.dma_start(ones_fb[:, :], d["ones_f"][0:1, :]),
         writes=["b_onesf"], dma=True, tag="b_ldonesf")
    ident_f2 = sb("b_ident_f2", [128, 128], F32)
    p.op("sync", lambda s: s.dma_start(ident_f2[:, :], d["ident_f"][:, :]),
         writes=["b_identf2"], dma=True, tag="b_ldif2")
    cosT = sb("b_cos", [128, 128], F32)
    sinT = sb("b_sin", [128, 128], F32)
    p.op("sync", lambda s: s.dma_start(cosT[:, :], d["rope0_cos"][:, :]),
         writes=[cosT.name], dma=True, tag="b_ld_cos")
    p.op("sync", lambda s: s.dma_start(sinT[:, :], d["rope0_sin"][:, :]),
         writes=[sinT.name], dma=True, tag="b_ld_sin")

    A4a = sb("b_A4a", [128, 16384], BF16)
    qT = sb("b_qT", [128, 16384], BF16)
    kT = sb("b_kT", [128, 16384], BF16)
    v65 = sb("b_v65", [128, 16384], BF16)
    lat = sb("b_lat", [128, 8192], BF16)
    wq_bf = sb("b_wq", [128, 8192], BF16)
    wsm = sb("b_wsm", [128, 8192], BF16)
    P1 = ps("b_P1", [128, 512], F32)
    P2 = ps("b_P2", [128, 512], F32)
    TP = ps("b_TP", [128, 2048], BF16)
    SC = ps("b_SC", [128, 2048], F32)
    SCKS = ["b_SC0a", "b_SC1a"]
    p.mark_psum(P1.name, P2.name, TP.name, "b_SC0a", "b_SC1a", "b_TP0", "b_TP1")

    xn = A4a
    XN = "b_xn"; AO = "b_ao"

    for dt_ in range(ND):
        p.op("sync", lambda s, dt_=dt_: s.dma_start(
            xn[:, dt_ * 2048:(dt_ + 1) * 2048], d["xn_d"][dt_ * 128:(dt_ + 1) * 128, :]),
            writes=[XN], dma=True, tag=f"b_ldxn{dt_}")
    p.op("gpsimd", lambda g_: g_.dma_start(
        wq_bf[:, :].rearrange("p (k n) -> p k n", n=1024),
        d["wqT"][:, :].rearrange("(k p) n -> p k n", p=128)),
        writes=["b_wq1"], dma=True, tag="b_ldwq")
    p.op("gpsimd", lambda g_: g_.dma_start(
        wsm[:, 0:2048].rearrange("p (k n) -> p k n", n=256),
        d["wkaT"][:, :].rearrange("(k p) n -> p k n", p=128)),
        writes=["b_wka"], dma=True, tag="b_ldwka")
    p.op("gpsimd", lambda g_: g_.dma_start(
        wsm[:, 2048:4096].rearrange("p (k n) -> p k n", n=1024),
        d["wkbT"][:, :].rearrange("(k p) n -> p k n", p=128)),
        writes=["b_wkb"], dma=True, tag="b_ldwkb")
    p.op("gpsimd", lambda g_: g_.dma_start(
        wsm[:, 4096:6144].rearrange("p (k n) -> p k n", n=256),
        d["wvaT"][:, :].rearrange("(k p) n -> p k n", p=128)),
        writes=["b_wva"], dma=True, tag="b_ldwva")
    p.op("gpsimd", lambda g_: g_.dma_start(
        wsm[:, 6144:8192].rearrange("p (k n) -> p k n", n=1024),
        d["wvbT"][:, :].rearrange("(k p) n -> p k n", p=128)),
        writes=["b_wvb"], dma=True, tag="b_ldwvb")

    for which, off, wkey, woff in (("k", 0, "b_wka", 0), ("v", 4096, "b_wva", 4096)):
        for m_ in range(2):
            for nch in range(4):
                pp = P1 if (nch % 2 == 0) else P2
                def mm_lat(t, m_=m_, nch=nch, pp=pp, woff=woff):
                    for kt in range(8):
                        i1 = t.matmul(pp[:, 0:512],
                                      wsm[:, woff + kt * 256 + m_ * 128: woff + kt * 256 + (m_ + 1) * 128],
                                      xn[:, kt * 2048 + nch * 512: kt * 2048 + (nch + 1) * 512],
                                      start=(kt == 0), stop=(kt == 7))
                    return i1
                p.op("tensor", mm_lat, reads=[wkey, XN], writes=[pp.name],
                     tag=f"b_lat{which}{m_}{nch}")
                p.op("vector", lambda v, m_=m_, nch=nch, pp=pp, off=off: v.tensor_copy(
                    lat[:, off + m_ * 2048 + nch * 512: off + m_ * 2048 + (nch + 1) * 512][0:128, :],
                    pp[:, 0:512]),
                    reads=[pp.name], writes=["b_lat" + which], tag=f"b_evlat{which}{m_}{nch}")

    if B_TRUNC == 1:
        return
    qtok = [sb(f"b_qtok{i}", [128, 1024], BF16) for i in range(3)]
    rsc = [sb(f"b_rsc{i}", [128, 256], F32) for i in range(3)]
    rs2 = [sb(f"b_rs2{i}", [128, 256], F32) for i in range(3)]
    rs3 = [sb(f"b_rs3{i}", [128, 256], F32) for i in range(3)]

    qn2 = sb("b_qn2", [128, 16 * 16], F32)
    kn2 = sb("b_kn2", [128, 16 * 16], F32)
    nsq = sb("b_nsq", [128, 1024], F32)

    def norms_from(tt, buf, dst2, dkey2, tag):
        p.op("scalar", lambda s, buf=buf: s.activation(nsq[:, :], buf[:, :], AF.Square),
             reads=[buf.name], writes=[nsq.name], tag=tag + "sq")
        p.op("vector", lambda v, tt=tt, dst2=dst2: v.tensor_reduce(
            dst2[:, tt * 16:(tt + 1) * 16],
            nsq[:, :].rearrange("p (h e) -> p h e", e=64), AX.X, ALU.add),
            reads=[nsq.name], writes=[dkey2], tag=tag + "n2")

    def transpose_store(tt, buf, dst, dkey, tag):
        def mm_tr(t, buf=buf):
            for h in range(16):
                i1 = t.transpose(TP[0:64, h * 128:(h + 1) * 128],
                                 buf[:, h * 64:(h + 1) * 64], ident_bf[:, :])
            return i1
        p.op("tensor", mm_tr, reads=[buf.name, ident_bf.name], writes=[TP.name],
             tag=tag + "tr")
        for h in range(16):
            p.op("vector", lambda v, h=h, dst=dst, tt=tt: v.tensor_copy(
                dst[(h % 2) * 64:(h % 2) * 64 + 64,
                    (h // 2) * 2048 + tt * 128:(h // 2) * 2048 + (tt + 1) * 128],
                TP[0:64, h * 128:(h + 1) * 128]),
                reads=[TP.name], writes=[dkey], tag=tag + f"e{h}")

    def rope_and_store(tt, buf, sc1, sc2, sc3, dst, dkey, tag):
        pv = SC[:, 0:1024].rearrange("p (h x) -> p h x", x=64)
        p.op("scalar", lambda s, buf=buf: s.copy(buf[:, :], SC[:, 0:1024]),
             reads=["b_SC0a"], writes=[buf.name], tag=tag + "cp")
        if B_TRUNC == 11:
            return
        bv = buf[:, :].rearrange("p (h x) -> p h x", x=64)
        p.op("vector", lambda v, sc1=sc1, bv=bv: v.tensor_copy(
            sc1[:, :].rearrange("p (h x) -> p h x", x=16), bv[:, :, 0:16]),
            reads=[buf.name], writes=[sc1.name], tag=tag + "g")
        if B_TRUNC == 12:
            return
        s4 = sc1[:, :].rearrange("p (h y z) -> p h y z", y=8, z=2)
        qe = s4[:, :, :, 0:1].squeeze(3)
        qo = s4[:, :, :, 1:2].squeeze(3)
        cosb = cosT[:, :].rearrange("p (h y) -> p h y", y=8)
        sinb = sinT[:, :].rearrange("p (h y) -> p h y", y=8)
        o4 = sc2[:, :].rearrange("p (h y z) -> p h y z", y=8, z=2)
        oe = o4[:, :, :, 0:1].squeeze(3)
        oo = o4[:, :, :, 1:2].squeeze(3)
        t4 = sc3[:, :].rearrange("p (h y z) -> p h y z", y=8, z=2)
        te = t4[:, :, :, 0:1].squeeze(3)
        to = t4[:, :, :, 1:2].squeeze(3)
        p.op("vector", lambda v: v.tensor_tensor(oe, qe, cosb, ALU.mult),
             reads=[sc1.name, cosT.name], writes=[sc2.name], tag=tag + "r1")
        p.op("vector", lambda v: v.tensor_tensor(oo, qe, sinb, ALU.mult),
             reads=[sc1.name, sinT.name], writes=[sc2.name], tag=tag + "r2")
        p.op("vector", lambda v: v.tensor_tensor(te, qo, sinb, ALU.mult),
             reads=[sc1.name, sinT.name], writes=[sc3.name], tag=tag + "r3")
        p.op("vector", lambda v: v.tensor_tensor(to, qo, cosb, ALU.mult),
             reads=[sc1.name, cosT.name], writes=[sc3.name], tag=tag + "r4")
        p.op("vector", lambda v: v.tensor_tensor(oe, oe, te, ALU.subtract),
             reads=[sc2.name, sc3.name], writes=[sc2.name], tag=tag + "r5")
        p.op("vector", lambda v: v.tensor_tensor(oo, oo, to, ALU.add),
             reads=[sc2.name, sc3.name], writes=[sc2.name], tag=tag + "r6")
        b4 = buf[:, :].rearrange("p (h y z) -> p h y z", y=32, z=2)
        p.op("vector", lambda v: v.tensor_copy(b4[:, :, 0:8, 0:1].squeeze(3), oe),
             reads=[sc2.name], writes=[buf.name], tag=tag + "w1")
        p.op("vector", lambda v: v.tensor_copy(b4[:, :, 0:8, 1:2].squeeze(3), oo),
             reads=[sc2.name], writes=[buf.name], tag=tag + "w2")
        if B_TRUNC == 13:
            return
        return

    pending = []
    gi = 0
    for part in ("q", "k", "v"):
        for tt in range(NT):
            gi += 1
            buf = qtok[gi % 3]; sc1 = rsc[gi % 3]; sc2 = rs2[gi % 3]; sc3 = rs3[gi % 3]
            if part == "q":
                def mm_q(t, tt=tt):
                    for och in range(2):
                        for kt in range(8):
                            i1 = t.matmul(SC[:, och * 512:(och + 1) * 512],
                                          xn[:, kt * 2048 + tt * 128: kt * 2048 + (tt + 1) * 128],
                                          wq_bf[:, kt * 1024 + och * 512: kt * 1024 + (och + 1) * 512],
                                          start=(kt == 0), stop=(kt == 7))
                    return i1
                p.op("tensor", mm_q, reads=[XN, "b_wq1"], writes=SCKS, tag=f"b_q{tt}")
                rope_and_store(tt, buf, sc1, sc2, sc3, qT, "b_qTf", f"b_rq{tt}_")
                pending.append((tt, buf, qT, "b_qTf", f"b_tq{tt}_"))
            elif part == "k":
                def mm_k(t, tt=tt):
                    for och in range(2):
                        for kt in range(2):
                            i1 = t.matmul(SC[:, och * 512:(och + 1) * 512],
                                          lat[:, kt * 2048 + tt * 128: kt * 2048 + (tt + 1) * 128],
                                          wsm[:, 2048 + kt * 1024 + och * 512: 2048 + kt * 1024 + (och + 1) * 512],
                                          start=(kt == 0), stop=(kt == 1))
                    return i1
                p.op("tensor", mm_k, reads=["b_latk", "b_wkb"], writes=SCKS, tag=f"b_k{tt}")
                rope_and_store(tt, buf, sc1, sc2, sc3, kT, "b_kTf", f"b_rk{tt}_")
                pending.append((tt, buf, kT, "b_kTf", f"b_tk{tt}_"))
            else:
                def mm_v(t, tt=tt):
                    for och in range(2):
                        for kt in range(2):
                            i1 = t.matmul(SC[:, och * 512:(och + 1) * 512],
                                          lat[:, 4096 + kt * 2048 + tt * 128: 4096 + kt * 2048 + (tt + 1) * 128],
                                          wsm[:, 6144 + kt * 1024 + och * 512: 6144 + kt * 1024 + (och + 1) * 512],
                                          start=(kt == 0), stop=(kt == 1))
                    return i1
                p.op("tensor", mm_v, reads=["b_latv", "b_wvb"], writes=SCKS, tag=f"b_v{tt}")
                p.op("vector", lambda v, tt=tt: v.tensor_copy(
                    v65[:, tt * 1024:(tt + 1) * 1024], SC[:, 0:1024]),
                    reads=["b_SC0a"], writes=["b_v65"], tag=f"b_ev{tt}")
            if len(pending) > 2:
                a_ = pending.pop(0)
                transpose_store(a_[0], a_[1], a_[2], a_[3], a_[4])
                if a_[2] is qT:
                    norms_from(a_[0], a_[1], qn2, qn2.name, a_[4] + "nq")
                elif a_[2] is kT:
                    norms_from(a_[0], a_[1], kn2, kn2.name, a_[4] + "nk")
    while pending:
        a_ = pending.pop(0)
        transpose_store(a_[0], a_[1], a_[2], a_[3], a_[4])
        if a_[2] is qT:
            norms_from(a_[0], a_[1], qn2, qn2.name, a_[4] + "nq")
        elif a_[2] is kT:
            norms_from(a_[0], a_[1], kn2, kn2.name, a_[4] + "nk")

    if B_TRUNC == 2:
        return
    p.op("scalar", lambda s: s.activation(qn2[:, :], qn2[:, :], AF.Ln),
         reads=[qn2.name], writes=[qn2.name], tag="b_qln")
    p.op("scalar", lambda s: s.activation(qn2[:, :], qn2[:, :], AF.Exp, scale=0.5),
         reads=[qn2.name], writes=[qn2.name], tag="b_qex")
    p.op("scalar", lambda s: s.activation(kn2[:, :], kn2[:, :], AF.Ln),
         reads=[kn2.name], writes=[kn2.name], tag="b_kln")
    p.op("scalar", lambda s: s.activation(kn2[:, :], kn2[:, :], AF.Exp, scale=0.5),
         reads=[kn2.name], writes=[kn2.name], tag="b_kex")
    kmaxa = sb("b_kmaxa", [128, 16], F32)
    kmaxr = sb("b_kmaxr", [16, 128], F32)
    maxkb = sb("b_maxkb", [128, 16], F32)
    p.op("vector", lambda v: v.tensor_reduce(
        kmaxa[:, :], kn2[:, :].rearrange("p (t h) -> p h t", h=16), AX.X, ALU.max),
        reads=[kn2.name], writes=[kmaxa.name], tag="b_kmx1")
    p.op("tensor", lambda t: t.transpose(SC[0:16, 0:128], kmaxa[:, :], ident_f2[:, :]),
         reads=[kmaxa.name, "b_identf2"], writes=["b_SC0a"], tag="b_kmt")
    p.op("vector", lambda v: v.tensor_copy(kmaxr[:, :], SC[0:16, 0:128]),
         reads=["b_SC0a"], writes=[kmaxr.name], tag="b_kme")
    p.op("vector", lambda v: v.tensor_reduce(
        kmaxr[0:16, 0:1], kmaxr[:, :], AX.X, ALU.max),
        reads=[kmaxr.name], writes=[kmaxr.name + "m"], tag="b_kmx2")
    p.op("tensor", lambda t: t.transpose(SC[0:1, 0:16], kmaxr[0:16, 0:1],
                                         ident_f2[0:16, 0:16]),
         reads=[kmaxr.name + "m", "b_identf2"], writes=["b_SC0a"], tag="b_kmt2")
    p.op("vector", lambda v: v.tensor_scalar(
        kmaxr[0:1, 16:32], SC[0:1, 0:16], -0.125, None, ALU.mult),
        reads=["b_SC0a"], writes=[kmaxr.name + "r"], tag="b_kms")
    p.op("tensor", lambda t: t.matmul(SC[:, 0:16], ones_fb[0:1, :],
                                      kmaxr[0:1, 16:32], start=True, stop=True),
         reads=["b_onesf", kmaxr.name + "r"], writes=["b_SC0a"], tag="b_kmb")
    p.op("vector", lambda v: v.tensor_copy(maxkb[:, :], SC[:, 0:16]),
         reads=["b_SC0a"], writes=[maxkb.name], tag="b_kme2")

    p.barrier()
    attn_bf = [lat[:, 0:2048], lat[:, 2048:4096]]
    attnT_sb = [lat[:, 4096:6144], lat[:, 6144:8192]]
    _ABK = ["b_abf0", "b_abf1"]
    _ATK = ["b_at0", "b_at1"]
    nmx = sb("b_nmx", [128, 16], F32)
    attn_out = A4a

    def stage1(h, qt):
        par = (h * 16 + qt) % 2
        ab = attn_bf[par]
        abk = _ABK[par]
        hb = (h % 2) * 64
        hc = (h // 2) * 2048
        nc0 = par * 8
        ik = f"{h}_{qt}"
        def mm_sc(t, hb=hb, hc=hc, qt=qt):
            for nch in range(4):
                i1 = t.matmul(SC[:, nch * 512:(nch + 1) * 512],
                              qT[hb:hb + 64, hc + qt * 128:hc + (qt + 1) * 128],
                              kT[hb:hb + 64, hc + nch * 512:hc + (nch + 1) * 512],
                              start=True, stop=True)
            return i1
        p.op("tensor", mm_sc, reads=["b_qTf", "b_kTf"], writes=SCKS,
             tag=f"b_s{ik}")
        p.op("vector", lambda v, nc0=nc0, h=h, qt=qt: v.tensor_tensor(
            nmx[:, nc0 + 3:nc0 + 4], qn2[:, qt * 16 + h:qt * 16 + h + 1],
            maxkb[:, h:h + 1], ALU.mult),
            reads=[qn2.name, maxkb.name], writes=[f"b_nm{par}"], tag=f"b_ns{ik}")
        p.op("scalar", lambda s, ab=ab, nc0=nc0: s.activation(
            ab[:, :], SC[:, 0:2048], AF.Exp,
            bias=nmx[:, nc0 + 3:nc0 + 4], scale=0.125,
            accum_out=nmx[:, nc0 + 6:nc0 + 7]),
            reads=SCKS + [f"b_nm{par}"], writes=[abk, f"b_nm{par}z"],
            tag=f"b_e{ik}")
        p.op("vector", lambda v, nc0=nc0: v.reciprocal(
            nmx[:, nc0 + 7:nc0 + 8], nmx[:, nc0 + 6:nc0 + 7]),
            reads=[f"b_nm{par}z"], writes=[f"b_nm{par}z"], tag=f"b_rz{ik}")
        p.op("vector", lambda v, ab=ab, nc0=nc0: v.tensor_scalar(
            ab[:, :], ab[:, :], nmx[:, nc0 + 7:nc0 + 8], None, ALU.mult),
            reads=[abk, f"b_nm{par}z"], writes=[abk], tag=f"b_n{ik}")

    def stage2(h, qt):
        par = (h * 16 + qt) % 2
        ab = attn_bf[par]
        at = attnT_sb[par]
        abk = _ABK[par]
        atk = _ATK[par]
        AVP = P1 if par == 0 else P2
        hb = (h % 2) * 64
        hc = (h // 2) * 2048
        ik = f"{h}_{qt}"
        for grp in range(2):
            TPg = TP[:, grp * 1024:(grp + 1) * 1024]
            tk = f"b_TP{grp}"
            def mm_at(t, ab=ab, grp=grp, TPg=TPg):
                for j in range(8):
                    kt_ = grp * 8 + j
                    i1 = t.transpose(TPg[:, j * 128:(j + 1) * 128],
                                     ab[:, kt_ * 128:(kt_ + 1) * 128],
                                     ident_bf[:, :])
                return i1
            p.op("tensor", mm_at, reads=[abk, ident_bf.name], writes=[tk],
                 tag=f"b_t{grp}_{ik}")
            p.op("scalar", lambda s, at=at, grp=grp, TPg=TPg: s.copy(
                at[:, grp * 1024:(grp + 1) * 1024], TPg),
                reads=[tk], writes=[atk], tag=f"b_v{grp}_{ik}")
        def mm_av(t, h=h, at=at, AVP=AVP):
            for kt_ in range(16):
                i1 = t.matmul(AVP[0:64, 0:128],
                              v65[:, kt_ * 1024 + h * 64: kt_ * 1024 + (h + 1) * 64],
                              at[:, kt_ * 128:(kt_ + 1) * 128],
                              start=(kt_ == 0), stop=(kt_ == 15))
            return i1
        p.op("tensor", mm_av, reads=["b_v65", atk], writes=[AVP.name],
             tag=f"b_a{ik}")
        p.op("vector", lambda v, hb=hb, hc=hc, qt=qt, AVP=AVP: v.tensor_copy(
            attn_out[hb:hb + 64, hc + qt * 128:hc + (qt + 1) * 128],
            AVP[0:64, 0:128]),
            reads=[AVP.name], writes=[AO], tag=f"b_o{ik}")

    prev = None
    for h in range(16):
        for qt in range(NT):
            stage1(h, qt)
            if prev is not None:
                stage2(*prev)
            prev = (h, qt)
    stage2(*prev)

    p.barrier()
    p.op("gpsimd", lambda g_: g_.dma_start(
        wq_bf[:, :].rearrange("p (k n) -> p k n", n=1024),
        d["woT"][:, :].rearrange("(k p) n -> p k n", p=128)),
        writes=["b_wo", "b_wq1"], dma=True, tag="b_ldwo")
    xres = lat[:, 0:4096].bitcast(F32)
    x1a = lat[:, 4096:8192].bitcast(F32)
    XRK = ["b_xres", "b_latk", "b_latv"]
    X1K = ["b_x1a", "b_latk", "b_latv"]
    for dt_ in range(ND):
        p.op("sync", lambda s, dt_=dt_: s.dma_start(
            xres[:, :], d["xg"][dt_ * 128:(dt_ + 1) * 128, 0:2048]),
            writes=XRK, dma=True, tag=f"b_ldx{dt_}")
        for nch in range(4):
            pp = P1 if nch % 2 == 0 else P2
            def mm_wo(t, dt_=dt_, nch=nch, pp=pp):
                for kt in range(8):
                    i1 = t.matmul(pp[:, 0:512],
                                  wq_bf[:, kt * 1024 + dt_ * 128: kt * 1024 + dt_ * 128 + 128],
                                  attn_out[:, kt * 2048 + nch * 512: kt * 2048 + (nch + 1) * 512],
                                  start=(kt == 0), stop=(kt == 7))
                return i1
            p.op("tensor", mm_wo, reads=["b_wo", AO], writes=[pp.name],
                 tag=f"b_wo{dt_}_{nch}")
            p.op("vector", lambda v, nch=nch, pp=pp: v.tensor_tensor(
                x1a[:, nch * 512:(nch + 1) * 512], pp[:, 0:512],
                xres[:, nch * 512:(nch + 1) * 512], ALU.add),
                reads=[pp.name, "b_xres"], writes=X1K, tag=f"b_ra{dt_}_{nch}")
        p.op("sync", lambda s, dt_=dt_: s.dma_start(
            d["x1a_d"][dt_ * 128:(dt_ + 1) * 128, :], x1a[:, :]),
            reads=["b_x1a"], writes=["x1a_d"], dma=True, tag=f"b_stx{dt_}")



def phase_moe(p, d, es):
    nc = p.nc
    sb = lambda name, shape, dt: es.enter_context(nc.sbuf_tensor(name, shape, dt))
    ps = lambda name, shape, dt: es.enter_context(nc.psum_tensor(name, shape, dt))

    ident_f = sb("c_ident_f", [128, 128], F32)
    p.op("sync", lambda s: s.dma_start(ident_f[:, :], d["ident_f"][:, :]),
         writes=[ident_f.name], dma=True, tag="c_ld_ident")
    xnsb = sb("c_xn", [128, 16384], BF16)
    for dt_ in range(ND):
        p.op("sync", lambda s, dt_=dt_: s.dma_start(
            xnsb[:, dt_ * 2048:(dt_ + 1) * 2048],
            d["xn_d"][dt_ * 128:(dt_ + 1) * 128, :]),
            writes=["c_xn"], dma=True, tag=f"c_ldxn{dt_}")
    ysb = sb("c_y", [128, 16384], F32)
    wA = sb("c_wA", [128, 16384], BF16)
    wB = sb("c_wB", [128, 16384], BF16)
    hsb = sb("c_h", [128, 8192], BF16)
    Wgsb = sb("c_Wg", [128, 1024], F32)
    WT9sb = sb("c_WT9", [33, 2048], F32)
    b2sb = sb("c_b2", [33, 1024], F32)
    b1sb = sb("c_b1", [128, 8 * 32], F32)
    sb1sb = sb("c_sb1", [128, 32], F32)
    p.op("sync", lambda s: s.dma_start(Wgsb[:, :], d["Wg_d"][:, :]),
         writes=[Wgsb.name], dma=True, tag="c_ldwg")
    p.op("sync", lambda s: s.dma_start(WT9sb[:, :], d["WT9_d"][:, :]),
         writes=[WT9sb.name], dma=True, tag="c_ldwt9")
    p.op("sync", lambda s: s.dma_start(b2sb[:, :], d["b2ext"][:, :]),
         writes=[b2sb.name], dma=True, tag="c_ldb2")
    p.op("sync", lambda s: s.dma_start(
        b1sb[:, :].rearrange("p (e n) -> p e n", n=32),
        d["moe_b1"][:, :, :].rearrange("e p n -> p e n")),
        writes=[b1sb.name], dma=True, tag="c_ldb1")
    p.op("sync", lambda s: s.dma_start(sb1sb[:, :], d["sb1"][:, :]),
         writes=[sb1sb.name], dma=True, tag="c_ldsb1")

    P1 = ps("c_P1", [128, 512], F32)
    P2 = ps("c_P2", [128, 512], F32)
    PO = ps("c_PO", [128, 1024], F32)
    PO2 = ps("c_PO2", [128, 1024], F32)
    PT = ps("c_PT", [128, 512], F32)
    p.mark_psum(P1.name, P2.name, PO.name, PO2.name, PT.name)

    for tt in range(NT):
        def mm_seed(t, tt=tt):
            for nch in range(2):
                i1 = t.matmul(PO[:, nch * 512:(nch + 1) * 512],
                              WT9sb[:, tt * 128:(tt + 1) * 128],
                              b2sb[:, nch * 512:(nch + 1) * 512],
                              start=True, stop=True)
            return i1
        p.op("tensor", mm_seed, reads=[WT9sb.name, b2sb.name], writes=[PO.name],
             tag=f"c_seed{tt}")
        p.op("vector", lambda v, tt=tt: v.tensor_copy(
            ysb[:, tt * 1024:(tt + 1) * 1024], PO[:, 0:1024]),
            reads=[PO.name], writes=[ysb.name], tag=f"c_evseed{tt}")

    NTB = 4
    for e in range(9):
        for fh in range(2):
            if e < 8:
                w1src = d["moe_w1"]
                p.op("gpsimd", lambda g_, e=e, fh=fh: g_.dma_start(
                    wA[:, :].rearrange("p (k n) -> p k n", n=2048),
                    d["moe_w1"][e, :, fh * 2048:(fh + 1) * 2048]
                    .rearrange("(k p) n -> p k n", p=128)),
                    writes=[wA.name], dma=True, tag=f"c_ldw1_{e}_{fh}")
                p.op("gpsimd", lambda g_, e=e, fh=fh: g_.dma_start(
                    wB[:, :].rearrange("p (k n) -> p k n", n=1024),
                    d["moe_w2"][e, fh * 2048:(fh + 1) * 2048, :]
                    .rearrange("(k p) n -> p k n", p=128)),
                    writes=[wB.name], dma=True, tag=f"c_ldw2_{e}_{fh}")
            else:
                p.op("gpsimd", lambda g_, fh=fh: g_.dma_start(
                    wA[:, :].rearrange("p (k n) -> p k n", n=2048),
                    d["sw1"][:, fh * 2048:(fh + 1) * 2048]
                    .rearrange("(k p) n -> p k n", p=128)),
                    writes=[wA.name], dma=True, tag=f"c_ldsw1_{fh}")
                p.op("gpsimd", lambda g_, fh=fh: g_.dma_start(
                    wB[:, :].rearrange("p (k n) -> p k n", n=1024),
                    d["sw2"][fh * 2048:(fh + 1) * 2048, :]
                    .rearrange("(k p) n -> p k n", p=128)),
                    writes=[wB.name], dma=True, tag=f"c_ldsw2_{fh}")
            for tb in range(NTB):
                for mt in range(16):
                    pp = P1 if mt % 2 == 0 else P2
                    def mm_h(t, mt=mt, tb=tb, pp=pp):
                        for kt in range(8):
                            i1 = t.matmul(
                                pp[:, 0:512],
                                wA[:, kt * 2048 + mt * 128: kt * 2048 + (mt + 1) * 128],
                                xnsb[:, kt * 2048 + tb * 512: kt * 2048 + (tb + 1) * 512],
                                start=(kt == 0), stop=(kt == 7))
                        return i1
                    p.op("tensor", mm_h, reads=[wA.name, "c_xn"], writes=[pp.name],
                         tag=f"c_h{e}_{fh}_{tb}_{mt}")
                    bcol = (sb1sb[:, fh * 16 + mt: fh * 16 + mt + 1] if e == 8
                            else b1sb[:, e * 32 + fh * 16 + mt: e * 32 + fh * 16 + mt + 1])
                    bkey = sb1sb.name if e == 8 else b1sb.name
                    p.op("scalar", lambda s, mt=mt, pp=pp, bcol=bcol: s.activation(
                        hsb[:, mt * 512:(mt + 1) * 512], pp[:, 0:512], AF.Silu,
                        bias=bcol),
                        reads=[pp.name, bkey], writes=[hsb.name],
                        tag=f"c_si{e}_{fh}_{tb}_{mt}")
                for tloc in range(4):
                    tt = tb * 4 + tloc
                    po_ = PO if tloc % 2 == 0 else PO2
                    def mm_o(t, tloc=tloc, po_=po_):
                        for nch in range(2):
                            for kt in range(16):
                                i1 = t.matmul(
                                    po_[:, nch * 512:(nch + 1) * 512],
                                    hsb[:, kt * 512 + tloc * 128: kt * 512 + (tloc + 1) * 128],
                                    wB[:, kt * 1024 + nch * 512: kt * 1024 + (nch + 1) * 512],
                                    start=(kt == 0), stop=(kt == 15))
                        return i1
                    p.op("tensor", mm_o, reads=[hsb.name, wB.name], writes=[po_.name],
                         tag=f"c_o{e}_{fh}_{tb}_{tloc}")
                    for nch in range(2):
                        if e < 8:
                            p.op("vector", lambda v, e=e, tt=tt, nch=nch, po_=po_:
                                 v.scalar_tensor_tensor(
                                     ysb[:, tt * 1024 + nch * 512: tt * 1024 + (nch + 1) * 512],
                                     po_[:, nch * 512:(nch + 1) * 512],
                                     Wgsb[:, e * 128 + tt: e * 128 + tt + 1],
                                     ysb[:, tt * 1024 + nch * 512: tt * 1024 + (nch + 1) * 512],
                                     ALU.mult, ALU.add),
                                 reads=[po_.name, Wgsb.name, ysb.name], writes=[ysb.name],
                                 tag=f"c_y{e}_{fh}_{tb}_{tloc}_{nch}")
                        else:
                            p.op("vector", lambda v, tt=tt, nch=nch, po_=po_:
                                 v.scalar_tensor_tensor(
                                     ysb[:, tt * 1024 + nch * 512: tt * 1024 + (nch + 1) * 512],
                                     po_[:, nch * 512:(nch + 1) * 512],
                                     0.1,
                                     ysb[:, tt * 1024 + nch * 512: tt * 1024 + (nch + 1) * 512],
                                     ALU.mult, ALU.add),
                                 reads=[po_.name, ysb.name], writes=[ysb.name],
                                 tag=f"c_ys{fh}_{tb}_{tloc}_{nch}")

    x1asb = wA[:, 0:4096].bitcast(F32)
    x1sb = wA[:, 4096:8192].bitcast(F32)
    XAK = ["c_x1a", wA.name]
    X1K = ["c_x1", wA.name]
    for dt_ in range(ND):
        p.op("sync", lambda s, dt_=dt_: s.dma_start(
            x1asb[:, :], d["x1a_d"][dt_ * 128:(dt_ + 1) * 128, :]),
            writes=XAK, dma=True, tag=f"c_ldx1a{dt_}")
        for g_ in range(4):
            def mm_t(t, dt_=dt_, g_=g_):
                for j in range(4):
                    tt = g_ * 4 + j
                    i1 = t.transpose(PT[:, j * 128:(j + 1) * 128],
                                     ysb[:, tt * 1024 + dt_ * 128: tt * 1024 + (dt_ + 1) * 128],
                                     ident_f[:, :])
                return i1
            p.op("tensor", mm_t, reads=[ysb.name, ident_f.name], writes=[PT.name],
                 tag=f"c_t{dt_}_{g_}")
            p.op("vector", lambda v, g_=g_: v.tensor_tensor(
                x1sb[:, g_ * 512:(g_ + 1) * 512], PT[:, 0:512],
                x1asb[:, g_ * 512:(g_ + 1) * 512], ALU.add),
                reads=[PT.name, "c_x1a"], writes=X1K, tag=f"c_a{dt_}_{g_}")
        p.op("sync", lambda s, dt_=dt_: s.dma_start(
            d["x1_d"][dt_ * 128:(dt_ + 1) * 128, :], x1sb[:, :]),
            reads=["c_x1"], writes=["x1_d"], dma=True, tag=f"c_stx1{dt_}")


def build_phaseC():
    nc = bass.Bass("TRN2", target_bir_lowering=False)
    es = contextlib.ExitStack()
    d = {}
    d["ident_f"] = nc.dram_tensor("ident_f", [128, 128], F32, kind="ExternalInput").ap()
    d["moe_w1"] = nc.dram_tensor("moe_w1", [8, D, 4096], F32, kind="ExternalInput").ap()
    d["moe_w2"] = nc.dram_tensor("moe_w2", [8, 4096, D], F32, kind="ExternalInput").ap()
    d["moe_b1"] = nc.dram_tensor("moe_b1", [8, 128, 32], F32, kind="ExternalInput").ap()
    d["b2ext"] = nc.dram_tensor("b2ext", [33, D], F32, kind="ExternalInput").ap()
    d["sw1"] = nc.dram_tensor("sw1", [D, 4096], F32, kind="ExternalInput").ap()
    d["sw2"] = nc.dram_tensor("sw2", [4096, D], F32, kind="ExternalInput").ap()
    d["sb1"] = nc.dram_tensor("sb1", [128, 32], F32, kind="ExternalInput").ap()
    d["Wg_d"] = nc.dram_tensor("Wg_d", [128, 1024], F32, kind="ExternalInput").ap()
    d["WT9_d"] = nc.dram_tensor("WT9_d", [33, 2048], F32, kind="ExternalInput").ap()
    d["xn_d"] = nc.dram_tensor("xn_d", [D, T], BF16, kind="ExternalInput").ap()
    d["x1a_d"] = nc.dram_tensor("x1a_d", [D, T], F32, kind="ExternalInput").ap()
    d["x1_d"] = nc.dram_tensor("x1_d", [D, T], F32, kind="ExternalOutput").ap()
    p = Prog(nc)
    with es:
        phase_moe(p, d, es)
        p.emit()
    nc.reset()
    return nc



def phase_l1_attn(p, d, es):
    nc = p.nc
    sb = lambda name, shape, dt: es.enter_context(nc.sbuf_tensor(name, shape, dt))
    ps = lambda name, shape, dt: es.enter_context(nc.psum_tensor(name, shape, dt))

    ones_f = sb("d_ones_f", [128, 128], F32)
    ones_bf = sb("d_ones_bf", [128, 128], BF16)
    p.op("sync", lambda s: s.dma_start(ones_f[:, :], d["ones_f"][:, :]),
         writes=[ones_f.name], dma=True, tag="d_ldones")
    p.op("gpsimd", lambda g_: g_.dma_start(ones_bf[:, :], d["ones_f"][:, :]),
         writes=[ones_bf.name], dma=True, tag="d_ldonesb")

    xn1 = sb("d_xn1", [128, 16384], BF16)
    qy = sb("d_qy", [128, 16384], BF16)
    ktok = sb("d_k", [128, 16384], BF16)
    v65 = sb("d_v65", [128, 16 * 1040], BF16)
    wbuf = sb("d_w", [128, 8192], BF16)
    x1dt = sb("d_x1dt", [128, 2048], F32)
    xsq = sb("d_xsq", [128, 2048], F32)
    Mb = sb("d_Mb", [128, 2048], F32)
    Ab = sb("d_Ab", [128, 2048], F32)
    rowsA = sb("d_rowsA", [1, 2048], F32)
    rowsB = sb("d_rowsB", [1, 2048], F32)
    bq1 = sb("d_bq1", [128, 8], F32)
    gam = sb("d_gam", [128, 8], F32)
    bet = sb("d_bet", [128, 8], F32)
    bkrow = sb("d_bkrow", [1, 1024], BF16)
    bvrow = sb("d_bvrow", [1, 1024], BF16)
    kv_sb = sb("d_kv", [64, 65], BF16)
    qcp = sb("d_qcp", [64, 512], BF16)
    rb_sb = sb("d_rb", [64, 512], F32)
    tmp_sb = sb("d_tmp", [128, 512], F32)
    dn_row = sb("d_dn", [1, 512], F32)
    epsc = sb("d_epsc", [128, 1], F32)
    p.op("vector", lambda v: v.memset(epsc[:, :], 1e-5), writes=[epsc.name], tag="d_eps")

    PA = ps("d_PA", [1, 512], F32)
    P1 = ps("d_P1", [128, 512], F32)
    P2 = ps("d_P2", [128, 512], F32)
    PT = ps("d_PT", [128, 512], F32)
    PO = ps("d_PO", [128, 1024], F32)
    p.mark_psum(PA.name, P1.name, P2.name, PT.name, PO.name)

    for nm, t_ in (("bq1", bq1), ("gamma1", gam), ("beta1", bet)):
        p.op("sync", lambda s, nm=nm, t_=t_: s.dma_start(t_[:, :], d[nm][:, :]),
             writes=[t_.name], dma=True, tag="d_ld" + nm)
    p.op("gpsimd", lambda g_: g_.dma_start(bkrow[:, :], d["bk1row"][:, :]),
         writes=[bkrow.name], dma=True, tag="d_ldbk")
    p.op("gpsimd", lambda g_: g_.dma_start(bvrow[:, :], d["bv1row"][:, :]),
         writes=[bvrow.name], dma=True, tag="d_ldbv")

    def ln_stats(load_src, srckey, is_bf, tagp):
        p.op("vector", lambda v: v.memset(rowsA[:, :], 0.0),
             writes=["d_rA"], tag=tagp + "za")
        p.op("vector", lambda v: v.memset(rowsB[:, :], 0.0),
             writes=["d_rB"], tag=tagp + "zb")
        for dt_ in range(ND):
            src = load_src(dt_)
            onesc = ones_bf if is_bf else ones_f
            for nch in range(4):
                p.op("tensor", lambda t, src=src, onesc=onesc, nch=nch: t.matmul(
                    PA[0:1, 0:512], onesc[:, 0:1],
                    src[:, nch * 512:(nch + 1) * 512], start=True, stop=True),
                    reads=[srckey, onesc.name], writes=[PA.name],
                    tag=tagp + f"s1_{dt_}_{nch}")
                p.op("vector", lambda v, nch=nch: v.tensor_tensor(
                    rowsA[:, nch * 512:(nch + 1) * 512],
                    rowsA[:, nch * 512:(nch + 1) * 512], PA[0:1, 0:512], ALU.add),
                    reads=[PA.name, "d_rA"], writes=["d_rA"], tag=tagp + f"a1_{dt_}_{nch}")
            p.op("scalar", lambda s, src=src: s.activation(xsq[:, :], src[:, :], AF.Square),
                 reads=[srckey], writes=[xsq.name], tag=tagp + f"sq_{dt_}")
            for nch in range(4):
                p.op("tensor", lambda t, nch=nch: t.matmul(
                    PA[0:1, 0:512], ones_f[:, 0:1],
                    xsq[:, nch * 512:(nch + 1) * 512], start=True, stop=True),
                    reads=[xsq.name, ones_f.name], writes=[PA.name],
                    tag=tagp + f"s2_{dt_}_{nch}")
                p.op("vector", lambda v, nch=nch: v.tensor_tensor(
                    rowsB[:, nch * 512:(nch + 1) * 512],
                    rowsB[:, nch * 512:(nch + 1) * 512], PA[0:1, 0:512], ALU.add),
                    reads=[PA.name, "d_rB"], writes=["d_rB"], tag=tagp + f"a2_{dt_}_{nch}")
        p.op("vector", lambda v: v.tensor_scalar(
            rowsA[:, :], rowsA[:, :], -1.0 / 1024.0, None, ALU.mult),
            reads=["d_rA"], writes=["d_rA"], tag=tagp + "mu")
        p.op("vector", lambda v: v.tensor_tensor(
            xsq[0:1, 0:2048], rowsA[:, :], rowsA[:, :], ALU.mult),
            reads=["d_rA"], writes=[xsq.name], tag=tagp + "mu2")
        p.op("vector", lambda v: v.scalar_tensor_tensor(
            rowsB[:, :], rowsB[:, :], 1.0 / 1024.0,
            xsq[0:1, 0:2048], ALU.mult, ALU.subtract),
            reads=["d_rB", xsq.name], writes=["d_rB"], tag=tagp + "var")
        p.op("scalar", lambda s: s.activation(
            rowsB[:, :], rowsB[:, :], AF.Ln, bias=epsc[0:1, :]),
            reads=["d_rB", epsc.name], writes=["d_rB"], tag=tagp + "ln")
        p.op("scalar", lambda s: s.activation(
            rowsB[:, :], rowsB[:, :], AF.Exp, scale=-0.5),
            reads=["d_rB"], writes=["d_rB"], tag=tagp + "rstd")
        for half in range(4):
            p.op("tensor", lambda t, half=half: t.matmul(
                P1[:, 0:512], ones_f[0:1, :],
                rowsA[:, half * 512:(half + 1) * 512],
                start=True, stop=True),
                reads=[ones_f.name, "d_rA"], writes=[P1.name], tag=tagp + f"bm{half}")
            p.op("vector", lambda v, half=half: v.tensor_copy(
                Mb[:, half * 512:(half + 1) * 512], P1[:, 0:512]),
                reads=[P1.name], writes=[Mb.name], tag=tagp + f"em{half}")
            p.op("tensor", lambda t, half=half: t.matmul(
                P2[:, 0:512], ones_f[0:1, :],
                rowsB[:, half * 512:(half + 1) * 512],
                start=True, stop=True),
                reads=[ones_f.name, "d_rB"], writes=[P2.name], tag=tagp + f"ba{half}")
            p.op("vector", lambda v, half=half: v.tensor_copy(
                Ab[:, half * 512:(half + 1) * 512], P2[:, 0:512]),
                reads=[P2.name], writes=[Ab.name], tag=tagp + f"ea{half}")

    def load_x1(dt_):
        p.op("sync", lambda s, dt_=dt_: s.dma_start(
            x1dt[:, :], d["x1_d"][dt_ * 128:(dt_ + 1) * 128, :]),
            writes=[x1dt.name], dma=True, tag=f"d_lx1s{dt_}")
        return x1dt[:, :]
    ln_stats(load_x1, x1dt.name, False, "d_l1_")
    p.op("sync", lambda s: s.dma_start(d["murow1_d"][:, :], rowsA[:, :]),
         reads=["d_rA"], writes=["murow1_d"], dma=True, tag="d_stmu")
    p.op("sync", lambda s: s.dma_start(d["arow1_d"][:, :], rowsB[:, :]),
         reads=["d_rB"], writes=["arow1_d"], dma=True, tag="d_star")
    for dt_ in range(ND):
        p.op("sync", lambda s, dt_=dt_: s.dma_start(
            x1dt[:, :], d["x1_d"][dt_ * 128:(dt_ + 1) * 128, :]),
            writes=[x1dt.name], dma=True, tag=f"d_lx1b{dt_}")
        p.op("vector", lambda v: v.tensor_tensor(
            xsq[:, :], x1dt[:, :], Mb[:, :], ALU.add),
            reads=[x1dt.name, Mb.name], writes=[xsq.name], tag=f"d_xn0_{dt_}")
        p.op("vector", lambda v, dt_=dt_: v.tensor_tensor(
            xn1[:, dt_ * 2048:(dt_ + 1) * 2048], xsq[:, :], Ab[:, :], ALU.mult),
            reads=[xsq.name, Ab.name], writes=[xn1.name], tag=f"d_xn1_{dt_}")
        p.op("sync", lambda s, dt_=dt_: s.dma_start(
            d["xn1_d"][dt_ * 128:(dt_ + 1) * 128, :],
            xn1[:, dt_ * 2048:(dt_ + 1) * 2048]),
            reads=[xn1.name], writes=["xn1_d"], dma=True, tag=f"d_stxn{dt_}")

    p.op("gpsimd", lambda g_: g_.dma_start(
        wbuf[:, :].rearrange("p (k n) -> p k n", n=1024),
        d["wq1T"][:, :].rearrange("(k p) n -> p k n", p=128)),
        writes=[wbuf.name], dma=True, tag="d_ldwq")
    for m_ in range(ND):
        for nch in range(4):
            pp = P1 if nch % 2 == 0 else P2
            def mm_q(t, m_=m_, nch=nch, pp=pp):
                for kt in range(8):
                    i1 = t.matmul(pp[:, 0:512],
                                  wbuf[:, kt * 1024 + m_ * 128: kt * 1024 + (m_ + 1) * 128],
                                  xn1[:, kt * 2048 + nch * 512: kt * 2048 + (nch + 1) * 512],
                                  start=(kt == 0), stop=(kt == 7))
                return i1
            p.op("tensor", mm_q, reads=[wbuf.name, xn1.name], writes=[pp.name],
                 tag=f"d_q{m_}_{nch}")
            p.op("scalar", lambda s, m_=m_, nch=nch, pp=pp: s.activation(
                qy[:, m_ * 2048 + nch * 512: m_ * 2048 + (nch + 1) * 512],
                pp[:, 0:512], AF.Relu, bias=bq1[:, m_:m_ + 1]),
                reads=[pp.name, bq1.name], writes=["d_qf"], tag=f"d_eq{m_}_{nch}")

    for which, wsrc, brow, bkey in (("k", "wk1T", bkrow, bkrow.name),
                                    ("v", "wv1T", bvrow, bvrow.name)):
        p.op("gpsimd", lambda g_, wsrc=wsrc: g_.dma_start(
            wbuf[:, :].rearrange("p (k n) -> p k n", n=1024),
            d[wsrc][:, :].rearrange("(k p) n -> p k n", p=128)),
            writes=[wbuf.name], dma=True, tag=f"d_ldw{which}")
        for tt in range(NT):
            def mm_kv(t, tt=tt, brow=brow):
                for och in range(2):
                    for kt in range(8):
                        t.matmul(PO[:, och * 512:(och + 1) * 512],
                                 xn1[:, kt * 2048 + tt * 128: kt * 2048 + (tt + 1) * 128],
                                 wbuf[:, kt * 1024 + och * 512: kt * 1024 + (och + 1) * 512],
                                 start=(kt == 0), stop=False)
                    i1 = t.matmul(PO[:, och * 512:(och + 1) * 512],
                                  ones_bf[0:1, :],
                                  brow[:, och * 512:(och + 1) * 512],
                                  start=False, stop=True)
                return i1
            p.op("tensor", mm_kv, reads=[xn1.name, wbuf.name, bkey, ones_bf.name],
                 writes=[PO.name], tag=f"d_{which}{tt}")
            if which == "k":
                p.op("scalar", lambda s, tt=tt: s.activation(
                    ktok[:, tt * 1024:(tt + 1) * 1024], PO[:, 0:1024], AF.Relu),
                    reads=[PO.name], writes=[ktok.name], tag=f"d_ek{tt}")
            else:
                v3v = v65[:, tt * 1040:(tt + 1) * 1040].rearrange("p (h c) -> p h c", c=65)
                pvv = PO[:, 0:1024].rearrange("p (h x) -> p h x", x=64)
                p.op("scalar", lambda s, v3v=v3v, pvv=pvv: s.copy(v3v[:, :, 0:64], pvv),
                    reads=[PO.name], writes=["d_v65"], tag=f"d_ev{tt}")
    p.op("vector", lambda v: v.memset(
        v65[:, :].rearrange("p (h c) -> p h c", c=65)[:, :, 64:65], 1.0),
        writes=["d_v65o"], tag="d_vones")

    for h in range(16):
        hb_ = (h % 2) * 64
        hc = (h // 2) * 2048
        def mm_kv2(t, h=h):
            for kt in range(16):
                i1 = t.matmul(P1[0:64, 0:65],
                              ktok[:, kt * 1024 + h * 64: kt * 1024 + (h + 1) * 64],
                              v65[:, kt * 1040 + h * 65: kt * 1040 + (h + 1) * 65],
                              start=(kt == 0), stop=(kt == 15))
            return i1
        p.op("tensor", mm_kv2, reads=[ktok.name, "d_v65", "d_v65o"], writes=[P1.name],
             tag=f"d_kv{h}")
        p.op("vector", lambda v: v.tensor_copy(kv_sb[:, :], P1[0:64, 0:65]),
             reads=[P1.name], writes=[kv_sb.name], tag=f"d_ekv{h}")
        for nch in range(4):
            p.op("vector", lambda v, hb_=hb_, hc=hc, nch=nch: v.tensor_copy(
                qcp[:, :], qy[hb_:hb_ + 64, hc + nch * 512: hc + (nch + 1) * 512]),
                reads=["d_qf"], writes=[qcp.name], tag=f"d_qc{h}_{nch}")
            p.op("tensor", lambda t: t.matmul(
                P2[0:65, 0:512], kv_sb[:, :], qcp[:, :],
                start=True, stop=True),
                reads=[kv_sb.name, qcp.name], writes=[P2.name], tag=f"d_cx{h}_{nch}")
            p.op("vector", lambda v: v.tensor_scalar(
                dn_row[:, :], P2[64:65, 0:512], 1e-6, None, ALU.max),
                reads=[P2.name], writes=[dn_row.name], tag=f"d_dn{h}_{nch}")
            p.op("vector", lambda v: v.reciprocal(dn_row[:, :], dn_row[:, :]),
                 reads=[dn_row.name], writes=[dn_row.name], tag=f"d_rc{h}_{nch}")
            p.op("tensor", lambda t: t.matmul(PT[0:64, 0:512], ones_f[0:1, 0:64],
                                              dn_row[:, :], start=True, stop=True),
                 reads=[ones_f.name, dn_row.name], writes=[PT.name], tag=f"d_rb{h}_{nch}")
            p.op("vector", lambda v: v.tensor_copy(rb_sb[:, :], PT[0:64, 0:512]),
                 reads=[PT.name], writes=[rb_sb.name], tag=f"d_erb{h}_{nch}")
            p.op("vector", lambda v, hb_=hb_: v.tensor_tensor(
                tmp_sb[hb_:hb_ + 64, :], P2[0:64, 0:512], rb_sb[:, :], ALU.mult),
                reads=[P2.name, rb_sb.name], writes=[tmp_sb.name], tag=f"d_o{h}_{nch}")
            p.op("vector", lambda v, hb_=hb_, hc=hc, nch=nch: v.tensor_tensor(
                qy[hb_:hb_ + 64, hc + nch * 512: hc + (nch + 1) * 512],
                tmp_sb[hb_:hb_ + 64, :],
                xn1[hb_:hb_ + 64, hc + nch * 512: hc + (nch + 1) * 512], ALU.add),
                reads=[tmp_sb.name, xn1.name], writes=["d_qf"], tag=f"d_y{h}_{nch}")

    def load_y(dt_):
        return qy[:, dt_ * 2048:(dt_ + 1) * 2048]
    ln_stats(load_y, "d_qf", True, "d_l2_")
    for dt_ in range(ND):
        p.op("sync", lambda s, dt_=dt_: s.dma_start(
            x1dt[:, :], d["x1_d"][dt_ * 128:(dt_ + 1) * 128, :]),
            writes=[x1dt.name], dma=True, tag=f"d_lx1c{dt_}")
        p.op("vector", lambda v, dt_=dt_: v.tensor_tensor(
            xsq[:, :], qy[:, dt_ * 2048:(dt_ + 1) * 2048], Mb[:, :], ALU.add),
            reads=["d_qf", Mb.name], writes=[xsq.name], tag=f"d_n0_{dt_}")
        p.op("vector", lambda v: v.tensor_tensor(
            xsq[:, :], xsq[:, :], Ab[:, :], ALU.mult),
            reads=[xsq.name, Ab.name], writes=[xsq.name], tag=f"d_n1_{dt_}")
        p.op("scalar", lambda s, dt_=dt_: s.activation(
            xsq[:, :], xsq[:, :], AF.Identity,
            bias=bet[:, dt_:dt_ + 1], scale=gam[:, dt_:dt_ + 1]),
            reads=[xsq.name, gam.name, bet.name], writes=[xsq.name], tag=f"d_n2_{dt_}")
        p.op("vector", lambda v: v.tensor_tensor(
            xsq[:, :], xsq[:, :], x1dt[:, :], ALU.add),
            reads=[xsq.name, x1dt.name], writes=[xsq.name], tag=f"d_n3_{dt_}")
        p.op("sync", lambda s, dt_=dt_: s.dma_start(
            d["x1b_d"][dt_ * 128:(dt_ + 1) * 128, :], xsq[:, :]),
            reads=[xsq.name], writes=["x1b_d"], dma=True, tag=f"d_stx{dt_}")


def build_phaseD():
    nc = bass.Bass("TRN2", target_bir_lowering=False)
    es = contextlib.ExitStack()
    d = {}
    for nm, shp in (("ones_f", (128, 128)), ("wq1T", (D, D)), ("wk1T", (D, D)),
                    ("wv1T", (D, D)), ("bq1", (128, 8)), ("gamma1", (128, 8)),
                    ("beta1", (128, 8)), ("bk1row", (1, D)), ("bv1row", (1, D))):
        d[nm] = nc.dram_tensor(nm, list(shp), F32, kind="ExternalInput").ap()
    d["x1_d"] = nc.dram_tensor("x1_d", [D, T], F32, kind="ExternalInput").ap()
    d["xn1_d"] = nc.dram_tensor("xn1_d", [D, T], BF16, kind="ExternalOutput").ap()
    d["murow1_d"] = nc.dram_tensor("murow1_d", [1, T], F32, kind="ExternalOutput").ap()
    d["arow1_d"] = nc.dram_tensor("arow1_d", [1, T], F32, kind="ExternalOutput").ap()
    d["x1b_d"] = nc.dram_tensor("x1b_d", [D, T], F32, kind="ExternalOutput").ap()
    p = Prog(nc)
    with es:
        phase_l1_attn(p, d, es)
        p.emit()
    nc.reset()
    return nc



XFW = 2052

def phase_l1_ffn(p, d, es):
    nc = p.nc
    sb = lambda name, shape, dt: es.enter_context(nc.sbuf_tensor(name, shape, dt))
    ps = lambda name, shape, dt: es.enter_context(nc.psum_tensor(name, shape, dt))

    ident_bf = sb("e_identb", [128, 128], BF16)
    ident_f = sb("e_identf", [128, 128], F32)
    ones_f = sb("e_ones", [128, 128], F32)
    p.op("gpsimd", lambda g_: g_.dma_start(ident_bf[:, :], d["ident_f"][:, :]),
         writes=[ident_bf.name], dma=True, tag="e_ldib")
    p.op("sync", lambda s: s.dma_start(ident_f[:, :], d["ident_f"][:, :]),
         writes=[ident_f.name], dma=True, tag="e_ldif")
    p.op("sync", lambda s: s.dma_start(ones_f[:, :], d["ones_f"][:, :]),
         writes=[ones_f.name], dma=True, tag="e_ldof")
    rcs = sb("e_rcs", [128, 1024], BF16)
    rss = sb("e_rss", [128, 1024], BF16)
    rci = sb("e_rci", [128, 1024], BF16)
    rsi = sb("e_rsi", [128, 1024], BF16)
    for nm, t_ in (("rope1_cs", rcs), ("rope1_ss", rss), ("rope1_ci", rci),
                   ("rope1_si", rsi)):
        p.op("gpsimd", lambda g_, nm=nm, t_=t_: g_.dma_start(
            t_[:, :].rearrange("p (t n) -> p t n", n=64),
            d[nm][:, :].rearrange("(t p) n -> p t n", p=128)),
            writes=[t_.name], dma=True, tag="e_ld" + nm)
    lm = sb("e_lm", [128, 1024], BF16)
    lm0 = sb("e_lm0", [128, 1024], BF16)
    for nm, t_ in (("lmask", lm), ("lmask0", lm0)):
        p.op("gpsimd", lambda g_, nm=nm, t_=t_: g_.dma_start(
            t_[:, :].rearrange("p (t n) -> p t n", n=512),
            d[nm][:, :].rearrange("(t p) n -> p t n", p=128)),
            writes=[t_.name], dma=True, tag="e_ld" + nm)
    qsr = sb("e_qsr", [1, 64], F32)
    ksr = sb("e_ksr", [1, 64], F32)
    QSM = sb("e_qsm", [128, 64], F32)
    KSM = sb("e_ksm", [128, 64], F32)
    lngc = sb("e_lng", [128, 8], F32)
    lnbc = sb("e_lnb", [128, 8], F32)
    b1ac = sb("e_b1a", [128, 22], F32)
    b1gc = sb("e_b1g", [128, 22], F32)
    b2c = sb("e_b2c", [128, 8], F32)
    epsc = sb("e_epsc", [128, 1], F32)
    p.op("vector", lambda v: v.memset(epsc[:, :], 1e-5), writes=[epsc.name], tag="e_eps")

    xf = sb("e_xf", [128, 8 * XFW], F32)
    castb = sb("e_cast", [128, 1024], BF16)
    castb2 = sb("e_cast2", [128, 1024], BF16)
    rowsA = sb("e_rA", [1, 2048], F32)
    rowsB = sb("e_rB", [1, 2048], F32)
    POOL = sb("e_pool", [128, 55296], BF16)

    def pv(off_kb, nelem, dt):
        off = int(off_kb * 512)
        if dt == BF16:
            return POOL[:, off:off + nelem]
        return POOL[:, off:off + nelem * 2].bitcast(F32)

    SIM0 = ps("e_SIM0", [128, 1024], F32)
    SIM1 = ps("e_SIM1", [128, 1024], F32)
    P1 = SIM0[:, 0:512]
    P2 = SIM0[:, 512:1024]
    PO = SIM1[:, 0:512]
    PR = SIM1[:, 512:1024]

    class _K:
        def __init__(self, n):
            self.name = n
    KP1 = None
    KP1, KP2, KPO, KPR = "e_P1k", "e_P2k", "e_POk", "e_PRk"
    PT = ps("e_PT", [128, 1024], BF16)
    PT2 = ps("e_PT2", [128, 1024], BF16)
    AVa = ps("e_AVa", [128, 512], F32)
    AVb = ps("e_AVb", [128, 512], F32)
    PA = AVa
    p.mark_psum(KP1, KP2, KPO, KPR, PT.name, PT2.name, AVa.name, AVb.name)

    def bcast_row(rowap, rowkey, dst, dstkey, nchunks, tagp):
        for ch in range(nchunks):
            p.op("tensor", lambda t, ch=ch: t.matmul(
                P1[:, 0:512], ones_f[0:1, :], rowap[:, ch * 512:(ch + 1) * 512],
                start=True, stop=True),
                reads=[ones_f.name, rowkey], writes=[KP1], tag=tagp + f"b{ch}")
            p.op("vector", lambda v, ch=ch: v.tensor_copy(
                dst[:, ch * 512:(ch + 1) * 512], P1[:, 0:512]),
                reads=[KP1], writes=[dstkey], tag=tagp + f"e{ch}")

    x1dt = pv(0, 2048, F32)
    Mb = pv(8, 2048, F32)
    Ab = pv(16, 2048, F32)
    murow = pv(24, 2048, F32)
    arow = pv(32, 2048, F32)
    p.op("sync", lambda s: s.dma_start(murow[0:1, :], d["murow1_d"][:, :]),
         writes=["e_mur"], dma=True, tag="e_ldmu")
    p.op("sync", lambda s: s.dma_start(arow[0:1, :], d["arow1_d"][:, :]),
         writes=["e_ar"], dma=True, tag="e_ldar")
    bcast_row(murow[0:1, :], "e_mur", Mb, "e_Mb", 4, "e_ibm")
    bcast_row(arow[0:1, :], "e_ar", Ab, "e_Ab", 4, "e_iba")
    for kt in range(8):
        p.op("vector", lambda v, kt=kt: v.memset(xf[:, kt * XFW:kt * XFW + 1], 0.0),
             writes=["e_xf"], tag=f"e_xz{kt}")
    for dt_ in range(ND):
        p.op("sync", lambda s, dt_=dt_: s.dma_start(
            x1dt[:, :], d["x1_d"][dt_ * 128:(dt_ + 1) * 128, :]),
            writes=["e_x1dt"], dma=True, tag=f"e_lx1{dt_}")
        p.op("vector", lambda v, dt_=dt_: v.tensor_tensor(
            xf[:, dt_ * XFW + 1: dt_ * XFW + 2049], x1dt[:, :], Mb[:, :], ALU.add),
            reads=["e_x1dt", "e_Mb", "e_xf"], writes=["e_xf"], tag=f"e_xi0{dt_}")
        p.op("vector", lambda v, dt_=dt_: v.tensor_tensor(
            xf[:, dt_ * XFW + 1: dt_ * XFW + 2049],
            xf[:, dt_ * XFW + 1: dt_ * XFW + 2049], Ab[:, :], ALU.mult),
            reads=["e_Ab", "e_xf"], writes=["e_xf"], tag=f"e_xi1{dt_}")
    p.barrier()

    def cast_lhsT(tt, cb, tagp):
        p.op("vector", lambda v, tt=tt, cb=cb: v.tensor_copy(
            cb[:, 0:512].rearrange("p (k n) -> p k n", n=128),
            xf[:, :].rearrange("p (k n) -> p k n", n=XFW)[:, 0:4, tt * 128 + 1: tt * 128 + 129]),
            reads=["e_xf"], writes=[cb.name], tag=tagp + "a")
        p.op("vector", lambda v, tt=tt, cb=cb: v.tensor_copy(
            cb[:, 512:1024].rearrange("p (k n) -> p k n", n=128),
            xf[:, :].rearrange("p (k n) -> p k n", n=XFW)[:, 4:8, tt * 128: tt * 128 + 128]),
            reads=["e_xf"], writes=[cb.name], tag=tagp + "b")

    for li in range(int(_os.environ.get("E_LAYERS", "6"))):
        lk = f"e{li}_"
        for nm, t_, tg in (("qs1", qsr, "qs"), ("ks1", ksr, "ks")):
            p.op("sync", lambda s, li=li, nm=nm, t_=t_: s.dma_start(
                t_[:, :], d[nm][li:li + 1, :]),
                writes=[t_.name], dma=True, tag=lk + "ld" + tg)
        p.op("tensor", lambda t: t.matmul(P1[:, 0:64], ones_f[0:1, :], qsr[:, :],
                                          start=True, stop=True),
             reads=[ones_f.name, qsr.name], writes=[KP1], tag=lk + "qsm")
        p.op("vector", lambda v: v.tensor_copy(QSM[:, :], P1[:, 0:64]),
             reads=[KP1], writes=[QSM.name], tag=lk + "eqsm")
        p.op("tensor", lambda t: t.matmul(P2[:, 0:64], ones_f[0:1, :], ksr[:, :],
                                          start=True, stop=True),
             reads=[ones_f.name, ksr.name], writes=[KP2], tag=lk + "ksm")
        p.op("vector", lambda v: v.tensor_copy(KSM[:, :], P2[:, 0:64]),
             reads=[KP2], writes=[KSM.name], tag=lk + "eksm")
        for nm, t_, tg in (("lng1", lngc, "g"), ("lnb1", lnbc, "b"),
                           ("b1a1", b1ac, "ba"), ("b1g1", b1gc, "bg"),
                           ("b21c", b2c, "b2")):
            p.op("sync", lambda s, li=li, nm=nm, t_=t_: s.dma_start(
                t_[:, :], d[nm][li, :, :]),
                writes=[t_.name], dma=True, tag=lk + "ld" + tg)

        Wq8 = pv(0, 4096, BF16)
        qraw = pv(8, 2048, F32)
        sqs = pv(16, 2048, F32)
        q8 = pv(24, 8192, BF16)
        k8 = pv(40, 8192, BF16)
        v8 = pv(56, 8192, BF16)
        ao = pv(72, 16384, BF16)
        qhT = pv(8, 2048, BF16)
        khT = pv(12, 2048, BF16)
        atn = pv(16, 1024, BF16)
        nrm = pv(18, 64, F32)
        nmx = pv(18.5, 64, F32)
        Wo8 = pv(0, 8192, BF16)
        WQ8K, QRK, SQK = lk + "W8", lk + "qraw", lk + "sqs"
        Q8K, K8K, V8K, AOK = lk + "q8", lk + "k8", lk + "v8", lk + "ao"
        KqhT, KkhT, Katn, Knrm, Knmx = (lk + "qhT", lk + "khT", lk + "atn",
                                        lk + "nrm", lk + "nmx")
        for hh in range(2):
            hk = lk + f"h{hh}_"
            for pi, pname in ((0, "q"), (1, "k"), (2, "v")):
                p.op("gpsimd", lambda g_, li=li, pi=pi, hh=hh: g_.dma_start(
                    Wq8.rearrange("p (k n) -> p k n", n=512),
                    d["Wqkv1"][li, :, pi * 1024 + hh * 512: pi * 1024 + (hh + 1) * 512]
                    .rearrange("(k p) n -> p k n", p=128)),
                    writes=[WQ8K], dma=True, tag=hk + "ldw" + pname)
                for tb4 in range(4):
                    for t4 in range(4):
                        tt = tb4 * 4 + t4
                        cb = castb if tt % 2 == 0 else castb2
                        pq, pqk = (PO, KPO) if tt % 2 == 0 else (PR, KPR)
                        cast_lhsT(tt, cb, hk + pname + f"c{tt}_")
                        def mm_p(t, cb=cb, pq=pq):
                            for kt in range(8):
                                i1 = t.matmul(pq[:, 0:512],
                                              cb[:, kt * 128:(kt + 1) * 128],
                                              Wq8[:, kt * 512:(kt + 1) * 512],
                                              start=(kt == 0), stop=(kt == 7))
                            return i1
                        p.op("tensor", mm_p, reads=[cb.name, WQ8K], writes=[pqk],
                             tag=hk + pname + f"m{tt}")
                        if pname == "v":
                            p.op("scalar", lambda s, tt=tt, pq=pq: s.copy(
                                v8[:, tt * 512:(tt + 1) * 512], pq[:, 0:512]),
                                reads=[pqk], writes=[V8K], tag=hk + f"ev{tt}")
                        else:
                            p.op("scalar", lambda s, t4=t4, pq=pq: s.copy(
                                qraw[:, t4 * 512:(t4 + 1) * 512], pq[:, 0:512]),
                                reads=[pqk], writes=[QRK], tag=hk + pname + f"e{tt}")
                    if pname == "v":
                        continue
                    dst = q8 if pname == "q" else k8
                    dkey = Q8K if pname == "q" else K8K
                    SM = QSM if pname == "q" else KSM
                    smk = QSM.name if pname == "q" else KSM.name
                    ct, st_ = (rcs, rss) if pname == "q" else (rci, rsi)
                    ck, sk = ct.name, st_.name
                    q3 = qraw.rearrange("p (g e) -> p g e", e=64)
                    s3 = sqs.rearrange("p (g e) -> p g e", e=64)
                    p.op("scalar", lambda s: s.activation(sqs[:, :], qraw[:, :], AF.Square),
                         reads=[QRK], writes=[SQK], tag=hk + pname + f"sq{tb4}")
                    p.op("vector", lambda v, q3=q3, s3=s3: v.tensor_reduce(
                        nrm[:, 0:32], s3, AX.X, ALU.add),
                        reads=[SQK], writes=[lk + "nrm"], tag=hk + pname + f"n2{tb4}")
                    p.op("scalar", lambda s: s.activation(nrm[:, 0:32], nrm[:, 0:32], AF.Ln),
                         reads=[lk + "nrm"], writes=[lk + "nrm"], tag=hk + pname + f"ln{tb4}")
                    p.op("scalar", lambda s: s.activation(nrm[:, 0:32], nrm[:, 0:32],
                                                          AF.Exp, scale=0.5),
                         reads=[lk + "nrm"], writes=[lk + "nrm"], tag=hk + pname + f"ex{tb4}")
                    p.op("vector", lambda v: v.tensor_scalar(
                        nrm[:, 0:32], nrm[:, 0:32], 1e-12, None, ALU.max),
                        reads=[lk + "nrm"], writes=[lk + "nrm"], tag=hk + pname + f"mx{tb4}")
                    p.op("vector", lambda v: v.reciprocal(nrm[:, 0:32], nrm[:, 0:32]),
                         reads=[lk + "nrm"], writes=[lk + "nrm"], tag=hk + pname + f"rc{tb4}")
                    nb = nrm[:, 0:32].unsqueeze(2).broadcast_to((128, 32, 64))
                    p.op("vector", lambda v, q3=q3, nb=nb: v.tensor_tensor(
                        q3, q3, nb, ALU.mult),
                        reads=[QRK, lk + "nrm"], writes=[QRK], tag=hk + pname + f"nm{tb4}")
                    smb = SM[:, :].unsqueeze(1).broadcast_to((128, 32, 64))
                    p.op("vector", lambda v, q3=q3, smb=smb: v.tensor_tensor(
                        q3, q3, smb, ALU.mult),
                        reads=[QRK, smk], writes=[QRK], tag=hk + pname + f"sm{tb4}")
                    for t4 in range(4):
                        tt = tb4 * 4 + t4
                        qt3 = qraw[:, t4 * 512:(t4 + 1) * 512].rearrange(
                            "p (h e) -> p h e", e=64)
                        st3 = sqs[:, t4 * 512:(t4 + 1) * 512].rearrange(
                            "p (h e) -> p h e", e=64)
                        d3 = dst[:, tt * 512:(tt + 1) * 512].rearrange(
                            "p (h e) -> p h e", e=64)
                        cb1 = ct[:, tt * 64:tt * 64 + 32].unsqueeze(1).broadcast_to((128, 8, 32))
                        cb2 = ct[:, tt * 64 + 32:tt * 64 + 64].unsqueeze(1).broadcast_to((128, 8, 32))
                        sb1 = st_[:, tt * 64:tt * 64 + 32].unsqueeze(1).broadcast_to((128, 8, 32))
                        sb2 = st_[:, tt * 64 + 32:tt * 64 + 64].unsqueeze(1).broadcast_to((128, 8, 32))
                        p.op("vector", lambda v, qt3=qt3, st3=st3, cb1=cb1: v.tensor_tensor(
                            st3[:, :, 0:32], qt3[:, :, 0:32], cb1, ALU.mult),
                            reads=[QRK, ck], writes=[SQK], tag=hk + pname + f"r1_{tt}")
                        p.op("vector", lambda v, qt3=qt3, st3=st3, sb1=sb1: v.tensor_tensor(
                            st3[:, :, 32:64], qt3[:, :, 32:64], sb1, ALU.mult),
                            reads=[QRK, sk], writes=[SQK], tag=hk + pname + f"r2_{tt}")
                        p.op("vector", lambda v, st3=st3, d3=d3: v.tensor_tensor(
                            d3[:, :, 0:32], st3[:, :, 0:32], st3[:, :, 32:64], ALU.subtract),
                            reads=[SQK], writes=[dkey], tag=hk + pname + f"r3_{tt}")
                        p.op("vector", lambda v, qt3=qt3, st3=st3, cb2=cb2: v.tensor_tensor(
                            st3[:, :, 0:32], qt3[:, :, 32:64], cb2, ALU.mult),
                            reads=[QRK, ck], writes=[SQK], tag=hk + pname + f"r4_{tt}")
                        p.op("vector", lambda v, qt3=qt3, st3=st3, sb2=sb2: v.tensor_tensor(
                            st3[:, :, 32:64], qt3[:, :, 0:32], sb2, ALU.mult),
                            reads=[QRK, sk], writes=[SQK], tag=hk + pname + f"r5_{tt}")
                        p.op("vector", lambda v, st3=st3, d3=d3: v.tensor_tensor(
                            d3[:, :, 32:64], st3[:, :, 0:32], st3[:, :, 32:64], ALU.add),
                            reads=[SQK], writes=[dkey], tag=hk + pname + f"r6_{tt}")
            eb = [pv(19, 1024, BF16), pv(21, 1024, BF16)]
            atn2 = [pv(16, 1024, BF16), pv(106, 1024, BF16)]
            nrm2v = pv(104, 128, F32)
            nmx2v = pv(105, 128, F32)

            def e_stage1a(hl, w, it2):
                par = it2 % 2
                simps = (P1, P2) if par == 0 else (PO, PR)
                simk = [KP1, KP2] if par == 0 else [KPO, KPR]
                maskt = lm0 if w == 0 else lm
                maskk = lm0.name if w == 0 else lm.name
                wk_ = hk + f"a{hl}_w{w}_"
                for it in range(2):
                    pp = simps[it]
                    def mm_sim(t, w=w, it=it, pp=pp, maskt=maskt):
                        t.matmul(pp[:, 0:512], ident_bf[:, :],
                                 maskt[:, it * 512:(it + 1) * 512],
                                 start=True, stop=False)
                        if w == 0:
                            return t.matmul(pp[:, 256:512],
                                            qhT[0:64, it * 128:(it + 1) * 128],
                                            khT[0:64, 0:256], start=False, stop=True)
                        return t.matmul(pp[:, 0:512],
                                        qhT[0:64, w * 256 + it * 128: w * 256 + (it + 1) * 128],
                                        khT[0:64, (w - 1) * 256:(w + 1) * 256],
                                        start=False, stop=True)
                    p.op("tensor", mm_sim,
                         reads=[KqhT, KkhT, ident_bf.name, maskk],
                         writes=[simk[it]], tag=wk_ + f"sim{it}")
                    nb = par * 8
                    p.op("vector", lambda v, pp=pp, nb=nb, it=it: v.reduce_max(
                        nmx2v[:, nb + it:nb + it + 1], pp[:, 0:512], AX.X),
                        reads=[simk[it]], writes=[f"{lk}nmx{par}_{it}"], tag=wk_ + f"mx{it}")
                    p.op("vector", lambda v, nb=nb, it=it: v.tensor_scalar(
                        nmx2v[:, nb + 2 + it:nb + 3 + it], nmx2v[:, nb + it:nb + it + 1],
                        -8.0, None, ALU.mult),
                        reads=[f"{lk}nmx{par}_{it}"], writes=[f"{lk}nmx{par}_{it}"],
                        tag=wk_ + f"nm{it}")

            def e_stage1b(hl, w, it2):
                par = it2 % 2
                simps = (P1, P2) if par == 0 else (PO, PR)
                simk = [KP1, KP2] if par == 0 else [KPO, KPR]
                ebuf = eb[par]
                nb = par * 8
                wk_ = hk + f"a{hl}_w{w}_"
                for it in range(2):
                    pp = simps[it]
                    p.op("scalar", lambda s, pp=pp, it=it, nb=nb, ebuf=ebuf: s.activation(
                        ebuf[:, it * 512:(it + 1) * 512], pp[:, 0:512], AF.Exp,
                        bias=nmx2v[:, nb + 2 + it:nb + 3 + it], scale=8.0,
                        accum_out=nmx2v[:, nb + 4 + it:nb + 5 + it]),
                        reads=[simk[it], f"{lk}nmx{par}_{it}"],
                        writes=[f"{lk}eb{par}", f"{lk}nmx{par}z"], tag=wk_ + f"ex{it}")
                p.op("vector", lambda v, nb=nb: v.reciprocal(
                    nmx2v[:, nb + 6:nb + 8], nmx2v[:, nb + 4:nb + 6]),
                    reads=[f"{lk}nmx{par}z"], writes=[f"{lk}nmx{par}z"],
                    tag=wk_ + "rc")
                for it in range(2):
                    p.op("vector", lambda v, it=it, nb=nb, ebuf=ebuf: v.tensor_scalar(
                        ebuf[:, it * 512:(it + 1) * 512],
                        ebuf[:, it * 512:(it + 1) * 512],
                        nmx2v[:, nb + 6 + it:nb + 7 + it], None, ALU.mult),
                        reads=[f"{lk}eb{par}", f"{lk}nmx{par}z"],
                        writes=[f"{lk}eb{par}"], tag=wk_ + f"sc{it}")

            def e_stage2a(hl, w, it2):
                par = it2 % 2
                ebuf = eb[par]
                PTp = PT if par == 0 else PT2
                ptk = f"e_PTx{par}"
                av = atn2[par]
                wk_ = hk + f"a{hl}_w{w}_"
                def mm_at(t, ebuf=ebuf, PTp=PTp):
                    for jt in range(4):
                        for it in range(2):
                            i1 = t.transpose(
                                PTp[:, jt * 256 + it * 128: jt * 256 + (it + 1) * 128],
                                ebuf[:, it * 512 + jt * 128: it * 512 + (jt + 1) * 128],
                                ident_bf[:, :])
                    return i1
                p.op("tensor", mm_at, reads=[f"{lk}eb{par}", ident_bf.name],
                     writes=[ptk], tag=wk_ + "atr")
                p.op("vector", lambda v, av=av, PTp=PTp: v.tensor_copy(av[:, :], PTp[:, 0:1024]),
                     reads=[ptk], writes=[f"{lk}atn{par}"], tag=wk_ + "ate")

            def e_stage2b(hl, w, it2):
                par = it2 % 2
                AVp = AVa if par == 0 else AVb
                av = atn2[par]
                wk_ = hk + f"a{hl}_w{w}_"
                def mm_av(t, hl=hl, w=w, av=av, AVp=AVp):
                    jts = list(range(2, 4) if w == 0 else range(4))
                    for ix, jt in enumerate(jts):
                        gt = (w - 1) * 2 + jt
                        i1 = t.matmul(AVp[0:64, 0:256],
                                      v8[:, gt * 512 + hl * 64: gt * 512 + (hl + 1) * 64],
                                      av[:, jt * 256:(jt + 1) * 256],
                                      start=(ix == 0), stop=(ix == len(jts) - 1))
                    return i1
                p.op("tensor", mm_av, reads=[V8K, f"{lk}atn{par}"], writes=[AVp.name],
                     tag=wk_ + "av")
                hg = hh * 8 + hl
                p.op("scalar", lambda s, hg=hg, w=w, AVp=AVp: s.copy(
                    ao[(hg % 2) * 64:(hg % 2) * 64 + 64,
                       (hg // 2) * 2048 + w * 256:(hg // 2) * 2048 + (w + 1) * 256],
                    AVp[0:64, 0:256]),
                    reads=[AVp.name], writes=[AOK], tag=wk_ + "no")

            pend = []
            it2 = 0
            for hl in range(8):
                ck_ = hk + f"a{hl}_"
                for srcb, dsT, dkT, srckey in ((q8, qhT, KqhT, Q8K),
                                               (k8, khT, KkhT, K8K)):
                    for grp in range(2):
                        def mm_tr8(t, srcb=srcb, hl=hl, grp=grp):
                            for j in range(8):
                                tt = grp * 8 + j
                                i1 = t.transpose(PT[0:64, j * 128:(j + 1) * 128],
                                                 srcb[:, tt * 512 + hl * 64: tt * 512 + (hl + 1) * 64],
                                                 ident_bf[:, :])
                            return i1
                        p.op("tensor", mm_tr8, reads=[srckey, ident_bf.name],
                             writes=["e_PTx0"], tag=ck_ + f"tr{grp}" + dkT)
                        p.op("vector", lambda v, dsT=dsT, grp=grp: v.tensor_copy(
                            dsT[0:64, grp * 1024:(grp + 1) * 1024], PT[0:64, 0:1024]),
                            reads=["e_PTx0"], writes=[dkT, QRK], tag=ck_ + f"te{grp}" + dkT)
                for w in range(8):
                    e_stage1a(hl, w, it2)
                    if len(pend) >= 1:
                        e_stage2a(*pend[-1])
                    e_stage1b(hl, w, it2)
                    if len(pend) >= 2:
                        e_stage2b(*pend[-2])
                    pend.append((hl, w, it2))
                    it2 += 1
            e_stage2a(*pend[-1])
            e_stage2b(*pend[-2])
            e_stage2b(*pend[-1])
        p.op("gpsimd", lambda g_, li=li: g_.dma_start(
            Wo8.rearrange("p (k n) -> p k n", n=1024),
            d["Wo1"][li, :, :].rearrange("(k p) n -> p k n", p=128)),
            writes=[lk + "Wo8", WQ8K, QRK, KqhT, KkhT], dma=True, tag=lk + "ldwo")
        for dt_ in range(ND):
            for nch in range(4):
                def mm_wo(t, dt_=dt_, nch=nch):
                    for kt in range(8):
                        i1 = t.matmul(PO[:, 0:512],
                                      Wo8[:, kt * 1024 + dt_ * 128: kt * 1024 + dt_ * 128 + 128],
                                      ao[:, kt * 2048 + nch * 512: kt * 2048 + (nch + 1) * 512],
                                      start=(kt == 0), stop=(kt == 7))
                    return i1
                p.op("tensor", mm_wo, reads=[lk + "Wo8", AOK], writes=[KPO],
                     tag=lk + f"wo{dt_}_{nch}")
                p.op("vector", lambda v, dt_=dt_, nch=nch: v.tensor_tensor(
                    xf[:, dt_ * XFW + 1 + nch * 512: dt_ * XFW + 1 + (nch + 1) * 512],
                    PO[:, 0:512],
                    xf[:, dt_ * XFW + 1 + nch * 512: dt_ * XFW + 1 + (nch + 1) * 512],
                    ALU.add),
                    reads=[KPO, "e_xf"], writes=["e_xf"], tag=lk + f"wa{dt_}_{nch}")
        p.barrier()
        if _os.environ.get("E_NOFFN"):
            continue

        W1a = pv(0, 8192, BF16)
        W1g = pv(16, 8192, BF16)
        W2s = pv(32, 8192, BF16)
        gsb = pv(48, 4096, BF16)
        xnf = pv(56, 16384, BF16)
        xsqe = pv(88, 512, F32)
        Mbc = pv(90, 512, F32)
        Abc = pv(92, 512, F32)
        KW1a, KW1g, KW2 = lk + "W1a", lk + "W1g", lk + "W2"
        Kg, Kxnf, Kxq = lk + "g", lk + "xnf", lk + "xq"
        Kmb, Kab = lk + "mbc", lk + "abc"
        p.op("vector", lambda v: v.memset(rowsA[:, :], 0.0), writes=["e_rA"], tag=lk + "za")
        p.op("vector", lambda v: v.memset(rowsB[:, :], 0.0), writes=["e_rB"], tag=lk + "zb")
        for dt_ in range(ND):
            off = dt_ * XFW + (1 if dt_ < 4 else 0)
            for nch in range(4):
                p.op("tensor", lambda t, off=off, nch=nch: t.matmul(
                    PA[0:1, 0:512], ones_f[:, 0:1],
                    xf[:, off + nch * 512: off + (nch + 1) * 512],
                    start=True, stop=True),
                    reads=["e_xf", ones_f.name], writes=[PA.name],
                    tag=lk + f"s1_{dt_}_{nch}")
                p.op("vector", lambda v, nch=nch: v.tensor_tensor(
                    rowsA[:, nch * 512:(nch + 1) * 512],
                    rowsA[:, nch * 512:(nch + 1) * 512], PA[0:1, 0:512], ALU.add),
                    reads=[PA.name, "e_rA"], writes=["e_rA"], tag=lk + f"a1_{dt_}_{nch}")
                p.op("scalar", lambda s, off=off, nch=nch: s.activation(
                    xsqe[:, 0:512], xf[:, off + nch * 512: off + (nch + 1) * 512],
                    AF.Square),
                    reads=["e_xf"], writes=[Kxq], tag=lk + f"sq_{dt_}_{nch}")
                p.op("tensor", lambda t: t.matmul(
                    PA[0:1, 0:512], ones_f[:, 0:1], xsqe[:, 0:512],
                    start=True, stop=True),
                    reads=[Kxq, ones_f.name], writes=[PA.name],
                    tag=lk + f"s2_{dt_}_{nch}")
                p.op("vector", lambda v, nch=nch: v.tensor_tensor(
                    rowsB[:, nch * 512:(nch + 1) * 512],
                    rowsB[:, nch * 512:(nch + 1) * 512], PA[0:1, 0:512], ALU.add),
                    reads=[PA.name, "e_rB"], writes=["e_rB"], tag=lk + f"a2_{dt_}_{nch}")
        p.op("vector", lambda v: v.tensor_scalar(
            rowsA[:, :], rowsA[:, :], -1.0 / 1024.0, None, ALU.mult),
            reads=["e_rA"], writes=["e_rA"], tag=lk + "mu")
        for nch in range(4):
            p.op("vector", lambda v, nch=nch: v.tensor_tensor(
                xsqe[0:1, 0:512], rowsA[:, nch * 512:(nch + 1) * 512],
                rowsA[:, nch * 512:(nch + 1) * 512], ALU.mult),
                reads=["e_rA"], writes=[Kxq], tag=lk + f"mu2{nch}")
            p.op("vector", lambda v, nch=nch: v.scalar_tensor_tensor(
                rowsB[:, nch * 512:(nch + 1) * 512],
                rowsB[:, nch * 512:(nch + 1) * 512], 1.0 / 1024.0,
                xsqe[0:1, 0:512], ALU.mult, ALU.subtract),
                reads=["e_rB", Kxq], writes=["e_rB"], tag=lk + f"var{nch}")
        p.op("scalar", lambda s: s.activation(rowsB[:, :], rowsB[:, :], AF.Ln,
                                              bias=epsc[0:1, :]),
             reads=["e_rB", epsc.name], writes=["e_rB"], tag=lk + "lnv")
        p.op("scalar", lambda s: s.activation(rowsB[:, :], rowsB[:, :], AF.Exp, scale=-0.5),
             reads=["e_rB"], writes=["e_rB"], tag=lk + "rstd")
        for tb in range(4):
            tbk0 = lk + f"n{tb}_"
            p.op("tensor", lambda t, tb=tb: t.matmul(
                P1[:, 0:512], ones_f[0:1, :], rowsA[:, tb * 512:(tb + 1) * 512],
                start=True, stop=True),
                reads=[ones_f.name, "e_rA"], writes=[KP1], tag=tbk0 + "bm")
            p.op("vector", lambda v: v.tensor_copy(Mbc[:, 0:512], P1[:, 0:512]),
                 reads=[KP1], writes=[Kmb], tag=tbk0 + "em")
            p.op("tensor", lambda t, tb=tb: t.matmul(
                P2[:, 0:512], ones_f[0:1, :], rowsB[:, tb * 512:(tb + 1) * 512],
                start=True, stop=True),
                reads=[ones_f.name, "e_rB"], writes=[KP2], tag=tbk0 + "ba")
            p.op("vector", lambda v: v.tensor_copy(Abc[:, 0:512], P2[:, 0:512]),
                 reads=[KP2], writes=[Kab], tag=tbk0 + "ea")
            for kt in range(8):
                off = kt * XFW + (1 if kt < 4 else 0) + tb * 512
                p.op("vector", lambda v, off=off: v.tensor_tensor(
                    xsqe[:, 0:512], xf[:, off:off + 512], Mbc[:, 0:512], ALU.add),
                    reads=["e_xf", Kmb], writes=[Kxq], tag=tbk0 + f"x0_{kt}")
                p.op("vector", lambda v: v.tensor_tensor(
                    xsqe[:, 0:512], xsqe[:, 0:512], Abc[:, 0:512], ALU.mult),
                    reads=[Kxq, Kab], writes=[Kxq], tag=tbk0 + f"x1_{kt}")
                p.op("scalar", lambda s, kt=kt, tb=tb: s.activation(
                    xnf[:, kt * 2048 + tb * 512: kt * 2048 + (tb + 1) * 512],
                    xsqe[:, 0:512], AF.Identity,
                    bias=lnbc[:, kt:kt + 1], scale=lngc[:, kt:kt + 1]),
                    reads=[Kxq, lngc.name, lnbc.name], writes=[Kxnf],
                    tag=tbk0 + f"x2_{kt}")
        for fc in range(3):
            fck = lk + f"f{fc}_"
            fbase = fc * 1024
            fwid = 1024 if fc < 2 else 682
            mt_sizes = [128] * 8 if fc < 2 else [128] * 5 + [42]
            nmt = len(mt_sizes)
            p.op("gpsimd", lambda g_, li=li, fbase=fbase, fwid=fwid: g_.dma_start(
                W1a.rearrange("p (k n) -> p k n", n=1024)[:, :, 0:fwid],
                d["W11"][li, :, fbase:fbase + fwid].rearrange("(k p) n -> p k n", p=128)),
                writes=[KW1a], dma=True, tag=fck + "ldw1a")
            p.op("gpsimd", lambda g_, li=li, fbase=fbase, fwid=fwid: g_.dma_start(
                W1g.rearrange("p (k n) -> p k n", n=1024)[:, :, 0:fwid],
                d["W11"][li, :, 2730 + fbase:2730 + fbase + fwid]
                .rearrange("(k p) n -> p k n", p=128)),
                writes=[KW1g], dma=True, tag=fck + "ldw1g")
            nfull = fwid // 128
            p.op("gpsimd", lambda g_, li=li, fbase=fbase, nfull=nfull: g_.dma_start(
                W2s[:, 0:nfull * 1024].rearrange("p (k n) -> p k n", n=1024),
                d["W21"][li, fbase:fbase + nfull * 128, :]
                .rearrange("(k p) n -> p k n", p=128)),
                writes=[KW2], dma=True, tag=fck + "ldw2")
            if fwid % 128 != 0:
                p.op("gpsimd", lambda g_, li=li, fbase=fbase, nfull=nfull: g_.dma_start(
                    W2s[0:42, nfull * 1024:(nfull + 1) * 1024],
                    d["W21"][li, fbase + nfull * 128:fbase + fwid, :]),
                    writes=[KW2], dma=True, tag=fck + "ldw2t")
            for tb in range(4):
                tbk = fck + f"t{tb}_"
                for mt in range(nmt):
                    msz = mt_sizes[mt]
                    pa_, pak = (P1, KP1) if mt % 2 == 0 else (PO, KPO)
                    pg_, pgk = (P2, KP2) if mt % 2 == 0 else (PR, KPR)
                    def mm_a(t, mt=mt, msz=msz, tb=tb, pa_=pa_):
                        for kt in range(8):
                            i1 = t.matmul(pa_[0:msz, 0:512],
                                          W1a[:, kt * 1024 + mt * 128: kt * 1024 + mt * 128 + msz],
                                          xnf[:, kt * 2048 + tb * 512: kt * 2048 + (tb + 1) * 512],
                                          start=(kt == 0), stop=(kt == 7))
                        return i1
                    p.op("tensor", mm_a, reads=[KW1a, Kxnf], writes=[pak],
                         tag=tbk + f"am{mt}")
                    def mm_g(t, mt=mt, msz=msz, tb=tb, pg_=pg_):
                        for kt in range(8):
                            i1 = t.matmul(pg_[0:msz, 0:512],
                                          W1g[:, kt * 1024 + mt * 128: kt * 1024 + mt * 128 + msz],
                                          xnf[:, kt * 2048 + tb * 512: kt * 2048 + (tb + 1) * 512],
                                          start=(kt == 0), stop=(kt == 7))
                        return i1
                    p.op("tensor", mm_g, reads=[KW1g, Kxnf], writes=[pgk],
                         tag=tbk + f"gm{mt}")
                    bia = b1ac[:, fbase // 128 + mt: fbase // 128 + mt + 1]
                    big = b1gc[:, fbase // 128 + mt: fbase // 128 + mt + 1]
                    p.op("scalar", lambda s, mt=mt, msz=msz, big=big, pg_=pg_: s.activation(
                        gsb[0:msz, mt * 512:(mt + 1) * 512], pg_[0:msz, 0:512],
                        AF.Gelu, bias=big[0:msz, :]),
                        reads=[pgk, b1gc.name], writes=[Kg], tag=tbk + f"ge{mt}")
                    p.op("vector", lambda v, mt=mt, msz=msz, bia=bia, pa_=pa_: v.scalar_tensor_tensor(
                        gsb[0:msz, mt * 512:(mt + 1) * 512], pa_[0:msz, 0:512],
                        bia[0:msz, :], gsb[0:msz, mt * 512:(mt + 1) * 512],
                        ALU.add, ALU.mult),
                        reads=[pak, b1ac.name, Kg], writes=[Kg], tag=tbk + f"h2_{mt}")
                for dt_ in range(ND):
                    po_ = AVa if dt_ % 2 == 0 else AVb
                    def mm_o(t, dt_=dt_, mt_sizes=mt_sizes, nmt=nmt, po_=po_):
                        for kt in range(nmt):
                            ksz = mt_sizes[kt]
                            i1 = t.matmul(po_[:, 0:512],
                                          W2s[0:ksz, kt * 1024 + dt_ * 128: kt * 1024 + (dt_ + 1) * 128],
                                          gsb[0:ksz, kt * 512:(kt + 1) * 512],
                                          start=(kt == 0), stop=(kt == nmt - 1))
                        return i1
                    p.op("tensor", mm_o, reads=[KW2, Kg], writes=[po_.name],
                         tag=tbk + f"o{dt_}")
                    xslice = xf[:, dt_ * XFW + 1 + tb * 512: dt_ * XFW + 1 + (tb + 1) * 512]
                    if fc == 2:
                        p.op("vector", lambda v, xslice=xslice, dt_=dt_, po_=po_: v.scalar_tensor_tensor(
                            xslice, po_[:, 0:512], b2c[:, dt_:dt_ + 1], xslice,
                            ALU.add, ALU.add),
                            reads=[po_.name, b2c.name, "e_xf"], writes=["e_xf"],
                            tag=tbk + f"oa{dt_}")
                    else:
                        p.op("vector", lambda v, xslice=xslice, po_=po_: v.tensor_tensor(
                            xslice, po_[:, 0:512], xslice, ALU.add),
                            reads=[po_.name, "e_xf"], writes=["e_xf"], tag=tbk + f"oa{dt_}")
        p.barrier()

    outsb = pv(0, 512, F32)
    sumb = pv(2, 2048, F32)
    x1f = pv(10, 2048, F32)
    for dt_ in range(ND):
        p.op("sync", lambda s, dt_=dt_: s.dma_start(
            x1f[:, :], d["x1b_d"][dt_ * 128:(dt_ + 1) * 128, :]),
            writes=["e_x1f"], dma=True, tag=f"e_fx{dt_}")
        p.op("vector", lambda v, dt_=dt_: v.tensor_tensor(
            sumb[:, :], xf[:, dt_ * XFW + 1: dt_ * XFW + 2049], x1f[:, :], ALU.add),
            reads=["e_xf", "e_x1f"], writes=["e_sumb"], tag=f"e_fs{dt_}")
        for g_ in range(4):
            def mm_ft(t, g_=g_):
                for j in range(4):
                    tt = g_ * 4 + j
                    i1 = t.transpose(PR[:, j * 128:(j + 1) * 128],
                                     sumb[:, tt * 128:(tt + 1) * 128], ident_f[:, :])
                return i1
            p.op("tensor", mm_ft, reads=["e_sumb", ident_f.name], writes=[KPR],
                 tag=f"e_ft{dt_}_{g_}")
            p.op("vector", lambda v: v.tensor_copy(outsb[:, 0:512], PR[:, 0:512]),
                 reads=[KPR], writes=["e_outsb"], tag=f"e_fe{dt_}_{g_}")
            for j in range(4):
                tt = g_ * 4 + j
                eng = "sync" if j % 2 == 0 else "gpsimd"
                p.op(eng, lambda s, dt_=dt_, tt=tt, j=j: s.dma_start(
                    d["out_d"][tt * 128:(tt + 1) * 128, dt_ * 128:(dt_ + 1) * 128],
                    outsb[:, j * 128:(j + 1) * 128]),
                    reads=["e_outsb"], writes=["out_d"], dma=True,
                    tag=f"e_st{dt_}_{tt}")


def build_phaseE():
    nc = bass.Bass("TRN2", target_bir_lowering=False)
    es = contextlib.ExitStack()
    d = {}
    for nm, shp in (("ident_f", (128, 128)), ("ones_f", (128, 128)),
                    ("rope1_cs", (2048, 64)), ("rope1_ss", (2048, 64)),
                    ("rope1_ci", (2048, 64)), ("rope1_si", (2048, 64)),
                    ("lmask", (256, 512)), ("lmask0", (256, 512)),
                    ("Wqkv1", (6, D, 3072)), ("Wo1", (6, D, D)),
                    ("W11", (6, D, 5460)), ("W21", (6, 2730, D)),
                    ("qs1", (6, 64)), ("ks1", (6, 64)),
                    ("lng1", (6, 128, 8)), ("lnb1", (6, 128, 8)),
                    ("b1a1", (6, 128, 22)), ("b1g1", (6, 128, 22)),
                    ("b21c", (6, 128, 8)),
                    ("murow1_d", (1, T)), ("arow1_d", (1, T))):
        d[nm] = nc.dram_tensor(nm, list(shp), F32, kind="ExternalInput").ap()
    d["x1_d"] = nc.dram_tensor("x1_d", [D, T], F32, kind="ExternalInput").ap()
    d["x1b_d"] = nc.dram_tensor("x1b_d", [D, T], F32, kind="ExternalInput").ap()
    d["out_d"] = nc.dram_tensor("out_d", [T, D], F32, kind="ExternalOutput").ap()
    p = Prog(nc)
    with es:
        phase_l1_ffn(p, d, es)
        p.emit()
    nc.reset()
    return nc


def build_phaseB():
    nc = bass.Bass("TRN2", target_bir_lowering=False)
    es = contextlib.ExitStack()
    d = {}
    shapes = {"ident_f": (128, 128), "ones_f": (128, 128),
              "rope0_cos": (128, 128), "rope0_sin": (128, 128),
              "wqT": (D, D), "wkaT": (D, 256), "wkbT": (256, D),
              "wvaT": (D, 256), "wvbT": (256, D), "woT": (D, D)}
    for nm, shp in shapes.items():
        d[nm] = nc.dram_tensor(nm, list(shp), F32, kind="ExternalInput").ap()
    d["xg"] = nc.dram_tensor("xg", [D, NTOK], F32, kind="ExternalInput").ap()
    d["xn_d"] = nc.dram_tensor("xn_d", [D, T], BF16, kind="ExternalInput").ap()
    d["x1a_d"] = nc.dram_tensor("x1a_d", [D, T], F32, kind="ExternalOutput").ap()
    if B_DEBUG:
        d["dbg_qT"] = nc.dram_tensor("dbg_qT", [128, 16384], BF16, kind="ExternalOutput").ap()
        d["dbg_kT"] = nc.dram_tensor("dbg_kT", [128, 16384], BF16, kind="ExternalOutput").ap()
        d["dbg_v65"] = nc.dram_tensor("dbg_v65", [128, 16640], BF16, kind="ExternalOutput").ap()
        d["dbg_ao"] = nc.dram_tensor("dbg_ao", [128, 16384], BF16, kind="ExternalOutput").ap()
    p = Prog(nc)
    with es:
        phase_l0_attn(p, d, es)
        p.emit()
    nc.reset()
    return nc


_PERCORE_SHAPES = {"xg": (D, NTOK), "permT": (128, 128), "permT2": (128, 128)}

_PHASEA_SHARED = ["ones_f", "tstrict", "iota8", "ident_f", "rw_ext"]


def build_phaseA():
    nc = bass.Bass("TRN2", target_bir_lowering=False)
    es = contextlib.ExitStack()
    d = {}
    shapes = {"ones_f": (128, 128), "tstrict": (128, 128), "iota8": (1, 8),
              "ident_f": (128, 128), "rw_ext": (D, 9)}
    for nm, shp in shapes.items():
        d[nm] = nc.dram_tensor(nm, list(shp), F32, kind="ExternalInput").ap()
    for nm, shp in _PERCORE_SHAPES.items():
        d[nm] = nc.dram_tensor(nm, list(shp), F32, kind="ExternalInput").ap()
    d["car_d"] = nc.dram_tensor("car_d", [8, 128], F32, kind="Internal").ap()
    d["mr_d"] = nc.dram_tensor("mr_d", [16, 256], F32, kind="Internal").ap()
    d["Wg_d"] = nc.dram_tensor("Wg_d", [128, 1024], F32, kind="ExternalOutput").ap()
    d["WT9_d"] = nc.dram_tensor("WT9_d", [33, 2048], F32, kind="ExternalOutput").ap()
    d["Wtr_d"] = nc.dram_tensor("Wtr_d", [16, 1024], F32, kind="ExternalOutput").ap()
    d["murow_d"] = nc.dram_tensor("murow_d", [1, 2048], F32, kind="ExternalOutput").ap()
    d["arow_d"] = nc.dram_tensor("arow_d", [1, 2048], F32, kind="ExternalOutput").ap()
    d["xn_d"] = nc.dram_tensor("xn_d", [D, T], BF16, kind="ExternalOutput").ap()
    p = Prog(nc)
    with es:
        phase_ln_gating(p, d, es)
        p.emit()
    nc.reset()
    return nc


def build_fused():
    nc = bass.Bass("TRN2", target_bir_lowering=False)
    d = {}

    def ein(nm, shp, dt=F32):
        d[nm] = nc.dram_tensor(nm, list(shp), dt, kind="ExternalInput").ap()

    def intl(nm, shp, dt=F32):
        d[nm] = nc.dram_tensor(nm, list(shp), dt, kind="Internal").ap()

    ein("ones_f", (128, 128)); ein("tstrict", (128, 128)); ein("iota8", (1, 8))
    ein("ident_f", (128, 128)); ein("rw_ext", (D, 9))
    ein("xg", (D, NTOK)); ein("permT", (128, 128)); ein("permT2", (128, 128))
    ein("rope0_cos", (128, 128)); ein("rope0_sin", (128, 128))
    ein("wqT", (D, D)); ein("wkaT", (D, 256)); ein("wkbT", (256, D))
    ein("wvaT", (D, 256)); ein("wvbT", (256, D)); ein("woT", (D, D))
    ein("moe_w1", (8, D, 4096)); ein("moe_w2", (8, 4096, D))
    ein("moe_b1", (8, 128, 32)); ein("b2ext", (33, D))
    ein("sw1", (D, 4096)); ein("sw2", (4096, D)); ein("sb1", (128, 32))
    ein("wq1T", (D, D)); ein("wk1T", (D, D)); ein("wv1T", (D, D))
    ein("bq1", (128, 8)); ein("gamma1", (128, 8)); ein("beta1", (128, 8))
    ein("bk1row", (1, D)); ein("bv1row", (1, D))
    ein("rope1_cs", (2048, 64)); ein("rope1_ss", (2048, 64))
    ein("rope1_ci", (2048, 64)); ein("rope1_si", (2048, 64))
    ein("lmask", (256, 512)); ein("lmask0", (256, 512))
    ein("Wqkv1", (6, D, 3072)); ein("Wo1", (6, D, D))
    ein("W11", (6, D, 5460)); ein("W21", (6, 2730, D))
    ein("qs1", (6, 64)); ein("ks1", (6, 64))
    ein("lng1", (6, 128, 8)); ein("lnb1", (6, 128, 8))
    ein("b1a1", (6, 128, 22)); ein("b1g1", (6, 128, 22)); ein("b21c", (6, 128, 8))
    intl("car_d", (8, 128)); intl("mr_d", (16, 256))
    intl("Wg_d", (128, 1024)); intl("WT9_d", (33, 2048)); intl("Wtr_d", (16, 1024))
    intl("murow_d", (1, T)); intl("arow_d", (1, T))
    intl("xn_d", (D, T), BF16)
    intl("x1a_d", (D, T)); intl("x1_d", (D, T))
    intl("xn1_d", (D, T), BF16)
    intl("murow1_d", (1, T)); intl("arow1_d", (1, T)); intl("x1b_d", (D, T))
    d["out_d"] = nc.dram_tensor("out_d", [T, D], F32, kind="ExternalOutput").ap()

    p = Prog(nc)
    for fn in (phase_ln_gating, phase_l0_attn, phase_moe, phase_l1_attn,
               phase_l1_ffn):
        with contextlib.ExitStack() as es:
            fn(p, d, es)
        p.barrier()
    with contextlib.ExitStack() as es:
        p.emit()
    nc.reset()
    return nc


_FUSED_IN = ["ones_f", "tstrict", "iota8", "ident_f", "rw_ext",
             "rope0_cos", "rope0_sin", "wqT", "wkaT", "wkbT", "wvaT", "wvbT",
             "woT", "moe_w1", "moe_w2", "moe_b1", "b2ext", "sw1", "sw2", "sb1",
             "wq1T", "wk1T", "wv1T", "bq1", "gamma1", "beta1", "bk1row",
             "bv1row", "rope1_cs", "rope1_ss", "rope1_ci", "rope1_si",
             "lmask", "lmask0", "Wqkv1", "Wo1", "W11", "W21", "qs1", "ks1",
             "lng1", "lnb1", "b1a1", "b1g1", "b21c"]


_PHASE_KEYS = {
    "A": (["ones_f", "tstrict", "iota8", "ident_f", "rw_ext"],
          ["xg", "permT", "permT2"]),
    "B": (["ident_f", "ones_f", "rope0_cos", "rope0_sin", "wqT", "wkaT",
           "wkbT", "wvaT", "wvbT", "woT"], ["xg"]),
    "C": (["ident_f", "moe_w1", "moe_w2", "moe_b1", "b2ext", "sw1", "sw2",
           "sb1"], []),
    "D": (["ones_f", "wq1T", "wk1T", "wv1T", "bq1", "gamma1", "beta1",
           "bk1row", "bv1row"], []),
    "E": (["ident_f", "ones_f", "rope1_cs", "rope1_ss", "rope1_ci", "rope1_si",
           "lmask", "lmask0", "Wqkv1", "Wo1", "W11", "W21", "qs1", "ks1",
           "lng1", "lnb1", "b1a1", "b1g1", "b21c"], []),
}

_HANDOFF = {
    "B": ["xn_d"],
    "C": ["Wg_d", "WT9_d", "xn_d", "x1a_d"],
    "D": ["x1_d"],
    "E": ["murow1_d", "arow1_d", "x1_d", "x1b_d"],
}

_BUILDERS = None
_FUSED_NC = None
UNFUSED = bool(int(_os.environ.get("UNFUSED", "0")))


def kernel(**inputs):
    global _BUILDERS, _FUSED_NC
    shared, cores = _prep(inputs)
    if not UNFUSED:
        if _FUSED_NC is None:
            _FUSED_NC = build_fused()
        in_maps = []
        for c in range(NCORES):
            m = {k: shared[k] for k in _FUSED_IN}
            m.update({k: cores[c][k] for k in ("xg", "permT", "permT2")})
            in_maps.append(m)
        res = run_bass_kernel_spmd(_FUSED_NC, in_maps,
                                   core_ids=list(range(NCORES)))
        return np.stack([res.results[c]["out_d"] for c in range(NCORES)])
    if _BUILDERS is None:
        _BUILDERS = {
            "A": build_phaseA(), "B": build_phaseB(), "C": build_phaseC(),
            "D": build_phaseD(), "E": build_phaseE(),
        }
    state = [dict() for _ in range(NCORES)]
    for ph in ("A", "B", "C", "D", "E"):
        skeys, ckeys = _PHASE_KEYS[ph]
        in_maps = []
        for c in range(NCORES):
            m = {k: shared[k] for k in skeys}
            m.update({k: cores[c][k] for k in ckeys})
            for k in _HANDOFF.get(ph, ()):
                m[k] = state[c][k]
            in_maps.append(m)
        res = run_bass_kernel_spmd(_BUILDERS[ph], in_maps,
                                   core_ids=list(range(NCORES)))
        for c in range(NCORES):
            state[c].update(res.results[c])
    out = np.stack([state[c]["out_d"] for c in range(NCORES)])
    return out
